# version 06
# baseline (speedup 1.0000x reference)
import os
import numpy as np
import ml_dtypes
import concourse.bass as bass
import concourse.mybir as mybir
from concourse.bass_utils import run_bass_kernel_spmd

F32 = mybir.dt.float32
BF16 = mybir.dt.bfloat16
I32 = mybir.dt.int32
AF = mybir.ActivationFunctionType
ALU = mybir.AluOpType
AX = mybir.AxisListType

MAGIC = 12582912.0
NCORE = 8
T = 2048
CH = 8
IC = 22
NT = 4
SBUF_BASE = 16384
SBUF_TOP = 229312
G8 = [[0, 1, 2, 3, 4, 5, 6, 7]]
G4 = [[0, 1, 2, 3], [4, 5, 6, 7]]
ENGINES = ("sync", "tensor", "vector", "scalar", "gpsimd")

TAPS = ()
DEV_STOP = os.environ.get("KB_DEV_STOP", "")


class Prog:
    RDMA = 8

    def __init__(self):
        self.steps = {e: [] for e in ENGINES}
        self.pc = {e: 0 for e in ENGINES}
        self.dn = {e: 0 for e in ENGINES}
        self.dtot = {e: [0] * self.RDMA for e in ENGINES}
        self.ccc = 0

    def _waits(self, eng, after):
        waits = {}
        items = []
        for h in after:
            if h is None:
                continue
            if isinstance(h, list):
                items.extend(h)
            else:
                items.append(h)
        for key, val in items:
            if key == ("p", eng):
                continue
            waits[key] = max(waits.get(key, 0), val)
        return waits

    def c(self, eng, fn, after=()):
        self.steps[eng].append(("c", fn, self._waits(eng, after)))
        self.pc[eng] += 1
        return (("p", eng), self.pc[eng])

    def d(self, eng, ndma, fn, after=()):
        assert ndma <= self.RDMA, f"dma step with {ndma} > {self.RDMA} dmas"
        waits = self._waits(eng, after)
        plan = []
        for i in range(ndma):
            slot = self.dn[eng] % self.RDMA
            pre = self.dtot[eng][slot]
            self.dtot[eng][slot] = pre + 16
            self.dn[eng] += 1
            plan.append((slot, pre))
            if pre > 0:
                k = ("d", eng, slot)
                waits[k] = max(waits.get(k, 0), pre)
        self.steps[eng].append(("d", fn, waits, plan))
        handle = {}
        for slot, pre in plan:
            handle[("d", eng, slot)] = pre + 16
        return [(k, v) for k, v in handle.items()]

    def cc(self, fn, after=()):
        self.steps["gpsimd"].append(("cc", fn, self._waits("gpsimd", after)))
        self.ccc += 1
        return (("cc",), self.ccc)

    def emit(self, nc):
        sems = {}
        for e in ENGINES:
            sems[("p", e)] = nc.semaphore(f"p_{e}").__enter__()
            for j in range(self.RDMA):
                sems[("d", e, j)] = nc.semaphore(f"d_{e}{j}").__enter__()
        sems[("cc",)] = nc.semaphore("ccsem").__enter__()
        block = nc.Block().__enter__()

        def run(engobj, ename):
            for step in self.steps[ename]:
                kind, fn, waits = step[0], step[1], step[2]
                for key in sorted(waits, key=str):
                    engobj.wait_ge(sems[key], waits[key])
                if kind == "c":
                    last = fn(engobj)
                    assert last is not None, f"compute step on {ename} returned None"
                    last.then_inc(sems[("p", ename)], 1)
                elif kind == "d":
                    plan = step[3]
                    instrs = fn(engobj)
                    assert isinstance(instrs, (list, tuple)) and len(instrs) == len(plan)
                    for ins, (slot, pre) in zip(instrs, plan):
                        ins.then_inc(sems[("d", ename, slot)], 16)
                else:
                    fn(engobj).then_inc(sems[("cc",)], 1)

        block.sync(lambda e: run(e, "sync"))
        block.tensor(lambda e: run(e, "tensor"))
        block.vector(lambda e: run(e, "vector"))
        block.scalar(lambda e: run(e, "scalar"))
        block.gpsimd(lambda e: run(e, "gpsimd"))
        block.__exit__(None, None, None)


class Alloc:
    def __init__(self, nc):
        self.nc = nc
        self.free_list = [(SBUF_BASE, SBUF_TOP)]
        self.live = {}
        self.counter = 0

    def alloc(self, name, shape, dtype, top=False):
        esz = 2 if dtype == BF16 else 4
        nb = (int(np.prod(shape[1:])) * esz + 31) // 32 * 32
        order = reversed(list(enumerate(self.free_list))) if top else enumerate(self.free_list)
        for i, (lo, hi) in order:
            if hi - lo >= nb:
                self.counter += 1
                start = hi - nb if top else lo
                t = self.nc.alloc_sbuf_tensor_at(f"{name}_{self.counter}", list(shape),
                                                 dtype, offset=start)
                self.live[name] = (start, start + nb)
                if top:
                    if start == lo:
                        self.free_list.pop(i)
                    else:
                        self.free_list[i] = (lo, start)
                else:
                    if lo + nb == hi:
                        self.free_list.pop(i)
                    else:
                        self.free_list[i] = (lo + nb, hi)
                return t
        used = sum(v[1] - v[0] for v in self.live.values())
        raise MemoryError(f"SBUF alloc {name} {nb}B failed (used {used}B live="
                          f"{sorted(self.live.items(), key=lambda kv: kv[1][0])})")

    def free(self, *names):
        for name in names:
            self.free_list.append(self.live.pop(name))
        self.free_list.sort()
        merged = []
        for seg in self.free_list:
            if merged and merged[-1][1] >= seg[0]:
                merged[-1] = (merged[-1][0], max(merged[-1][1], seg[1]))
            else:
                merged.append(seg)
        self.free_list = merged


class KB:
    def __init__(self):
        self.nc = bass.Bass("TRN2", target_bir_lowering=False, detect_race_conditions=False)
        self.p = Prog()
        self.al = Alloc(self.nc)
        self.tap_outs = {}
        self.build()

    def decl(self):
        nc, al = self.nc, self.al
        D = lambda n, s, dt=F32: nc.dram_tensor(n, list(s), dt, kind="ExternalInput").ap()
        self.x_in = D("x_fm", [1024, T])
        self.Wq_d = D("Wq", [1024, 1024]); self.Wk_d = D("Wk", [1024, 1024])
        self.WvT_d = D("WvT", [1024, 1024]); self.Wo0_d = D("Wo0", [1024, 1024])
        self.bo0_d = D("bo0", [1, 1024])
        self.wg_d = [D("wg0", [2816, 1024]), D("wg1", [2816, 1024])]
        self.wu_d = [D("wu0", [2816, 1024]), D("wu1", [2816, 1024])]
        self.wd_d = [D("wd0", [1024, 2816]), D("wd1", [1024, 2816])]
        self.WzT_d = D("WzT", [1024, 1024]); self.WhT_d = D("WhT", [1024, 1024])
        self.Wo1T_d = D("Wo1T", [1024, 1024])
        self.bz_d = D("bz_fm", [128, 8]); self.bh_d = D("bh_fm", [128, 8])
        self.bo1_d = D("bo1_fm", [128, 8])
        self.clips_d = D("clips", [1, 4])
        self.m64_d = D("m64row", [128, 1024])
        self.ident_d = D("ident_bf", [128, 128], BF16)
        self.swap64_d = D("swap64_bf", [128, 128], BF16)
        self.u16a_d = D("u16a", [16, 1024], BF16)
        self.u16b_d = D("u16b", [16, 1024], BF16)
        self.cp_d = D("coreparams", [128, 16])
        self.uidx_d = D("uidx", [128, 16], I32)

        self.y_out = nc.dram_tensor("y", [1024, T], F32, kind="ExternalOutput").ap()

        self.x1_save = nc.dram_tensor("x1_save", [1024, T], F32).ap()
        self.h_save = nc.dram_tensor("h_save", [2816, T], F32).ap()
        self.cc_mx_in = [nc.dram_tensor(f"cc_mx_in{i}", [128, 1], F32).ap() for i in range(4)]
        self.cc_mx_out = [nc.dram_tensor(f"cc_mx_out{i}", [128, 1], F32).ap() for i in range(4)]
        self.cc_e_in = nc.dram_tensor("cc_e_in", [64, 512], F32).ap()
        self.cc_e_out = nc.dram_tensor("cc_e_out", [64, 512], F32).ap()
        self.cc_u_in = nc.dram_tensor("cc_u_in", [2048, 1024], F32).ap()
        self.cc_u_out = nc.dram_tensor("cc_u_out", [8192, 1024], F32).ap()
        self.cc_s_in = nc.dram_tensor("cc_s_in", [128, 16], F32).ap()
        self.att_d = nc.dram_tensor("att_d", [16, 16], F32).ap()
        self.cc_s_out = nc.dram_tensor("cc_s_out", [512, 16], F32).ap()

        self.ps = [nc.alloc_psum_tensor(f"ps{i}", [128, 512], F32) for i in range(8)]
        self.psT = self.ps[6][:, :].bitcast(BF16)
        self.psX = self.ps[7]

        A = al.alloc
        self.ident = A("ident", [128, 128], BF16)
        self.swap64 = A("swap64", [128, 128], BF16)
        self.m64 = A("m64", [128, 1024], F32)
        self.u16a = A("u16a", [16, 1024], BF16)
        self.u16b = A("u16b", [16, 1024], BF16)
        self.onesbf = A("onesbf", [128, 128], BF16)
        self.ones64 = A("ones64", [64, 1], F32)
        self.onesrow = A("onesrow", [1, 128], F32)
        self.zeros1 = A("zeros1", [128, 1], F32)
        self.eps_t = A("eps_t", [128, 1], F32)
        self.cp = A("cp", [128, 16], F32)
        self.uidx = A("uidx", [128, 16], I32)
        self.clips = A("clips", [1, 4], F32)
        self.bo_row = A("bo_row", [1, 1024], F32)
        self.bo_b = A("bo_b", [128, 1024], F32)
        self.bz_fm = A("bz_fm", [128, 8], F32)
        self.bh_fm = A("bh_fm", [128, 8], F32)
        self.bo1_fm = A("bo1_fm", [128, 8], F32)
        self.negbz = A("negbz", [128, 8], F32)
        self.bhm05 = A("bhm05", [128, 8], F32)
        self.scal = A("scal", [128, 8], F32)
        self.smrow = A("smrow", [1, 192], F32)
        self.xmaxc = A("xmaxc", [128, 12], F32)
        self.hmall = A("hmall", [128, 96], F32)
        self.csg = A("csg", [128, 22], F32)
        self.csu = A("csu", [128, 22], F32)
        self.rmd = A("rmd", [128, 8], F32)
        self.csd = A("csd", [128, 8], F32)
        self.ssum = A("ssum", [128, 16], F32)
        self.sgth = [A(f"sgth{j}", [128, 16], F32) for j in range(4)]
        self.pfx = A("pfx", [128, 24], F32)
        self.attf = A("attf", [1, 256], F32)
        self.attT = A("attT", [16, 16], BF16)
        self.Ep = A("Ep", [128, 128], F32)
        self.Epk = A("Epk", [64, 768], F32)
        self.En = A("En", [64, 256], F32)
        self.erdbg = A("erdbg", [1, 256], F32)
        self.qsc = A("qsc", [128, 4], F32)
        self.mt = A("mt", [128, T], F32)
        self.rstdt = A("rstdt", [128, T], F32)
        self.tfA = A("tfA", [128, T], F32)
        self.tfB = A("tfB", [128, T], F32)

    def tap(self, name, src_fn, nrow, ncol, dt, after):
        if name not in TAPS:
            return
        out = self.nc.dram_tensor("tap_" + name, [128 * nrow, ncol], dt,
                                  kind="ExternalOutput").ap()
        self.tap_outs[name] = out
        self.p.d("scalar", nrow,
                 lambda e: [e.dma_start(out=out[128 * c:128 * c + 128, :], in_=src_fn(c))
                            for c in range(nrow)], after=after)

    def setup(self):
        p = self.p
        h_cst1 = p.d("sync", 6, lambda e: [
            e.dma_start(out=self.m64[:, :], in_=self.m64_d),
            e.dma_start(out=self.ident[:, :], in_=self.ident_d),
            e.dma_start(out=self.swap64[:, :], in_=self.swap64_d),
            e.dma_start(out=self.u16a[:, :], in_=self.u16a_d),
            e.dma_start(out=self.u16b[:, :], in_=self.u16b_d),
            e.dma_start(out=self.cp[:, :], in_=self.cp_d),
        ])
        h_cst2 = p.d("sync", 6, lambda e: [
            e.dma_start(out=self.uidx[:, :], in_=self.uidx_d),
            e.dma_start(out=self.clips[:, :], in_=self.clips_d),
            e.dma_start(out=self.bo_row[:, :], in_=self.bo0_d),
            e.dma_start(out=self.bz_fm[:, :], in_=self.bz_d),
            e.dma_start(out=self.bh_fm[:, :], in_=self.bh_d),
            e.dma_start(out=self.bo1_fm[:, :], in_=self.bo1_d),
        ])
        self.h_cst = h_cst1 + h_cst2

        def init_c(e):
            e.memset(self.onesbf[:, :], 1.0)
            e.memset(self.ones64[:, :], 1.0)
            e.memset(self.onesrow[:, :], 1.0)
            e.memset(self.zeros1[:, :], 0.0)
            e.memset(self.eps_t[:, :], 1e-5)
            return e.memset(self.hmall[:, :], 0.0)
        self.h_init = p.c("vector", init_c)

        def bias_prep(e):
            e.tensor_scalar(out=self.negbz[:, :], in0=self.bz_fm[:, :], scalar1=-1.0,
                            scalar2=None, op0=ALU.mult)
            return e.tensor_scalar(out=self.bhm05[:, :], in0=self.bh_fm[:, :], scalar1=-0.5,
                                   scalar2=None, op0=ALU.add)
        self.h_bias = p.c("vector", bias_prep, after=[self.h_cst])

        def bob_mm(e):
            e.matmul(self.ps[4][:, :], self.onesrow[0:1, :], self.bo_row[0:1, 0:512],
                     start=True, stop=True)
            return e.matmul(self.ps[5][:, :], self.onesrow[0:1, :], self.bo_row[0:1, 512:1024],
                            start=True, stop=True)
        hm = p.c("tensor", bob_mm, after=[self.h_cst, self.h_init])

        def bob_ev(e):
            e.tensor_copy(out=self.bo_b[:, 0:512], in_=self.ps[4][:, :])
            return e.tensor_copy(out=self.bo_b[:, 512:1024], in_=self.ps[5][:, :])
        self.h_bob = p.c("vector", bob_ev, after=[hm])

    def emit_ln(self, layer, x_src, deps):
        p, al = self.p, self.al
        xn, tfA, tfB = self.xn, self.tfA, self.tfB
        xt = self.xt
        ps = self.ps
        mxslot = 0 if layer == 0 else 2
        cx_col = 0 if layer == 0 else 2
        tbf = [al.alloc(f"tbf{i}", [128, T], BF16) for i in range(2)]
        tsq = al.alloc("tsq", [128, T], F32)

        hld = [None, None]
        hmm = []
        hsplit = []
        for c in range(CH):
            b = c % 2
            hld[b] = p.d("sync", 1,
                         lambda e, c=c, b=b: [e.dma_start(out=xt[b][:, :],
                                                          in_=x_src[128 * c:128 * c + 128, :])],
                         after=list(deps) + ([hmm[c - 2]] if c >= 2 else []))
            h1 = p.c("vector", lambda e, b=b: e.tensor_copy(out=tbf[0][:, :], in_=xt[b][:, :]),
                     after=[hld[b], self.h_init] + ([hmm[c - 1]] if c >= 1 else []))
            h2 = p.c("vector", lambda e, b=b: e.tensor_tensor(out=tbf[1][:, :], in0=xt[b][:, :],
                                                              in1=tbf[0][:, :], op=ALU.subtract),
                     after=[h1])
            hsq = p.c("scalar", lambda e, b=b: e.activation(out=tsq[:, :], in_=xt[b][:, :],
                                                            func=AF.Square,
                                                            bias=self.zeros1[:, 0:1]),
                      after=[hld[b], self.h_init] + ([hsplit[c - 1]] if c >= 1 else []))

            def mm_x(e, c=c):
                last = None
                for part in range(2):
                    for n in range(NT):
                        last = e.matmul(ps[n][:, :], self.onesbf[:, :],
                                        tbf[part][:, 512 * n:512 * n + 512],
                                        start=(c == 0 and part == 0),
                                        stop=(c == CH - 1 and part == 1))
                return last
            hx = p.c("tensor", mm_x, after=[h2])
            h4 = p.c("vector", lambda e: e.tensor_copy(out=tbf[0][:, :], in_=tsq[:, :]),
                     after=[hsq, hx])
            h5 = p.c("vector", lambda e: e.tensor_tensor(out=tbf[1][:, :], in0=tsq[:, :],
                                                         in1=tbf[0][:, :], op=ALU.subtract),
                     after=[h4])
            hsplit.append(h5)

            def mm_sq(e, c=c):
                last = None
                for part in range(2):
                    for n in range(NT):
                        last = e.matmul(ps[4 + n][:, :], self.onesbf[:, :],
                                        tbf[part][:, 512 * n:512 * n + 512],
                                        start=(c == 0 and part == 0),
                                        stop=(c == CH - 1 and part == 1))
                return last
            hmm.append(p.c("tensor", mm_sq, after=[h5]))

        def fin_m(e):
            last = None
            for n in range(NT):
                last = e.tensor_scalar(out=self.mt[:, 512 * n:512 * n + 512], in0=ps[n][:, :],
                                       scalar1=1.0 / 1024.0, scalar2=None, op0=ALU.mult)
            return last
        hm = p.c("vector", fin_m, after=[hmm[-1]])

        def fin_v(e):
            last = None
            for n in range(NT):
                sl = slice(512 * n, 512 * n + 512)
                e.tensor_tensor(out=tfA[:, sl], in0=self.mt[:, sl], in1=self.mt[:, sl],
                                op=ALU.mult)
                last = e.scalar_tensor_tensor(out=tfB[:, sl], in0=ps[4 + n][:, :],
                                              scalar=1.0 / 1024.0, in1=tfA[:, sl],
                                              op0=ALU.mult, op1=ALU.subtract)
            return last
        hv = p.c("vector", fin_v, after=[hm])
        hsd = p.c("scalar", lambda e: e.activation(out=tsq[:, :], in_=tfB[:, :], func=AF.Sqrt,
                                                   bias=self.eps_t[:, 0:1]), after=[hv])
        hrs = p.c("vector", lambda e: e.reciprocal(out=self.rstdt[:, :], in_=tsq[:, :]),
                  after=[hsd])

        hxn = []
        for c in range(CH):
            b = c % 2
            h0 = p.d("sync", 1,
                     lambda e, c=c, b=b: [e.dma_start(out=xt[b][:, :],
                                                      in_=x_src[128 * c:128 * c + 128, :])],
                     after=[hmm[-1]] + ([hxn[c - 2]] if c >= 2 else []))

            def xn_c(e, c=c, b=b):
                e.tensor_tensor(out=tfA[:, :], in0=xt[b][:, :], in1=self.mt[:, :],
                                op=ALU.subtract)
                e.tensor_tensor(out=tfB[:, :], in0=tfA[:, :], in1=self.rstdt[:, :], op=ALU.mult)
                e.tensor_copy(out=xn[:, T * c:T * c + T], in_=tfB[:, :])
                return e.reduce_max(out=self.xmaxc[:, c:c + 1], in_=tfB[:, :], axis=AX.X,
                                    apply_absolute_value=True)
            hxn.append(p.c("vector", xn_c, after=[h0, hrs]))

        def cm(e):
            e.drain()
            return e.reduce_max(out=self.xmaxc[:, 8:9], in_=self.xmaxc[:, 0:8], axis=AX.X,
                                apply_absolute_value=True)
        hcm = p.c("vector", cm, after=hxn)
        hdm = p.d("scalar", 1, lambda e: [e.dma_start(out=self.cc_mx_in[mxslot],
                                                      in_=self.xmaxc[:, 8:9])], after=[hcm])
        hccm = p.cc(lambda e: e.collective_compute(
            "AllReduce", ALU.max, replica_groups=G8,
            ins=[self.cc_mx_in[mxslot].opt()], outs=[self.cc_mx_out[mxslot].opt()]),
            after=[hdm])
        def fin_sx():
            hrb = p.d("gpsimd", 1, lambda e: [e.dma_start(
                out=self.smrow[0:1, 64:192],
                in_=self.cc_mx_out[mxslot].rearrange("p one -> one p"))], after=[hccm])
            hg1 = p.c("vector", lambda e: e.reduce_max(out=self.smrow[0:1, 0:1],
                                                       in_=self.smrow[0:1, 64:192], axis=AX.X),
                      after=[hrb])
            hg2 = p.c("scalar", lambda e: e.activation(out=self.smrow[0:1, 4:5],
                                                       in_=self.clips[0:1, cx_col:cx_col + 1],
                                                       func=AF.Sigmoid,
                                                       bias=self.zeros1[0:1, 0:1]),
                      after=[self.h_cst, hg1])

            def sx_fin(e):
                e.drain()
                e.tensor_scalar(out=self.smrow[0:1, 5:6], in0=self.smrow[0:1, 4:5],
                                scalar1=self.smrow[0:1, 0:1], scalar2=1.0 / 127.0,
                                op0=ALU.mult, op1=ALU.mult)
                e.drain()
                return e.reciprocal(out=self.smrow[0:1, 6:7], in_=self.smrow[0:1, 5:6])
            hg3 = p.c("vector", sx_fin, after=[hg2])

            def sx_bc(e):
                e.matmul(self.psX[:, 0:1], self.onesrow[0:1, :], self.smrow[0:1, 5:6],
                         start=True, stop=True)
                return e.matmul(self.psX[:, 1:2], self.onesrow[0:1, :], self.smrow[0:1, 6:7],
                                start=True, stop=True)
            hg4 = p.c("tensor", sx_bc, after=[hg3])

            def sx_ev(e):
                e.tensor_copy(out=self.scal[:, 0:1], in_=self.psX[:, 0:1])
                e.tensor_copy(out=self.scal[:, 1:2], in_=self.psX[:, 1:2])
                return e.drain()
            return p.c("vector", sx_ev, after=[hg4])

        al.free("tbf0", "tbf1", "tsq")
        return {"xn": hxn[-1], "stats": hrs, "sx": None, "sx_fin": fin_sx,
                "mm_last": hmm[-1]}

    def emit_quant(self, x_src, deps):
        p = self.p
        xq, xt, tfA, tfB = self.xq, self.xt, self.tfA, self.tfB
        hs = []
        for c in range(CH):
            b = c % 2
            hld = p.d("sync", 1,
                      lambda e, c=c, b=b: [e.dma_start(out=xt[b][:, :],
                                                       in_=x_src[128 * c:128 * c + 128, :])],
                      after=list(deps) + ([hs[c - 2]] if c >= 2 else []))

            def q_c(e, c=c, b=b):
                e.tensor_tensor(out=tfA[:, :], in0=xt[b][:, :], in1=self.mt[:, :],
                                op=ALU.subtract)
                e.tensor_tensor(out=tfB[:, :], in0=tfA[:, :], in1=self.rstdt[:, :], op=ALU.mult)
                e.tensor_scalar(out=tfA[:, :], in0=tfB[:, :], scalar1=self.scal[:, 1:2],
                                scalar2=MAGIC, op0=ALU.mult, op1=ALU.add)
                e.tensor_scalar(out=tfB[:, :], in0=tfA[:, :], scalar1=MAGIC, scalar2=127.0,
                                op0=ALU.subtract, op1=ALU.min)
                return e.tensor_scalar(out=xq[:, T * c:T * c + T], in0=tfB[:, :],
                                       scalar1=-128.0, scalar2=None, op0=ALU.max)
            hs.append(p.c("vector", q_c, after=[hld] + list(deps)))
        return hs[-1]

    def emit_attn0(self, hln, post_energy_hook=None):
        p, al = self.p, self.al
        ps, psT = self.ps, self.psT
        xn = self.xn

        Wqb = al.alloc("Wqb", [128, 8 * 1024], BF16)
        Wkb = al.alloc("Wkb", [128, 8 * 1024], BF16)
        h_wq = p.d("gpsimd", 1, lambda e: [e.dma_start(
            out=Wqb[:, :].rearrange("p (c n) -> p c n", c=8),
            in_=self.Wq_d.rearrange("(c p) n -> p c n", p=128))],
            after=[hln["stats"], hln["mm_last"]])
        h_wk = p.d("gpsimd", 1, lambda e: [e.dma_start(
            out=Wkb[:, :].rearrange("p (c n) -> p c n", c=8),
            in_=self.Wk_d.rearrange("(c p) n -> p c n", p=128))],
            after=[hln["stats"], hln["mm_last"]])
        qT = al.alloc("qT", [128, 16 * 1024], BF16)
        kT = al.alloc("kT", [128, 16 * 1024], BF16)

        def proj_tokmajor(dst, Wb, h_w, hprev_ev=None):
            ev = []
            for tt in range(16):
                b0, b1 = (0, 1) if tt % 2 == 0 else (2, 3)

                def mm(e, tt=tt, b0=b0, b1=b1, Wb=Wb):
                    last = None
                    for k in range(CH):
                        e.matmul(ps[b0][:, :],
                                 xn[:, T * k + 128 * tt:T * k + 128 * tt + 128],
                                 Wb[:, 1024 * k:1024 * k + 512],
                                 start=(k == 0), stop=(k == CH - 1))
                        last = e.matmul(ps[b1][:, :],
                                        xn[:, T * k + 128 * tt:T * k + 128 * tt + 128],
                                        Wb[:, 1024 * k + 512:1024 * k + 1024],
                                        start=(k == 0), stop=(k == CH - 1))
                    return last
                bank_wait = []
                if tt >= 2:
                    bank_wait = [ev[tt - 2]]
                elif hprev_ev is not None:
                    bank_wait = [hprev_ev[tt - 2]]
                hmmx = p.c("tensor", mm, after=[h_w, hln["xn"]] + bank_wait)

                def evf(e, tt=tt, b0=b0, b1=b1, dst=dst):
                    e.tensor_copy(out=dst[:, 1024 * tt:1024 * tt + 512], in_=ps[b0][:, :])
                    return e.tensor_copy(out=dst[:, 1024 * tt + 512:1024 * tt + 1024],
                                         in_=ps[b1][:, :])
                ev.append(p.c("vector", evf, after=[hmmx]))
            return ev

        if getattr(self, "mtprobe", None):
            pass
        hq_ev = proj_tokmajor(qT, Wqb, h_wq)
        if getattr(self, "mtprobe", None):
            pass
        hk_ev = proj_tokmajor(kT, Wkb, h_wk, hprev_ev=hq_ev)
        if getattr(self, "mtprobe", None):
            pass

        hg_ev = []
        for qc in range(CH):
            b0, b1 = (0, 1) if qc % 2 == 0 else (2, 3)

            def mm_g(e, qc=qc, b0=b0, b1=b1):
                last = None
                for tt in range(16):
                    e.matmul(ps[b0][:, :],
                             qT[:, 1024 * tt + 128 * qc:1024 * tt + 128 * qc + 128],
                             kT[:, 1024 * tt:1024 * tt + 512],
                             start=(tt == 0), stop=(tt == 15))
                    last = e.matmul(ps[b1][:, :],
                                    qT[:, 1024 * tt + 128 * qc:1024 * tt + 128 * qc + 128],
                                    kT[:, 1024 * tt + 512:1024 * tt + 1024],
                                    start=(tt == 0), stop=(tt == 15))
                return last
            bank_wait = [hg_ev[qc - 2]] if qc >= 2 else [hk_ev[-2], hk_ev[-1]]
            hmg = p.c("tensor", mm_g, after=bank_wait)

            def ev_g(e, qc=qc, b0=b0, b1=b1):
                e.tensor_tensor(out=self.tfA[:, 0:512], in0=ps[b0][:, :],
                                in1=self.m64[:, 0:512], op=ALU.mult)
                e.tensor_tensor(out=self.tfA[:, 512:1024], in0=ps[b1][:, :],
                                in1=self.m64[:, 512:1024], op=ALU.mult)
                return e.reduce_sum(out=self.Ep[:, 16 * qc:16 * qc + 16],
                                    in_=self.tfA[:, 0:1024].rearrange("p (a b) -> p a b", b=64),
                                    axis=AX.X)
            hg_ev.append(p.c("vector", ev_g, after=[hmg, self.h_cst]))

        def pack_e(e):
            for qc in range(CH):
                for par in range(2):
                    a = 2 * qc + par
                    e.tensor_copy(out=self.Epk[:, 16 * a:16 * a + 16],
                                  in_=self.Ep[64 * par:64 * par + 64, 16 * qc:16 * qc + 16])
            e.tensor_scalar(out=self.Epk[:, 256:512], in0=self.Epk[:, 0:256],
                            scalar1=self.cp[0:64, 0:1], scalar2=None, op0=ALU.mult)
            return e.tensor_scalar(out=self.Epk[:, 512:768], in0=self.Epk[:, 0:256],
                                   scalar1=self.cp[0:64, 1:2], scalar2=None, op0=ALU.mult)
        hpk = p.c("vector", pack_e, after=hg_ev)
        hde = p.d("scalar", 1, lambda e: [e.dma_start(out=self.cc_e_in,
                                                      in_=self.Epk[0:64, 256:768])],
                  after=[hpk])
        hce = p.cc(lambda e: e.collective_compute(
            "AllReduce", ALU.add, replica_groups=G8,
            ins=[self.cc_e_in.opt()], outs=[self.cc_e_out.opt()]), after=[hde])
        hre = p.d("gpsimd", 1, lambda e: [e.dma_start(out=self.Epk[0:64, 256:768],
                                                      in_=self.cc_e_out)], after=[hce])
        hln["sx"] = hln["sx_fin"]()

        def en_fin(e):
            e.tensor_scalar(out=self.En[:, :], in0=self.Epk[0:64, 256:512],
                            scalar1=self.cp[0:64, 0:1], scalar2=None, op0=ALU.mult)
            e.drain()
            return e.scalar_tensor_tensor(out=self.En[:, :], in0=self.Epk[0:64, 512:768],
                                          scalar=self.cp[0:64, 1:2], in1=self.En[:, :],
                                          op0=ALU.mult, op1=ALU.add)
        hen = p.c("vector", en_fin, after=[hre])
        if getattr(self, "mtprobe", None):
            pass

        hrow = p.c("tensor", lambda e: e.matmul(self.psX[0:1, 0:256], self.ones64[0:64, 0:1],
                                                self.En[0:64, :], start=True, stop=True),
                   after=[hen, hg_ev[-1], hln["sx"]])
        def rowc(e):
            e.tensor_copy(out=self.erdbg[0:1, :], in_=self.psX[0:1, 0:256])
            return e.drain()
        hrowc = p.c("vector", rowc, after=[hrow])

        def sm_mx(e):
            for a in range(16):
                e.reduce_max(out=self.smrow[0:1, 96 + a:97 + a],
                             in_=self.erdbg[0:1, 16 * a:16 * a + 16], axis=AX.X)
            e.drain()
            last = None
            for a in range(16):
                last = e.tensor_scalar(out=self.attf[0:1, 16 * a:16 * a + 16],
                                       in0=self.erdbg[0:1, 16 * a:16 * a + 16],
                                       scalar1=self.smrow[0:1, 96 + a:97 + a],
                                       scalar2=None, op0=ALU.subtract)
            return last
        hmx = p.c("vector", sm_mx, after=[hrowc])
        hex_ = p.c("scalar", lambda e: e.activation(
            out=self.attf[0:1, :], in_=self.attf[0:1, :], func=AF.Exp, scale=0.125,
            bias=self.zeros1[0:1, 0:1]), after=[hmx])

        def sm_nrm(e):
            e.reduce_sum(out=self.smrow[0:1, 128:144],
                         in_=self.attf[0:1, :].rearrange("p (a b) -> p a b", b=16), axis=AX.X)
            e.drain()
            e.reciprocal(out=self.smrow[0:1, 144:160], in_=self.smrow[0:1, 128:144])
            e.drain()
            last = None
            for a in range(16):
                last = e.tensor_scalar(out=self.attf[0:1, 16 * a:16 * a + 16],
                                       in0=self.attf[0:1, 16 * a:16 * a + 16],
                                       scalar1=self.smrow[0:1, 144 + a:145 + a],
                                       scalar2=None, op0=ALU.mult)
            return last
        hsm = p.c("vector", sm_nrm, after=[hex_])
        if DEV_STOP == "att0":
            hfin = p.d("gpsimd", 9, lambda e: [
                e.dma_start(out=self.y_out[0:64, 0:768], in_=self.Epk[0:64, :]),
                e.dma_start(out=self.y_out[64:128, 0:256], in_=self.En[0:64, :]),
                e.dma_start(out=self.y_out[128:129, 0:256], in_=self.attf[0:1, :]),
                e.dma_start(out=self.y_out[129:130, 0:256], in_=self.erdbg[0:1, :]),
                e.dma_start(out=self.y_out[130:131, 0:192], in_=self.smrow[0:1, :]),
                e.dma_start(out=self.y_out[256:384, 0:1024], in_=qT[:, 0:1024]),
                e.dma_start(out=self.y_out[384:512, 0:1024], in_=kT[:, 0:1024]),
                e.dma_start(out=self.y_out[512:640, 0:1024], in_=Wqb[:, 0:1024]),
                e.dma_start(out=self.y_out[640:768, 0:128], in_=self.Ep[:, :]),
            ], after=[hsm, hen])
            p.c("sync", lambda e: e.nop(), after=[hfin])
            self.p.emit(self.nc)
            self.dev_stopped = True
            return {}

        hat0 = p.d("gpsimd", 1, lambda e: [e.dma_start(
            out=self.att_d.rearrange("a b -> (a b)").rearrange("(one f) -> one f", one=1),
            in_=self.attf[0:1, :])], after=[hsm])
        hat = p.d("gpsimd", 1, lambda e: [e.dma_start(
            out=self.attT[:, :], in_=self.att_d)], after=[hat0])

        al.free("qT", "kT", "Wqb", "Wkb")
        WvTb = al.alloc("WvTb", [128, 8 * 1024], BF16)
        atxp = al.alloc("atxp", [16, 1024], BF16)
        Aexp = al.alloc("Aexp", [128, 8 * 1024], BF16)
        h_wv = p.d("gpsimd", 1, lambda e: [e.dma_start(
            out=WvTb[:, :].rearrange("p (c n) -> p c n", c=8),
            in_=self.WvT_d.rearrange("(c p) n -> p c n", p=128))], after=[hg_ev[-1]])
        if post_energy_hook:
            post_energy_hook()

        def mm_ax(e):
            e.matmul(ps[4][0:16, :], self.attT[:, :], self.u16a[:, 0:512], start=True, stop=True)
            return e.matmul(ps[5][0:16, :], self.attT[:, :], self.u16a[:, 512:1024],
                            start=True, stop=True)
        hax = p.c("tensor", mm_ax, after=[hat, self.h_cst, self.h_bob])

        def ev_ax(e):
            e.tensor_copy(out=atxp[:, 0:512], in_=ps[4][0:16, :])
            return e.tensor_copy(out=atxp[:, 512:1024], in_=ps[5][0:16, :])
        hax2 = p.c("vector", ev_ax, after=[hax, hg_ev[-1]])

        hAe = []
        for vc in range(CH):
            b0, b1 = (0, 1) if vc % 2 == 0 else (2, 3)

            def mm_ae(e, vc=vc, b0=b0, b1=b1):
                e.matmul(ps[b0][:, :], self.u16b[:, 128 * vc:128 * vc + 128], atxp[:, 0:512],
                         start=True, stop=True)
                return e.matmul(ps[b1][:, :], self.u16b[:, 128 * vc:128 * vc + 128],
                                atxp[:, 512:1024], start=True, stop=True)
            hma = p.c("tensor", mm_ae, after=[hax2] + ([hAe[vc - 2]] if vc >= 2 else []))

            def ev_ae(e, vc=vc, b0=b0, b1=b1):
                e.tensor_tensor(out=Aexp[:, 1024 * vc:1024 * vc + 512], in0=ps[b0][:, :],
                                in1=self.m64[:, 0:512], op=ALU.mult)
                return e.tensor_tensor(out=Aexp[:, 1024 * vc + 512:1024 * vc + 1024],
                                       in0=ps[b1][:, :], in1=self.m64[:, 512:1024], op=ALU.mult)
            hAe.append(p.c("vector", ev_ae, after=[hma]))

        Wvmix = al.alloc("Wvmix", [128, 8 * 1024], BF16)
        hWm = []
        for xc in range(CH):
            b0, b1 = (0, 1) if xc % 2 == 0 else (2, 3)

            def mm_wm(e, xc=xc, b0=b0, b1=b1):
                last = None
                for k in range(CH):
                    e.matmul(ps[b0][:, :],
                             WvTb[:, 1024 * k + 128 * xc:1024 * k + 128 * xc + 128],
                             Aexp[:, 1024 * k:1024 * k + 512],
                             start=(k == 0), stop=(k == CH - 1))
                    last = e.matmul(ps[b1][:, :],
                                    WvTb[:, 1024 * k + 128 * xc:1024 * k + 128 * xc + 128],
                                    Aexp[:, 1024 * k + 512:1024 * k + 1024],
                                    start=(k == 0), stop=(k == CH - 1))
                return last
            hmw = p.c("tensor", mm_wm, after=[hAe[-1], h_wv] + ([hWm[xc - 2]] if xc >= 2 else []))

            def ev_wm(e, xc=xc, b0=b0, b1=b1):
                e.tensor_copy(out=Wvmix[:, 1024 * xc:1024 * xc + 512], in_=ps[b0][:, :])
                return e.tensor_copy(out=Wvmix[:, 1024 * xc + 512:1024 * xc + 1024],
                                     in_=ps[b1][:, :])
            hWm.append(p.c("vector", ev_wm, after=[hmw]))

        if getattr(self, "mtprobe", None):
            self.mtprobe(0, [hWm[-1]])
        al.free("Aexp", "WvTb", "atxp")
        vmixF = al.alloc("vmixF", [128, 8 * T], BF16)
        hvF = []
        for mc in range(CH):
            def mm_vf(e, mc=mc):
                last = None
                for k in range(CH):
                    for n in range(NT):
                        last = e.matmul(ps[n][:, :],
                                        Wvmix[:, 1024 * k + 128 * mc:1024 * k + 128 * mc + 128],
                                        xn[:, T * k + 512 * n:T * k + 512 * n + 512],
                                        start=(k == 0), stop=(k == CH - 1))
                return last
            hmv = p.c("tensor", mm_vf, after=[hWm[-1]] + ([hvF[mc - 1]] if mc >= 1 else []))

            def ev_vf(e, mc=mc):
                last = None
                for n in range(NT):
                    last = e.tensor_copy(out=vmixF[:, T * mc + 512 * n:T * mc + 512 * n + 512],
                                         in_=ps[n][:, :])
                return last
            hvF.append(p.c("vector", ev_vf, after=[hmv]))
        self.tap("vmixF", lambda c: vmixF[:, T * c:T * c + T], 8, T, BF16, [hvF[-1]])

        al.free("Wvmix")
        Wob = al.alloc("Wob", [128, 8 * 1024], BF16)
        h_wo = p.d("gpsimd", 1, lambda e: [e.dma_start(
            out=Wob[:, :].rearrange("p (c n) -> p c n", c=8),
            in_=self.Wo0_d.rearrange("(c p) n -> p c n", p=128))], after=[hvF[-1]])
        vmixr = [al.alloc(f"vmixr{i}", [128, 8 * 128], BF16) for i in range(2)]
        vstg = [al.alloc(f"vstg{i}", [128, 8 * 256], BF16) for i in range(2)]
        ubuf = [al.alloc(f"ubuf{i}", [128, 1024], F32) for i in range(2)]
        hvr = [None, None]
        hvp = [None, None]
        hvpe = [None, None]
        humm = [None, None]
        huev = [None, None]
        hu_dma = []
        for a in range(16):
            b = a % 2
            r0 = 64 * (a % 2)

            def cp_vr(e, a=a, b=b, r0=r0):
                last = None
                base = T * (a // 2)
                for j in range(16):
                    last = e.tensor_copy(
                        out=vstg[b][r0:r0 + 64, 256 * (j // 2) + 128 * (j % 2):
                                    256 * (j // 2) + 128 * (j % 2) + 128],
                        in_=vmixF[r0:r0 + 64, base + j:base + T:16])
                return last
            hvr[b] = p.c("vector", cp_vr, after=[hvF[a // 2]] + ([hvp[b]] if hvp[b] else []))

            def perm_vr(e, a=a, b=b, r0=r0):
                last = None
                b4 = 4 if b == 0 else 5
                for ci in range(CH):
                    for js in range(2):
                        tgt = 64 * js
                        sel = self.ident if tgt == r0 else self.swap64
                        last = e.matmul(
                            ps[b4][:, 0:128].bitcast(BF16)[:, 0:128] if False else ps[b4][:, 0:128],
                            sel[r0:r0 + 64, :],
                            vstg[b][r0:r0 + 64, 256 * ci + 128 * js:256 * ci + 128 * js + 128],
                            start=(js == 0), stop=(js == 1))
                return last

            def perm_and_evac(e, a=a, b=b, r0=r0):
                return None
            hplast = [hvr[b]]
            b4 = 4 if b == 0 else 5
            def perm_all(e, a=a, b=b, r0=r0, b4=b4):
                last = None
                for ci in range(CH):
                    for js in range(2):
                        tgt = 64 * js
                        sel = self.ident if tgt == r0 else self.swap64
                        last = e.matmul(
                            ps[b4][:, 128 * (ci % 4):128 * (ci % 4) + 128],
                            sel[r0:r0 + 64, :],
                            vstg[b][r0:r0 + 64, 256 * ci + 128 * js:256 * ci + 128 * js + 128],
                            start=(js == 0), stop=(js == 1))
                return last
            def perm_half(e, a=a, b=b, r0=r0, b4=b4, half=0):
                last = None
                for ci in range(4 * half, 4 * half + 4):
                    for js in range(2):
                        tgt = 64 * js
                        sel = self.ident if tgt == r0 else self.swap64
                        last = e.matmul(
                            ps[b4][:, 128 * (ci % 4):128 * (ci % 4) + 128],
                            sel[r0:r0 + 64, :],
                            vstg[b][r0:r0 + 64, 256 * ci + 128 * js:256 * ci + 128 * js + 128],
                            start=(js == 0), stop=(js == 1))
                return last
            def evac_half(e, b=b, b4=b4, half=0):
                last = None
                for ci in range(4 * half, 4 * half + 4):
                    last = e.tensor_copy(out=vmixr[b][:, 128 * ci:128 * ci + 128],
                                         in_=ps[b4][:, 128 * (ci % 4):128 * (ci % 4) + 128])
                return last
            import functools
            hp0 = p.c("tensor", functools.partial(perm_half, half=0),
                      after=[hvr[b]] + ([hvpe[b]] if hvpe[b] else []))
            he0 = p.c("vector", functools.partial(evac_half, half=0), after=[hp0])
            hp1 = p.c("tensor", functools.partial(perm_half, half=1), after=[he0])
            he1 = p.c("vector", functools.partial(evac_half, half=1), after=[hp1])
            hvp[b] = he1
            hvpe[b] = he1

            def mm_u(e, a=a, b=b):
                last = None
                b0, b1 = (0, 1) if b == 0 else (2, 3)
                for ci in range(CH):
                    e.matmul(ps[b0][:, :], vmixr[b][:, 128 * ci:128 * ci + 128],
                             Wob[:, 1024 * ci:1024 * ci + 512],
                             start=(ci == 0), stop=(ci == CH - 1))
                    last = e.matmul(ps[b1][:, :], vmixr[b][:, 128 * ci:128 * ci + 128],
                                    Wob[:, 1024 * ci + 512:1024 * ci + 1024],
                                    start=(ci == 0), stop=(ci == CH - 1))
                return last
            humm[b] = p.c("tensor", mm_u, after=[he1, h_wo] + ([huev[b]] if huev[b] else []))

            def ev_u(e, a=a, b=b):
                b0, b1 = (0, 1) if b == 0 else (2, 3)
                e.tensor_tensor(out=ubuf[b][:, 0:512], in0=ps[b0][:, :],
                                in1=self.bo_b[:, 0:512], op=ALU.add)
                return e.tensor_tensor(out=ubuf[b][:, 512:1024], in0=ps[b1][:, :],
                                       in1=self.bo_b[:, 512:1024], op=ALU.add)
            huev[b] = p.c("vector", ev_u, after=[humm[b], self.h_bob] +
                          ([hu_dma[-2]] if len(hu_dma) >= 2 else []))
            hu_dma.append(p.d("scalar", 1, lambda e, a=a, b=b: [
                e.dma_start(out=self.cc_u_in[128 * a:128 * a + 128, :], in_=ubuf[b][:, :])],
                after=[huev[b]]))

        if getattr(self, "mtprobe", None):
            self.mtprobe(1, [hvF[-1]])
            self.mtprobe(2, [huev[0], huev[1]])
        hag = p.cc(lambda e: e.collective_compute(
            "AllGather", ALU.bypass, replica_groups=G4,
            ins=[self.cc_u_in.opt()], outs=[self.cc_u_out.opt()]), after=hu_dma)

        return {"ag": hag, "vf": hvF[-1], "last_reads": [humm[0], humm[1], hu_dma[-1],
                                                         hu_dma[-2], hvr[0], hvr[1]]}

    def emit_consume_round(self, rr, hag, rounds, extra_after):
        p = self.p
        r, m = rr // 4, rr % 4
        b = rr % 2
        ust, ustb = self.ust, self.ustb
        prev2 = rounds[rr - 2] if rr >= 2 else None
        prev1 = rounds[rr - 1] if rr >= 1 else None
        hgat = p.d("gpsimd", 1, lambda e, r=r, m=m, b=b: [
            e.indirect_dma_start(
                out=ust[b][:, :], out_offset=None,
                in_=self.cc_u_out,
                in_offset=bass.IndirectOffsetOnAxis(
                    ap=self.uidx[:, 4 * r + m:4 * r + m + 1], axis=0))],
            after=[hag] + list(extra_after) + ([prev2["cst"]] if prev2 else []))
        hcst = p.c("scalar", lambda e, b=b: e.copy(out=ustb[b][:, :], in_=ust[b][:, :]),
                   after=[hgat] + ([prev2["tp"]] if prev2 else []))

        def tp(e, b=b):
            last = None
            for c in range(CH):
                last = e.transpose(self.psT[:, 128 * c:128 * c + 128],
                                   ustb[b][:, 128 * c:128 * c + 128], self.ident[:, :])
            return last
        htp = p.c("tensor", tp, after=[hcst] + ([prev1["ev"]] if prev1 else []) + extra_after)

        def ev(e, r=r, m=m):
            last = None
            for c in range(CH):
                last = e.copy(out=self.s_fm[:, T * c + 512 * m + 128 * r:
                                            T * c + 512 * m + 128 * r + 128],
                              in_=self.psT[:, 128 * c:128 * c + 128])
            return last
        hev = p.c("scalar", ev, after=[htp] + list(extra_after))
        return {"gat": hgat, "cst": hcst, "tp": htp, "ev": hev}

    def emit_ffn(self, layer, s_get, s_ready, x_src, x1_dst, deps, hook=None,
                 mid_free=None):
        p, al = self.p, self.al
        ps, psT = self.ps, self.psT
        xq = self.xq
        cd_col = 1 if layer == 0 else 3
        mxslot = 1 if layer == 0 else 3
        wg_d, wu_d, wd_d = self.wg_d[layer], self.wu_d[layer], self.wd_d[layer]

        wnat = [al.alloc(f"wnat{i}", [128, 1024], F32) for i in range(2)]
        unat = [al.alloc(f"unat{i}", [128, 1024], F32) for i in range(2)]
        wqn = [al.alloc(f"wqn{i}", [128, 1024], BF16) for i in range(2)]
        uqn = [al.alloc(f"uqn{i}", [128, 1024], BF16) for i in range(2)]
        wgT = [al.alloc(f"wgT{i}", [128, 1024], BF16) for i in range(2)]
        wuT = [al.alloc(f"wuT{i}", [128, 1024], BF16) for i in range(2)]
        qsc = self.qsc
        th1 = al.alloc("th1", [128, 512], F32, top=True)
        th2 = al.alloc("th2", [128, 512], F32, top=True)
        th3 = [al.alloc(f"th3{i}", [128, 512], F32, top=True) for i in range(2)]

        hz = p.c("vector", lambda e: e.memset(self.hmall[:, :], 0.0), after=list(deps))

        hqg = [None, None]
        hqu = [None, None]
        hwev = [None, None]
        hprod = {}
        hdh = {}
        hge_hist = {}
        last_hpd = None
        pending = None
        for i in range(IC):
            b = i % 2
            hlg = p.d("sync", 1, lambda e, i=i, b=b: [
                e.dma_start(out=wnat[b][:, :], in_=wg_d[128 * i:128 * i + 128, :])],
                after=list(deps) + ([hqg[b]] if hqg[b] else []))
            hlu = p.d("sync", 1, lambda e, i=i, b=b: [
                e.dma_start(out=unat[b][:, :], in_=wu_d[128 * i:128 * i + 128, :])],
                after=list(deps) + ([hqu[b]] if hqu[b] else []))

            def qg(e, i=i, b=b):
                e.reduce_max(out=qsc[:, 0:1], in_=wnat[b][:, :], axis=AX.X,
                             apply_absolute_value=True)
                e.drain()
                e.reciprocal(out=qsc[:, 1:2], in_=qsc[:, 0:1])
                e.drain()
                e.tensor_scalar(out=self.tfA[:, 0:1024], in0=wnat[b][:, :],
                                scalar1=qsc[:, 1:2], scalar2=127.0, op0=ALU.mult, op1=ALU.mult)
                e.tensor_scalar(out=wqn[b][:, :], in0=self.tfA[:, 0:1024], scalar1=MAGIC,
                                scalar2=MAGIC, op0=ALU.add, op1=ALU.subtract)
                return e.tensor_scalar(out=self.csg[:, i:i + 1], in0=qsc[:, 0:1],
                                       scalar1=self.scal[:, 0:1], scalar2=1.0 / 127.0,
                                       op0=ALU.mult, op1=ALU.mult)
            hq1 = p.c("vector", qg, after=[hlg, self.sx_h[layer]] +
                      ([hwev[b]] if hwev[b] else []) +
                      ([hge_hist[i - 2]] if i >= 2 else []))
            hqg[b] = hq1

            def qu(e, i=i, b=b):
                e.reduce_max(out=qsc[:, 0:1], in_=unat[b][:, :], axis=AX.X,
                             apply_absolute_value=True)
                e.drain()
                e.reciprocal(out=qsc[:, 1:2], in_=qsc[:, 0:1])
                e.drain()
                e.tensor_scalar(out=self.tfA[:, 0:1024], in0=unat[b][:, :],
                                scalar1=qsc[:, 1:2], scalar2=127.0, op0=ALU.mult, op1=ALU.mult)
                e.tensor_scalar(out=uqn[b][:, :], in0=self.tfA[:, 0:1024], scalar1=MAGIC,
                                scalar2=MAGIC, op0=ALU.add, op1=ALU.subtract)
                return e.tensor_scalar(out=self.csu[:, i:i + 1], in0=qsc[:, 0:1],
                                       scalar1=self.scal[:, 0:1], scalar2=1.0 / 127.0,
                                       op0=ALU.mult, op1=ALU.mult)
            hq2 = p.c("vector", qu, after=[hlu, hq1])
            hqu[b] = hq2

            def tw(e, b=b):
                last = None
                for k in range(CH):
                    last = e.transpose(psT[:, 128 * k:128 * k + 128],
                                       wqn[b][:, 128 * k:128 * k + 128], self.ident[:, :])
                return last
            htw = p.c("tensor", tw, after=[hq1] + ([hwev[b]] if hwev[b] else []) +
                      ([pending] if pending else []))
            pending = None
            hge = p.c("scalar", lambda e, b=b: e.copy(out=wgT[b][:, :], in_=psT[:, :]),
                      after=[htw])
            hge_hist[i] = hge

            def tu(e, b=b):
                last = None
                for k in range(CH):
                    last = e.transpose(psT[:, 128 * k:128 * k + 128],
                                       uqn[b][:, 128 * k:128 * k + 128], self.ident[:, :])
                return last
            htu = p.c("tensor", tu, after=[hq2, hge])
            hue = p.c("scalar", lambda e, b=b: e.copy(out=wuT[b][:, :], in_=psT[:, :]),
                      after=[htu])
            hwev[b] = hue

            for n in range(NT):
                bg, bu = (0, 1) if n % 2 == 0 else (2, 3)
                prev_pair = hprod.get((i, n - 2)) or (hprod.get((i - 1, n + 2)) if n < 2 else None)

                def mm_gu(e, i=i, b=b, n=n, bg=bg, bu=bu):
                    last = None
                    for k in range(CH):
                        e.matmul(ps[bg][:, :], wgT[b][:, 128 * k:128 * k + 128],
                                 xq[:, T * k + 512 * n:T * k + 512 * n + 512],
                                 start=(k == 0), stop=(k == CH - 1))
                    for k in range(CH):
                        last = e.matmul(ps[bu][:, :], wuT[b][:, 128 * k:128 * k + 128],
                                        xq[:, T * k + 512 * n:T * k + 512 * n + 512],
                                        start=(k == 0), stop=(k == CH - 1))
                    return last
                hmm = p.c("tensor", mm_gu, after=[hge, hue, self.xq_h[layer]] +
                          ([prev_pair] if prev_pair else []))
                hmm_last = hmm
                hsl = p.c("scalar", lambda e, i=i, bg=bg: e.activation(
                    out=th1[:, :], in_=ps[bg][:, :], func=AF.Silu,
                    scale=self.csg[:, i:i + 1], bias=self.zeros1[:, 0:1]),
                    after=[hmm, hz] + ([last_hpd] if last_hpd else []))

                def prod(e, i=i, n=n, bu=bu):
                    e.tensor_scalar(out=th2[:, :], in0=ps[bu][:, :],
                                    scalar1=self.csu[:, i:i + 1], scalar2=None, op0=ALU.mult)
                    e.tensor_tensor(out=th3[n % 2][:, :], in0=th1[:, :], in1=th2[:, :],
                                    op=ALU.mult)
                    return e.reduce_max(out=self.hmall[:, 4 * i + n:4 * i + n + 1],
                                        in_=th3[n % 2][:, :], axis=AX.X,
                                        apply_absolute_value=True)
                hpd = p.c("vector", prod, after=[hsl] +
                          ([hdh[(i, n - 2)]] if (i, n - 2) in hdh else
                           ([hdh[(i - 1, n + 2)]] if (i - 1, n + 2) in hdh else [])))
                last_hpd = hpd
                hprod[(i, n)] = hpd
                hdh[(i, n)] = p.d("scalar", 1, lambda e, i=i, n=n: [
                    e.dma_start(out=self.h_save[128 * i:128 * i + 128, 512 * n:512 * n + 512],
                                in_=th3[n % 2][:, :])], after=[hpd])
            if hook:
                pending = hook(i, hwev[b])

        al.free("wnat0", "wnat1", "unat0", "unat1", "wqn0", "wqn1", "uqn0", "uqn1",
                "wgT0", "wgT1", "wuT0", "wuT1")
        gateup_done = [hprod[(IC - 1, n)] for n in range(NT)]
        war = [hqg[0], hqg[1], hqu[0], hqu[1], hwev[0], hwev[1], hmm_last]
        if mid_free:
            war += list(mid_free() or [])

        wdnat = [al.alloc(f"wdnat{i}", [128, 2816], F32) for i in range(2)]
        wdq = al.alloc("wdq", [128, 2816], BF16)
        wdT = al.alloc("wdT", [128, 22 * 1024], BF16, top=True)
        hwd = []
        hdq = [None, None]
        first_td_extra = [pending] if pending else []
        for c in range(CH):
            b = c % 2
            hld = p.d("sync", 1, lambda e, c=c, b=b: [
                e.dma_start(out=wdnat[b][:, :], in_=wd_d[128 * c:128 * c + 128, :])],
                after=list(deps) + war + ([hdq[b]] if hdq[b] else []))

            def qd(e, c=c, b=b):
                e.reduce_max(out=self.rmd[:, c:c + 1], in_=wdnat[b][:, :], axis=AX.X,
                             apply_absolute_value=True)
                e.drain()
                e.reciprocal(out=qsc[:, 3:4], in_=self.rmd[:, c:c + 1])
                e.drain()
                for half in range(2):
                    sl = slice(1408 * half, 1408 * half + 1408)
                    e.tensor_scalar(out=self.tfA[:, 0:1408], in0=wdnat[b][:, sl],
                                    scalar1=qsc[:, 3:4], scalar2=127.0,
                                    op0=ALU.mult, op1=ALU.mult)
                    e.tensor_scalar(out=wdq[:, sl], in0=self.tfA[:, 0:1408], scalar1=MAGIC,
                                    scalar2=MAGIC, op0=ALU.add, op1=ALU.subtract)
                return e.engine_nop()
            hq = p.c("vector", qd, after=[hld] + ([hwd[-1]] if hwd else []))
            hdq[b] = hq
            hlast = hq
            for g in range(3):
                ks = list(range(8 * g, min(8 * g + 8, IC)))

                def tdg(e, ks=tuple(ks)):
                    last = None
                    for j, k in enumerate(ks):
                        last = e.transpose(psT[:, 128 * j:128 * j + 128],
                                           wdq[:, 128 * k:128 * k + 128], self.ident[:, :])
                    return last
                htd = p.c("tensor", tdg, after=[hlast] + (first_td_extra if (c == 0 and g == 0) else []))

                def evd(e, c=c, ks=tuple(ks)):
                    last = None
                    for j, k in enumerate(ks):
                        last = e.copy(out=wdT[:, 1024 * k + 128 * c:1024 * k + 128 * c + 128],
                                      in_=psT[:, 128 * j:128 * j + 128])
                    return last
                hlast = p.c("scalar", evd, after=[htd])
            hwd.append(hlast)

        hhm = p.c("vector", lambda e: e.reduce_max(out=self.xmaxc[:, 9:10],
                                                   in_=self.hmall[:, 0:88], axis=AX.X,
                                                   apply_absolute_value=True),
                  after=gateup_done)
        hdm = p.d("scalar", 1, lambda e: [e.dma_start(out=self.cc_mx_in[mxslot],
                                                      in_=self.xmaxc[:, 9:10])],
                  after=[hhm, hdh[(IC - 1, NT - 1)]])
        hcm2 = p.cc(lambda e: e.collective_compute(
            "AllReduce", ALU.max, replica_groups=G8,
            ins=[self.cc_mx_in[mxslot].opt()], outs=[self.cc_mx_out[mxslot].opt()]),
            after=[hdm])
        hrb = p.d("gpsimd", 1, lambda e: [e.dma_start(
            out=self.smrow[0:1, 64:192],
            in_=self.cc_mx_out[mxslot].rearrange("p one -> one p"))], after=[hcm2])
        hg1 = p.c("vector", lambda e: e.reduce_max(out=self.smrow[0:1, 8:9],
                                                   in_=self.smrow[0:1, 64:192], axis=AX.X),
                  after=[hrb])
        hg2 = p.c("scalar", lambda e: e.activation(out=self.smrow[0:1, 9:10],
                                                   in_=self.clips[0:1, cd_col:cd_col + 1],
                                                   func=AF.Sigmoid, bias=self.zeros1[0:1, 0:1]),
                  after=[hg1])

        def sh_fin(e):
            e.drain()
            e.tensor_scalar(out=self.smrow[0:1, 10:11], in0=self.smrow[0:1, 9:10],
                            scalar1=self.smrow[0:1, 8:9], scalar2=1.0 / 127.0,
                            op0=ALU.mult, op1=ALU.mult)
            e.drain()
            return e.reciprocal(out=self.smrow[0:1, 11:12], in_=self.smrow[0:1, 10:11])
        hg3 = p.c("vector", sh_fin, after=[hg2])

        def sh_bc(e):
            e.matmul(self.psX[:, 0:1], self.onesrow[0:1, :], self.smrow[0:1, 10:11],
                     start=True, stop=True)
            return e.matmul(self.psX[:, 1:2], self.onesrow[0:1, :], self.smrow[0:1, 11:12],
                            start=True, stop=True)
        hg4 = p.c("tensor", sh_bc, after=[hg3])

        def sh_ev(e):
            e.tensor_copy(out=self.scal[:, 2:3], in_=self.psX[:, 0:1])
            e.tensor_copy(out=self.scal[:, 3:4], in_=self.psX[:, 1:2])
            e.drain()
            e.tensor_scalar(out=self.csd[:, :], in0=self.rmd[:, :],
                            scalar1=self.scal[:, 2:3], scalar2=1.0 / 127.0,
                            op0=ALU.mult, op1=ALU.mult)
            return e.drain()
        hg5 = p.c("vector", sh_ev, after=[hg4, hwd[-1]])

        al.free("wdnat0", "wdnat1", "wdq")
        qhbuf = [al.alloc(f"qh{i}", [128, 22 * 512], BF16, top=True) for i in range(2)]
        xload = [al.alloc(f"xl{i}", [128, 512], F32) for i in range(2)]
        hstg = [al.alloc(f"hstg{i}", [128, 512], F32) for i in range(4)]

        hcomb = []
        hqh = {}
        y_dmas = []
        for n in range(NT):
            qbuf = qhbuf[n % 2]
            hq_handles = []
            for k in range(IC):
                bb = k % 4
                war_h = ([hq_handles[k - 4]] if k >= 4 else
                         ([hqh[(n - 1, IC - 4 + k)]] if n >= 1 else []))
                hl = p.d("sync", 1, lambda e, k=k, n=n, bb=bb: [
                    e.dma_start(out=hstg[bb][:, :],
                                in_=self.h_save[128 * k:128 * k + 128, 512 * n:512 * n + 512])],
                    after=[hdh[(k, n)], hwd[-1]] + war_h +
                          ([hqh[(n - 2, k)]] if (n - 2, k) in hqh else []))

                def qh_c(e, k=k, bb=bb, qbuf=qbuf):
                    e.tensor_scalar(out=th2[:, :], in0=hstg[bb][:, :],
                                    scalar1=self.scal[:, 3:4], scalar2=MAGIC,
                                    op0=ALU.mult, op1=ALU.add)
                    e.tensor_scalar(out=th2[:, :], in0=th2[:, :], scalar1=MAGIC, scalar2=127.0,
                                    op0=ALU.subtract, op1=ALU.min)
                    return e.tensor_scalar(out=qbuf[:, 512 * k:512 * k + 512], in0=th2[:, :],
                                           scalar1=-128.0, scalar2=None, op0=ALU.max)
                hq = p.c("vector", qh_c, after=[hl, hg5])
                hq_handles.append(hq)
                hqh[(n, k)] = hq
            for c in range(CH):
                bk = c % 4
                bx = c % 2
                hlx = p.d("sync", 1, lambda e, c=c, n=n, bx=bx: [
                    e.dma_start(out=xload[bx][:, :],
                                in_=x_src[128 * c:128 * c + 128, 512 * n:512 * n + 512])],
                    after=[hg5] + ([hcomb[-2]] if len(hcomb) >= 2 else []))

                def mm_d(e, ks, c=c, bk=bk, qbuf=qbuf):
                    last = None
                    for k in ks:
                        last = e.matmul(ps[bk][:, :],
                                        wdT[:, 1024 * k + 128 * c:1024 * k + 128 * c + 128],
                                        qbuf[:, 512 * k:512 * k + 512],
                                        start=(k == 0), stop=(k == IC - 1))
                    return last
                import functools as _ft
                war_bank = [hcomb[-4]] if len(hcomb) >= 4 else []
                hd1 = p.c("tensor", _ft.partial(mm_d, ks=range(0, 8)),
                          after=[hq_handles[7]] + war_bank)
                hd2 = p.c("tensor", _ft.partial(mm_d, ks=range(8, 16)),
                          after=[hq_handles[15], hd1])
                hmmd = p.c("tensor", _ft.partial(mm_d, ks=range(16, IC)),
                           after=[hq_handles[IC - 1], hd2])
                hsl2 = p.c("scalar", lambda e, c=c, bk=bk: e.activation(
                    out=th1[:, :], in_=ps[bk][:, :], func=AF.Silu,
                    scale=self.csd[:, c:c + 1], bias=self.zeros1[:, 0:1]),
                    after=[hmmd] + ([hcomb[-1]] if hcomb else []))

                def comb(e, c=c, n=n, bx=bx):
                    e.tensor_tensor(out=th2[:, :], in0=th1[:, :],
                                    in1=s_get(c)[:, 512 * n:512 * n + 512], op=ALU.mult)
                    return e.tensor_tensor(out=th3[c % 2][:, :], in0=th2[:, :],
                                           in1=xload[bx][:, :], op=ALU.add)
                hcb = p.c("vector", comb,
                          after=[hsl2, hlx] + list(s_ready) +
                          ([y_dmas[-2]] if len(y_dmas) >= 2 else []))
                hcomb.append(hcb)
                y_dmas.append(p.d("scalar", 1, lambda e, c=c, n=n: [
                    e.dma_start(out=x1_dst[128 * c:128 * c + 128, 512 * n:512 * n + 512],
                                in_=th3[c % 2][:, :])], after=[hcb]))

        al.free("qh0", "qh1", "xl0", "xl1", "hstg0", "hstg1", "hstg2", "hstg3",
                "wdT", "th1", "th2", "th30", "th31")
        return {"done": hcomb[-1], "y_dmas": y_dmas}

    def emit_attn1(self, hln, deps, wdeps):
        p, al = self.p, self.al
        ps = self.ps
        xn = self.xn
        tfB = self.tfB

        WzTb = al.alloc("WzTb", [128, 8 * 1024], BF16, top=True)
        WhTb = al.alloc("WhTb", [128, 8 * 1024], BF16, top=True)
        hwz = p.d("gpsimd", 1, lambda e: [e.dma_start(
            out=WzTb[:, :].rearrange("p (c n) -> p c n", c=8),
            in_=self.WzT_d.rearrange("(c p) n -> p c n", p=128))], after=list(wdeps))
        hwh = p.d("gpsimd", 1, lambda e: [e.dma_start(
            out=WhTb[:, :].rearrange("p (c n) -> p c n", c=8),
            in_=self.WhT_d.rearrange("(c p) n -> p c n", p=128))], after=list(wdeps))

        zt = al.alloc("zt", [128, T], F32)
        hloc = al.alloc("hloc", [128, 8 * T], BF16)
        Aloc = al.alloc("Aloc", [128, 8 * T], BF16)
        tfC = al.alloc("tfC", [128, T], F32)
        sA = [self.tfA, al.alloc("tfA2", [128, T], F32)]
        tfD = al.alloc("tfD", [128, T], F32)
        sD = [tfD, tfD]
        hz0 = p.c("gpsimd", lambda e: e.memset(zt[:, :], 0.0), after=list(deps))

        hsc = []
        het_prev = None
        for c in range(CH):
            A, D = sA[c % 2], sD[c % 2]

            def mm_z(e, c=c):
                last = None
                for k in range(CH):
                    for n in range(NT):
                        last = e.matmul(ps[n][:, :],
                                        WzTb[:, 1024 * k + 128 * c:1024 * k + 128 * c + 128],
                                        xn[:, T * k + 512 * n:T * k + 512 * n + 512],
                                        start=(k == 0), stop=(k == CH - 1))
                return last
            hmz = p.c("tensor", mm_z, after=[hwz, hln["xn"]] +
                      ([het_prev] if het_prev else []))

            def ev_c(e, c=c, A=A):
                last = None
                for n in range(NT):
                    last = e.activation(out=A[:, 512 * n:512 * n + 512], in_=ps[n][:, :],
                                        func=AF.Sigmoid, scale=-1.0,
                                        bias=self.negbz[:, c:c + 1])
                return last
            hec = p.c("scalar", ev_c, after=[hmz, self.h_bias] +
                      ([hsc[c - 2]] if c >= 2 else []))

            def mm_h(e, c=c):
                last = None
                for k in range(CH):
                    for n in range(NT):
                        last = e.matmul(ps[n][:, :],
                                        WhTb[:, 1024 * k + 128 * c:1024 * k + 128 * c + 128],
                                        xn[:, T * k + 512 * n:T * k + 512 * n + 512],
                                        start=(k == 0), stop=(k == CH - 1))
                return last
            hmh = p.c("tensor", mm_h, after=[hwh, hec])

            def ev_t(e, c=c, A=A, D=D):
                for n in range(NT):
                    sl = slice(512 * n, 512 * n + 512)
                    e.scalar_tensor_tensor(out=tfB[:, sl], in0=ps[n][:, :],
                                           scalar=self.bhm05[:, c:c + 1], in1=A[:, sl],
                                           op0=ALU.add, op1=ALU.add)
                e.tensor_scalar(out=tfB[:, :], in0=tfB[:, :], scalar1=0.0, scalar2=None,
                                op0=ALU.max)
                e.tensor_scalar(out=tfC[:, :], in0=A[:, :], scalar1=-1.0, scalar2=1.0,
                                op0=ALU.mult, op1=ALU.add)
                e.tensor_tensor(out=tfB[:, :], in0=tfB[:, :], in1=tfC[:, :], op=ALU.add)
                return e.tensor_tensor(out=D[:, :], in0=tfC[:, :], in1=tfB[:, :],
                                       op=ALU.mult)
            het = p.c("vector", ev_t, after=[hmh])
            het_prev = het

            def scans(e, c=c, A=A, D=D):
                e.tensor_tensor_scan(out=hloc[:, T * c:T * c + T], data0=A[:, :],
                                     data1=D[:, :], initial=0.0, op0=ALU.mult, op1=ALU.add)
                return e.tensor_tensor_scan(out=Aloc[:, T * c:T * c + T], data0=A[:, :],
                                            data1=zt[:, :], initial=1.0, op0=ALU.mult,
                                            op1=ALU.add)
            hsc.append(p.c("vector", scans, after=[het, hz0]))

            def ssum_c(e, c=c):
                e.tensor_copy(out=self.ssum[:, 2 * c:2 * c + 1],
                              in_=Aloc[:, T * c + T - 1:T * c + T])
                return e.tensor_copy(out=self.ssum[:, 2 * c + 1:2 * c + 2],
                                     in_=hloc[:, T * c + T - 1:T * c + T])
            hsum = p.c("gpsimd", ssum_c, after=[hsc[-1]])

        hln["sx"] = hln["sx_fin"]()
        hds = p.d("scalar", 1, lambda e: [e.dma_start(out=self.cc_s_in, in_=self.ssum[:, :])],
                  after=[hsc[-1], hsum])
        hcs = p.cc(lambda e: e.collective_compute(
            "AllGather", ALU.bypass, replica_groups=G4,
            ins=[self.cc_s_in.opt()], outs=[self.cc_s_out.opt()]), after=[hds])
        hrs = p.d("gpsimd", 4, lambda e: [
            e.dma_start(out=self.sgth[j][:, :], in_=self.cc_s_out[128 * j:128 * j + 128, :])
            for j in range(4)], after=[hcs])

        hfx = []
        hpre_tap = None
        if "h1pre" in TAPS:
            hpo = self.nc.dram_tensor("tap_h1pre", [128 * 8, T], BF16,
                                      kind="ExternalOutput").ap()
            self.tap_outs["h1pre"] = hpo
            hpre_tap = p.d("scalar", 8, lambda e: [
                e.dma_start(out=hpo[128 * c:128 * c + 128, :], in_=hloc[:, T * c:T * c + T])
                for c in range(CH)], after=[hsc[-1]])
        if "A1" in TAPS:
            apo = self.nc.dram_tensor("tap_A1", [128 * 8, T], BF16,
                                      kind="ExternalOutput").ap()
            self.tap_outs["A1"] = apo
            hpre_a = p.d("scalar", 8, lambda e: [
                e.dma_start(out=apo[128 * c:128 * c + 128, :], in_=Aloc[:, T * c:T * c + T])
                for c in range(CH)], after=[hsc[-1]])
        if "cv7" in TAPS:
            cvo = self.nc.dram_tensor("tap_cv7", [256, T], F32, kind="ExternalOutput").ap()
            self.tap_outs["cv7"] = cvo
            hpre_cv = p.d("scalar", 2, lambda e: [
                e.dma_start(out=cvo[0:128, :], in_=sA[1][:, :]),
                e.dma_start(out=cvo[128:256, :], in_=sD[1][:, :])], after=[hsc[-1]])

        def pfx_all(e):
            sg = self.sgth
            e.tensor_tensor(out=self.pfx[:, 0:8], in0=sg[1][:, 0:16:2],
                            in1=sg[0][:, 1:16:2], op=ALU.mult)
            e.drain()
            e.tensor_tensor(out=self.pfx[:, 0:8], in0=self.pfx[:, 0:8],
                            in1=sg[1][:, 1:16:2], op=ALU.add)
            e.drain()
            e.tensor_tensor(out=self.pfx[:, 8:16], in0=sg[2][:, 0:16:2],
                            in1=self.pfx[:, 0:8], op=ALU.mult)
            e.drain()
            e.tensor_tensor(out=self.pfx[:, 8:16], in0=self.pfx[:, 8:16],
                            in1=sg[2][:, 1:16:2], op=ALU.add)
            e.drain()
            e.tensor_scalar(out=self.pfx[:, 16:24], in0=sg[0][:, 1:16:2],
                            scalar1=self.cp[:, 3:4], scalar2=None, op0=ALU.mult)
            e.drain()
            e.scalar_tensor_tensor(out=self.pfx[:, 16:24], in0=self.pfx[:, 0:8],
                                   scalar=self.cp[:, 4:5], in1=self.pfx[:, 16:24],
                                   op0=ALU.mult, op1=ALU.add)
            e.drain()
            e.scalar_tensor_tensor(out=self.pfx[:, 16:24], in0=self.pfx[:, 8:16],
                                   scalar=self.cp[:, 5:6], in1=self.pfx[:, 16:24],
                                   op0=ALU.mult, op1=ALU.add)
            return e.drain()
        hpfx = p.c("vector", pfx_all, after=[hrs, self.h_cst])
        if "sg1" in TAPS:
            sgo = self.nc.dram_tensor("tap_sg1", [128, 88], F32, kind="ExternalOutput").ap()
            self.tap_outs["sg1"] = sgo
            p.d("scalar", 5, lambda e: [
                e.dma_start(out=sgo[:, 0:16], in_=self.sgth[0][:, :]),
                e.dma_start(out=sgo[:, 16:32], in_=self.sgth[1][:, :]),
                e.dma_start(out=sgo[:, 32:48], in_=self.sgth[2][:, :]),
                e.dma_start(out=sgo[:, 48:64], in_=self.sgth[3][:, :]),
                e.dma_start(out=sgo[:, 64:88], in_=self.pfx[:, :]),
            ], after=[hpfx])
        for c in range(CH):
            def fix_c(e, c=c):
                return e.scalar_tensor_tensor(out=hloc[:, T * c:T * c + T],
                                              in0=Aloc[:, T * c:T * c + T],
                                              scalar=self.pfx[:, 16 + c:17 + c],
                                              in1=hloc[:, T * c:T * c + T],
                                              op0=ALU.mult, op1=ALU.add)
            hfx.append(p.c("vector", fix_c, after=[hpfx] +
                           ([hpre_tap] if hpre_tap else [])))
        self.tap("h1", lambda c: hloc[:, T * c:T * c + T], 8, T, BF16, hfx[-1:])

        al.free("tfC", "tfD", "tfA2", "zt", "WzTb", "WhTb")
        Wo1Tb = al.alloc("Wo1Tb", [128, 8 * 1024], BF16)
        hwo = p.d("gpsimd", 1, lambda e: [e.dma_start(
            out=Wo1Tb[:, :].rearrange("p (c n) -> p c n", c=8),
            in_=self.Wo1T_d.rearrange("(c p) n -> p c n", p=128))], after=[hsc[-1]])
        s1 = al.alloc("s1", [128, 8 * T], BF16)
        hs1 = []
        for dc in range(CH):
            def mm_o(e, dc=dc):
                last = None
                for k in range(CH):
                    for n in range(NT):
                        last = e.matmul(ps[n][:, :],
                                        Wo1Tb[:, 1024 * k + 128 * dc:1024 * k + 128 * dc + 128],
                                        hloc[:, T * k + 512 * n:T * k + 512 * n + 512],
                                        start=(k == 0), stop=(k == CH - 1))
                return last
            hmo = p.c("tensor", mm_o, after=[hwo, hfx[-1]] + ([hs1[dc - 1]] if dc >= 1 else []))

            def ev_o(e, dc=dc):
                last = None
                for n in range(NT):
                    last = e.tensor_scalar(out=s1[:, T * dc + 512 * n:T * dc + 512 * n + 512],
                                           in0=ps[n][:, :],
                                           scalar1=self.bo1_fm[:, dc:dc + 1],
                                           scalar2=None, op0=ALU.add)
                return last
            hs1.append(p.c("vector", ev_o, after=[hmo]))
        self.tap("s1", lambda c: s1[:, T * c:T * c + T], 8, T, BF16, hs1[-1:])

        al.free("Wo1Tb", "hloc", "Aloc")
        self.s1 = s1
        return {"s1": hs1[-1], "hloc_reads": [hs1[-1]]}

    def build(self):
        al = self.al
        self.decl()
        self.setup()
        al.free("bo_row")
        self.sx_h = {}
        self.xq_h = {}

        self.xn = al.alloc("xn", [128, 8 * T], BF16)
        self.xt = [al.alloc(f"xt{i}", [128, T], F32) for i in range(2)]
        hln0 = self.emit_ln(0, self.x_in, [self.h_cst, self.h_init, self.h_bob])
        self.tap("xn0", lambda c, t=self.xn: t[:, T * c:T * c + T], 8, T, BF16, [hln0["xn"]])

        self.qdbg = al.alloc("qdbg", [128, 512], F32) if os.environ.get("KB_DBGQ") else None

        def probe(slot, after):
            if self.qdbg is None:
                return
            self.p.c("vector", lambda e, slot=slot: e.tensor_copy(
                out=self.qdbg[:, slot * 128:(slot + 1) * 128], in_=self.mt[:, 0:128]),
                after=after)
        self.mtprobe = probe

        def post_energy():
            self.xq = al.alloc("xq", [128, 8 * T], BF16)
            self.xq_h[0] = self.emit_quant(self.x_in, [hln0["sx"], hln0["stats"]])
        a0 = self.emit_attn0(hln0, post_energy_hook=post_energy)
        self.sx_h[0] = hln0["sx"]
        if getattr(self, "dev_stopped", False):
            return
        if self.qdbg is not None:
            qdo = self.nc.dram_tensor("tap_qdbg", [128, 512], F32, kind="ExternalOutput").ap()
            self.tap_outs["qdbg"] = qdo
            self.p.d("scalar", 1, lambda e: [e.dma_start(out=qdo, in_=self.qdbg[:, :])],
                     after=[self.xq_h[0]])
            al.free("qdbg")
        self.tap("xq0", lambda c, t=self.xq: t[:, T * c:T * c + T], 8, T, BF16, [self.xq_h[0]])
        if DEV_STOP == "quant0":
            self.p.c("sync", lambda e: e.nop(), after=[self.xq_h[0], a0["ag"]])
            self.p.emit(self.nc)
            return

        al.free("vmixF", "Wob", "vmixr0", "vmixr1", "vstg0", "vstg1", "ubuf0", "ubuf1")
        al.free("xt0", "xt1")

        self.ust = [al.alloc(f"ust{i}", [128, 1024], F32, top=True) for i in range(2)]
        self.ustb = [al.alloc(f"ustb{i}", [128, 1024], BF16, top=True) for i in range(2)]
        self.s_fm = al.alloc("s_fm", [128, 8 * T], BF16, top=True)
        rounds = []
        s_evs = []

        def hook(i, hwev_b):
            if os.environ.get("KB_NO_ROUNDS"):
                return None
            if i < 5 or len(rounds) >= 16:
                return None
            r = self.emit_consume_round(len(rounds), a0["ag"],
                                        rounds, [hwev_b, self.xq_h[0]] + a0["last_reads"])
            rounds.append(r)
            s_evs.append(r["ev"])
            return r["ev"]

        def mid_free0():
            al.free("ust0", "ust1", "ustb0", "ustb1", "xn", "xq")
            return ([r["ev"] for r in rounds[-2:]] +
                    ([rounds[-1]["gat"]] if rounds else []))

        f0 = self.emit_ffn(0, lambda c: self.s_fm[:, T * c:T * c + T], s_evs,
                           self.x_in, self.x1_save,
                           [self.xq_h[0], a0["vf"]] + a0["last_reads"],
                           hook=hook, mid_free=mid_free0)
        assert os.environ.get("KB_NO_ROUNDS") or len(rounds) == 16, len(rounds)
        self.tap("s_fm", lambda c, t=self.s_fm: t[:, T * c:T * c + T], 8, T, BF16,
                 s_evs[-1:] + [f0["done"]])

        if DEV_STOP == "sfm0":
            hfin = self.p.d("gpsimd", 8, lambda e: [
                e.dma_start(out=self.y_out[128 * c:128 * c + 128, :].bitcast(BF16)[:, 0:T],
                            in_=self.s_fm[:, T * c:T * c + T]) for c in range(8)],
                after=s_evs[-1:] + [f0["done"]])
            self.p.c("sync", lambda e: e.nop(), after=[hfin])
            self.p.emit(self.nc)
            return

        if DEV_STOP == "hsave0":
            hfin = self.p.d("gpsimd", 1, lambda e: [
                e.dma_start(out=self.y_out, in_=self.h_save[0:1024, :])],
                after=[f0["done"], f0["y_dmas"][-1]])
            self.p.c("sync", lambda e: e.nop(), after=[hfin])
            self.p.emit(self.nc)
            return

        if DEV_STOP == "ffn0":
            hfin = self.p.d("gpsimd", 1, lambda e: [
                e.dma_start(out=self.y_out, in_=self.x1_save)],
                after=[f0["done"], f0["y_dmas"][-1]])
            self.p.c("sync", lambda e: e.nop(), after=[hfin])
            self.p.emit(self.nc)
            return

        al.free("m64", "bo_b", "u16a", "u16b")
        self.xn = al.alloc("xn", [128, 8 * T], BF16)
        self.xt = [al.alloc(f"xt{i}", [128, T], F32) for i in range(2)]
        hln1 = self.emit_ln(1, self.x1_save, [f0["done"], f0["y_dmas"][-1]])
        self.tap("xn1", lambda c, t=self.xn: t[:, T * c:T * c + T], 8, T, BF16, [hln1["xn"]])

        al.free("s_fm", "xt0", "xt1")
        a1 = self.emit_attn1(hln1, [f0["done"], hln1["xn"]],
                             wdeps=[f0["done"], f0["y_dmas"][-1]])
        self.sx_h[1] = hln1["sx"]

        self.xt = [al.alloc(f"xt{i}", [128, T], F32) for i in range(2)]
        self.xq = al.alloc("xq", [128, 8 * T], BF16)
        self.xq_h[1] = self.emit_quant(self.x1_save,
                                       [hln1["sx"], hln1["stats"], a1["s1"]])

        def mid_free1():
            al.free("xn", "xq", "xt0", "xt1")

        f1 = self.emit_ffn(1, lambda c: self.s1[:, T * c:T * c + T], [a1["s1"]],
                           self.x1_save, self.y_out, [self.xq_h[1], a1["s1"]],
                           hook=None, mid_free=mid_free1)

        self.p.c("scalar", lambda e: e.nop(), after=f1["y_dmas"][-1:])
        self.p.c("sync", lambda e: e.nop(), after=f1["y_dmas"][-1:])

        self.p.emit(self.nc)


def _host_inputs(inputs):
    x = np.ascontiguousarray(np.asarray(inputs["L0_attn_x"], dtype=np.float32))
    f32 = lambda a: np.ascontiguousarray(np.asarray(a, dtype=np.float32))
    clips = np.array([[np.asarray(inputs["L0_ffn_clip_x"]).reshape(-1)[0],
                       np.asarray(inputs["L0_ffn_clip_down"]).reshape(-1)[0],
                       np.asarray(inputs["L1_ffn_clip_x"]).reshape(-1)[0],
                       np.asarray(inputs["L1_ffn_clip_down"]).reshape(-1)[0]]],
                     dtype=np.float32)
    pp = np.arange(128)
    kk = np.arange(1024)
    m64 = ((pp[:, None] % 64) == (kk[None, :] % 64)).astype(np.float32)
    ident = np.eye(128, dtype=ml_dtypes.bfloat16)
    aa = np.arange(16)
    u16 = (aa[:, None] == (kk[None, :] // 64)).astype(ml_dtypes.bfloat16)
    common = {
        "Wq": f32(inputs["L0_attn_Wq"]), "Wk": f32(inputs["L0_attn_Wk"]),
        "WvT": np.ascontiguousarray(f32(inputs["L0_attn_Wv"]).T),
        "Wo0": f32(inputs["L0_attn_Wo"]),
        "bo0": f32(inputs["L0_attn_bo"]).reshape(1, 1024),
        "wg0": f32(inputs["L0_ffn_w_gate"]), "wu0": f32(inputs["L0_ffn_w_up"]),
        "wd0": f32(inputs["L0_ffn_w_down"]),
        "wg1": f32(inputs["L1_ffn_w_gate"]), "wu1": f32(inputs["L1_ffn_w_up"]),
        "wd1": f32(inputs["L1_ffn_w_down"]),
        "WzT": np.ascontiguousarray(f32(inputs["L1_attn_Wz"]).T),
        "WhT": np.ascontiguousarray(f32(inputs["L1_attn_Wh"]).T),
        "Wo1T": np.ascontiguousarray(f32(inputs["L1_attn_Wo"]).T),
        "bz_fm": np.ascontiguousarray(f32(inputs["L1_attn_bz"]).reshape(8, 128).T),
        "bh_fm": np.ascontiguousarray(f32(inputs["L1_attn_bh"]).reshape(8, 128).T),
        "bo1_fm": np.ascontiguousarray(f32(inputs["L1_attn_bo"]).reshape(8, 128).T),
        "clips": clips, "m64row": m64, "ident_bf": ident,
        "swap64_bf": np.block([[np.zeros((64, 64)), np.eye(64)],
                               [np.eye(64), np.zeros((64, 64))]]).astype(ml_dtypes.bfloat16),
        "u16a": u16, "u16b": u16.copy(),
    }
    in_maps = []
    for q in range(NCORE):
        n, qq = q // 4, q % 4
        xs = np.ascontiguousarray(x[n, 2048 * qq:2048 * (qq + 1), :].T)
        cp = np.zeros((128, 16), np.float32)
        cp[:, 0] = 1.0 if n == 0 else 0.0
        cp[:, 1] = 1.0 if n == 1 else 0.0
        cp[:, 2 + qq] = 1.0
        uidx = np.zeros((128, 16), np.int32)
        for r in range(4):
            for m in range(4):
                uidx[:, 4 * r + m] = 2048 * r + 128 * (4 * qq + m) + np.arange(128)
        in_maps.append(dict(common, x_fm=xs, coreparams=cp, uidx=uidx))
    return in_maps, x.shape


_CACHE = {}


def kernel(**inputs):
    in_maps, xshape = _host_inputs(inputs)
    if "kb" not in _CACHE:
        _CACHE["kb"] = KB()
    kb = _CACHE["kb"]
    res = run_bass_kernel_spmd(kb.nc, in_maps, core_ids=list(range(NCORE)))
    out = np.empty(xshape, dtype=np.float32)
    for q in range(NCORE):
        n, qq = q // 4, q % 4
        out[n, 2048 * qq:2048 * (qq + 1), :] = res.results[q]["y"].T
    kernel.last_results = res
    return out


# version 08 (final)
# speedup vs baseline: 1.0242x; 1.0242x over previous
import os
import numpy as np
import ml_dtypes
import concourse.bass as bass
import concourse.mybir as mybir
from concourse.bass_utils import run_bass_kernel_spmd

F32 = mybir.dt.float32
BF16 = mybir.dt.bfloat16
I32 = mybir.dt.int32
AF = mybir.ActivationFunctionType
ALU = mybir.AluOpType
AX = mybir.AxisListType

MAGIC = 12582912.0
NCORE = 8
T = 2048
CH = 8
IC = 22
NT = 4
SBUF_BASE = 16384
SBUF_TOP = 229312
G8 = [[0, 1, 2, 3, 4, 5, 6, 7]]
G4 = [[0, 1, 2, 3], [4, 5, 6, 7]]
ENGINES = ("sync", "tensor", "vector", "scalar", "gpsimd")

TAPS = ()
DEV_STOP = os.environ.get("KB_DEV_STOP", "")


class Prog:
    RDMA = 8

    def __init__(self):
        self.steps = {e: [] for e in ENGINES}
        self.pc = {e: 0 for e in ENGINES}
        self.dn = {e: 0 for e in ENGINES}
        self.dtot = {e: [0] * self.RDMA for e in ENGINES}
        self.ccc = 0

    def _waits(self, eng, after):
        waits = {}
        items = []
        for h in after:
            if h is None:
                continue
            if isinstance(h, list):
                items.extend(h)
            else:
                items.append(h)
        for key, val in items:
            if key == ("p", eng):
                continue
            waits[key] = max(waits.get(key, 0), val)
        return waits

    def c(self, eng, fn, after=()):
        self.steps[eng].append(("c", fn, self._waits(eng, after)))
        self.pc[eng] += 1
        return (("p", eng), self.pc[eng])

    def d(self, eng, ndma, fn, after=()):
        assert ndma <= self.RDMA, f"dma step with {ndma} > {self.RDMA} dmas"
        waits = self._waits(eng, after)
        plan = []
        for i in range(ndma):
            slot = self.dn[eng] % self.RDMA
            pre = self.dtot[eng][slot]
            self.dtot[eng][slot] = pre + 16
            self.dn[eng] += 1
            plan.append((slot, pre))
            if pre > 0:
                k = ("d", eng, slot)
                waits[k] = max(waits.get(k, 0), pre)
        self.steps[eng].append(("d", fn, waits, plan))
        handle = {}
        for slot, pre in plan:
            handle[("d", eng, slot)] = pre + 16
        return [(k, v) for k, v in handle.items()]

    def cc(self, fn, after=()):
        self.steps["gpsimd"].append(("cc", fn, self._waits("gpsimd", after)))
        self.ccc += 1
        return (("cc",), self.ccc)

    def emit(self, nc):
        sems = {}
        for e in ENGINES:
            sems[("p", e)] = nc.semaphore(f"p_{e}").__enter__()
            for j in range(self.RDMA):
                sems[("d", e, j)] = nc.semaphore(f"d_{e}{j}").__enter__()
        sems[("cc",)] = nc.semaphore("ccsem").__enter__()
        block = nc.Block().__enter__()

        def run(engobj, ename):
            for step in self.steps[ename]:
                kind, fn, waits = step[0], step[1], step[2]
                for key in sorted(waits, key=str):
                    engobj.wait_ge(sems[key], waits[key])
                if kind == "c":
                    last = fn(engobj)
                    assert last is not None, f"compute step on {ename} returned None"
                    last.then_inc(sems[("p", ename)], 1)
                elif kind == "d":
                    plan = step[3]
                    instrs = fn(engobj)
                    assert isinstance(instrs, (list, tuple)) and len(instrs) == len(plan)
                    for ins, (slot, pre) in zip(instrs, plan):
                        ins.then_inc(sems[("d", ename, slot)], 16)
                else:
                    fn(engobj).then_inc(sems[("cc",)], 1)

        block.sync(lambda e: run(e, "sync"))
        block.tensor(lambda e: run(e, "tensor"))
        block.vector(lambda e: run(e, "vector"))
        block.scalar(lambda e: run(e, "scalar"))
        block.gpsimd(lambda e: run(e, "gpsimd"))
        block.__exit__(None, None, None)


class Alloc:
    def __init__(self, nc):
        self.nc = nc
        self.free_list = [(SBUF_BASE, SBUF_TOP)]
        self.live = {}
        self.counter = 0

    def alloc(self, name, shape, dtype, top=False):
        esz = 2 if dtype == BF16 else 4
        nb = (int(np.prod(shape[1:])) * esz + 31) // 32 * 32
        order = reversed(list(enumerate(self.free_list))) if top else enumerate(self.free_list)
        for i, (lo, hi) in order:
            if hi - lo >= nb:
                self.counter += 1
                start = hi - nb if top else lo
                t = self.nc.alloc_sbuf_tensor_at(f"{name}_{self.counter}", list(shape),
                                                 dtype, offset=start)
                self.live[name] = (start, start + nb)
                if top:
                    if start == lo:
                        self.free_list.pop(i)
                    else:
                        self.free_list[i] = (lo, start)
                else:
                    if lo + nb == hi:
                        self.free_list.pop(i)
                    else:
                        self.free_list[i] = (lo + nb, hi)
                return t
        used = sum(v[1] - v[0] for v in self.live.values())
        raise MemoryError(f"SBUF alloc {name} {nb}B failed (used {used}B live="
                          f"{sorted(self.live.items(), key=lambda kv: kv[1][0])})")

    def free(self, *names):
        for name in names:
            self.free_list.append(self.live.pop(name))
        self.free_list.sort()
        merged = []
        for seg in self.free_list:
            if merged and merged[-1][1] >= seg[0]:
                merged[-1] = (merged[-1][0], max(merged[-1][1], seg[1]))
            else:
                merged.append(seg)
        self.free_list = merged


class KB:
    def __init__(self):
        self.nc = bass.Bass("TRN2", target_bir_lowering=False, detect_race_conditions=False)
        self.p = Prog()
        self.al = Alloc(self.nc)
        self.tap_outs = {}
        self.build()

    def decl(self):
        nc, al = self.nc, self.al
        D = lambda n, s, dt=F32: nc.dram_tensor(n, list(s), dt, kind="ExternalInput").ap()
        self.x_in = D("x_fm", [1024, T])
        self.Wq_d = D("Wq", [1024, 1024]); self.Wk_d = D("Wk", [1024, 1024])
        self.WvT_d = D("WvT", [1024, 1024]); self.Wo0_d = D("Wo0", [1024, 1024])
        self.bo0_d = D("bo0", [1, 1024])
        self.wg_d = [D("wg0", [2816, 1024]), D("wg1", [2816, 1024])]
        self.wu_d = [D("wu0", [2816, 1024]), D("wu1", [2816, 1024])]
        self.wd_d = [D("wd0", [1024, 2816]), D("wd1", [1024, 2816])]
        self.WzT_d = D("WzT", [1024, 1024]); self.WhT_d = D("WhT", [1024, 1024])
        self.Wo1T_d = D("Wo1T", [1024, 1024])
        self.bz_d = D("bz_fm", [128, 8]); self.bh_d = D("bh_fm", [128, 8])
        self.bo1_d = D("bo1_fm", [128, 8])
        self.clips_d = D("clips", [1, 4])
        self.m64_d = D("m64row", [128, 1024])
        self.ident_d = D("ident_bf", [128, 128], BF16)
        self.swap64_d = D("swap64_bf", [128, 128], BF16)
        self.u16a_d = D("u16a", [16, 1024], BF16)
        self.u16b_d = D("u16b", [16, 1024], BF16)
        self.cp_d = D("coreparams", [128, 16])
        self.uidx_d = D("uidx", [128, 16], I32)

        self.y_out = nc.dram_tensor("y", [1024, T], F32, kind="ExternalOutput").ap()

        self.x1_save = nc.dram_tensor("x1_save", [1024, T], F32).ap()
        self.h_save = nc.dram_tensor("h_save", [2816, T], F32).ap()
        self.cc_mx_in = [nc.dram_tensor(f"cc_mx_in{i}", [128, 1], F32).ap() for i in range(4)]
        self.cc_mx_out = [nc.dram_tensor(f"cc_mx_out{i}", [128, 1], F32).ap() for i in range(4)]
        self.cc_e_in = nc.dram_tensor("cc_e_in", [64, 512], F32).ap()
        self.cc_e_out = nc.dram_tensor("cc_e_out", [64, 512], F32).ap()
        self.cc_u_in = nc.dram_tensor("cc_u_in", [2048, 1024], F32).ap()
        self.cc_u_out = nc.dram_tensor("cc_u_out", [8192, 1024], F32).ap()
        self.cc_s_in = nc.dram_tensor("cc_s_in", [128, 16], F32).ap()
        self.att_d = nc.dram_tensor("att_d", [16, 16], F32).ap()
        self.cc_s_out = nc.dram_tensor("cc_s_out", [512, 16], F32).ap()

        self.ps = [nc.alloc_psum_tensor(f"ps{i}", [128, 512], F32) for i in range(8)]
        self.psT = self.ps[6][:, :].bitcast(BF16)
        self.psX = self.ps[7]

        A = al.alloc
        self.ident = A("ident", [128, 128], BF16)
        self.swap64 = A("swap64", [128, 128], BF16)
        self.m64 = A("m64", [128, 1024], F32)
        self.u16a = A("u16a", [16, 1024], BF16)
        self.u16b = A("u16b", [16, 1024], BF16)
        self.onesbf = A("onesbf", [128, 128], BF16)
        self.ones64 = A("ones64", [64, 1], F32)
        self.onesrow = A("onesrow", [1, 128], F32)
        self.zeros1 = A("zeros1", [128, 1], F32)
        self.eps_t = A("eps_t", [128, 1], F32)
        self.cp = A("cp", [128, 16], F32)
        self.uidx = A("uidx", [128, 16], I32)
        self.clips = A("clips", [1, 4], F32)
        self.bo_row = A("bo_row", [1, 1024], F32)
        self.bo_b = A("bo_b", [128, 1024], F32)
        self.bz_fm = A("bz_fm", [128, 8], F32)
        self.bh_fm = A("bh_fm", [128, 8], F32)
        self.bo1_fm = A("bo1_fm", [128, 8], F32)
        self.negbz = A("negbz", [128, 8], F32)
        self.bhm05 = A("bhm05", [128, 8], F32)
        self.scal = A("scal", [128, 8], F32)
        self.smrow = A("smrow", [1, 192], F32)
        self.xmaxc = A("xmaxc", [128, 12], F32)
        self.hmall = A("hmall", [128, 96], F32)
        self.csg = A("csg", [128, 22], F32)
        self.csu = A("csu", [128, 22], F32)
        self.rmd = A("rmd", [128, 8], F32)
        self.csd = A("csd", [128, 8], F32)
        self.ssum = A("ssum", [128, 16], F32)
        self.sgth = [A(f"sgth{j}", [128, 16], F32) for j in range(4)]
        self.pfx = A("pfx", [128, 24], F32)
        self.attf = A("attf", [1, 256], F32)
        self.attT = A("attT", [16, 16], BF16)
        self.Ep = A("Ep", [128, 128], F32)
        self.Epk = A("Epk", [64, 768], F32)
        self.En = A("En", [64, 256], F32)
        self.erdbg = A("erdbg", [1, 256], F32)
        self.qsc = A("qsc", [128, 4], F32)
        self.mt = A("mt", [128, T], F32)
        self.rstdt = A("rstdt", [128, T], F32)
        self.tfA = A("tfA", [128, T], F32)
        self.tfB = A("tfB", [128, T], F32)

    def tap(self, name, src_fn, nrow, ncol, dt, after):
        if name not in TAPS:
            return
        out = self.nc.dram_tensor("tap_" + name, [128 * nrow, ncol], dt,
                                  kind="ExternalOutput").ap()
        self.tap_outs[name] = out
        self.p.d("scalar", nrow,
                 lambda e: [e.dma_start(out=out[128 * c:128 * c + 128, :], in_=src_fn(c))
                            for c in range(nrow)], after=after)

    def setup(self):
        p = self.p
        h_cst1 = p.d("sync", 6, lambda e: [
            e.dma_start(out=self.m64[:, :], in_=self.m64_d),
            e.dma_start(out=self.ident[:, :], in_=self.ident_d),
            e.dma_start(out=self.swap64[:, :], in_=self.swap64_d),
            e.dma_start(out=self.u16a[:, :], in_=self.u16a_d),
            e.dma_start(out=self.u16b[:, :], in_=self.u16b_d),
            e.dma_start(out=self.cp[:, :], in_=self.cp_d),
        ])
        h_cst2 = p.d("sync", 6, lambda e: [
            e.dma_start(out=self.uidx[:, :], in_=self.uidx_d),
            e.dma_start(out=self.clips[:, :], in_=self.clips_d),
            e.dma_start(out=self.bo_row[:, :], in_=self.bo0_d),
            e.dma_start(out=self.bz_fm[:, :], in_=self.bz_d),
            e.dma_start(out=self.bh_fm[:, :], in_=self.bh_d),
            e.dma_start(out=self.bo1_fm[:, :], in_=self.bo1_d),
        ])
        self.h_cst = h_cst1 + h_cst2

        def init_c(e):
            e.memset(self.onesbf[:, :], 1.0)
            e.memset(self.ones64[:, :], 1.0)
            e.memset(self.onesrow[:, :], 1.0)
            e.memset(self.zeros1[:, :], 0.0)
            e.memset(self.eps_t[:, :], 1e-5)
            return e.memset(self.hmall[:, :], 0.0)
        self.h_init = p.c("vector", init_c)

        def bias_prep(e):
            e.tensor_scalar(out=self.negbz[:, :], in0=self.bz_fm[:, :], scalar1=-1.0,
                            scalar2=None, op0=ALU.mult)
            return e.tensor_scalar(out=self.bhm05[:, :], in0=self.bh_fm[:, :], scalar1=-0.5,
                                   scalar2=None, op0=ALU.add)
        self.h_bias = p.c("vector", bias_prep, after=[self.h_cst])

        def bob_mm(e):
            e.matmul(self.ps[4][:, :], self.onesrow[0:1, :], self.bo_row[0:1, 0:512],
                     start=True, stop=True)
            return e.matmul(self.ps[5][:, :], self.onesrow[0:1, :], self.bo_row[0:1, 512:1024],
                            start=True, stop=True)
        hm = p.c("tensor", bob_mm, after=[self.h_cst, self.h_init])

        def bob_ev(e):
            e.tensor_copy(out=self.bo_b[:, 0:512], in_=self.ps[4][:, :])
            return e.tensor_copy(out=self.bo_b[:, 512:1024], in_=self.ps[5][:, :])
        self.h_bob = p.c("vector", bob_ev, after=[hm])

    def emit_ln(self, layer, x_src, deps):
        p, al = self.p, self.al
        xn, tfA, tfB = self.xn, self.tfA, self.tfB
        xt = self.xt
        ps = self.ps
        mxslot = 0 if layer == 0 else 2
        cx_col = 0 if layer == 0 else 2
        tbf = [al.alloc(f"tbf{i}", [128, T], BF16) for i in range(2)]
        tsq = al.alloc("tsq", [128, T], F32)

        hld = [None, None]
        hmm = []
        hsplit = []
        for c in range(CH):
            b = c % 2
            hld[b] = p.d("sync", 1,
                         lambda e, c=c, b=b: [e.dma_start(out=xt[b][:, :],
                                                          in_=x_src[128 * c:128 * c + 128, :])],
                         after=list(deps) + ([hmm[c - 2]] if c >= 2 else []))
            h1 = p.c("vector", lambda e, b=b: e.tensor_copy(out=tbf[0][:, :], in_=xt[b][:, :]),
                     after=[hld[b], self.h_init] + ([hmm[c - 1]] if c >= 1 else []))
            h2 = p.c("vector", lambda e, b=b: e.tensor_tensor(out=tbf[1][:, :], in0=xt[b][:, :],
                                                              in1=tbf[0][:, :], op=ALU.subtract),
                     after=[h1])
            hsq = p.c("scalar", lambda e, b=b: e.activation(out=tsq[:, :], in_=xt[b][:, :],
                                                            func=AF.Square,
                                                            bias=self.zeros1[:, 0:1]),
                      after=[hld[b], self.h_init] + ([hsplit[c - 1]] if c >= 1 else []))

            def mm_x(e, c=c):
                last = None
                for part in range(2):
                    for n in range(NT):
                        last = e.matmul(ps[n][:, :], self.onesbf[:, :],
                                        tbf[part][:, 512 * n:512 * n + 512],
                                        start=(c == 0 and part == 0),
                                        stop=(c == CH - 1 and part == 1))
                return last
            hx = p.c("tensor", mm_x, after=[h2])
            h4 = p.c("vector", lambda e: e.tensor_copy(out=tbf[0][:, :], in_=tsq[:, :]),
                     after=[hsq, hx])
            h5 = p.c("vector", lambda e: e.tensor_tensor(out=tbf[1][:, :], in0=tsq[:, :],
                                                         in1=tbf[0][:, :], op=ALU.subtract),
                     after=[h4])
            hsplit.append(h5)

            def mm_sq(e, c=c):
                last = None
                for part in range(2):
                    for n in range(NT):
                        last = e.matmul(ps[4 + n][:, :], self.onesbf[:, :],
                                        tbf[part][:, 512 * n:512 * n + 512],
                                        start=(c == 0 and part == 0),
                                        stop=(c == CH - 1 and part == 1))
                return last
            hmm.append(p.c("tensor", mm_sq, after=[h5]))

        def fin_m(e):
            last = None
            for n in range(NT):
                last = e.tensor_scalar(out=self.mt[:, 512 * n:512 * n + 512], in0=ps[n][:, :],
                                       scalar1=1.0 / 1024.0, scalar2=None, op0=ALU.mult)
            return last
        hm = p.c("vector", fin_m, after=[hmm[-1]])

        def fin_v(e):
            last = None
            for n in range(NT):
                sl = slice(512 * n, 512 * n + 512)
                e.tensor_tensor(out=tfA[:, sl], in0=self.mt[:, sl], in1=self.mt[:, sl],
                                op=ALU.mult)
                last = e.scalar_tensor_tensor(out=tfB[:, sl], in0=ps[4 + n][:, :],
                                              scalar=1.0 / 1024.0, in1=tfA[:, sl],
                                              op0=ALU.mult, op1=ALU.subtract)
            return last
        hv = p.c("vector", fin_v, after=[hm])
        hsd = p.c("scalar", lambda e: e.activation(out=tsq[:, :], in_=tfB[:, :], func=AF.Sqrt,
                                                   bias=self.eps_t[:, 0:1]), after=[hv])
        hrs = p.c("vector", lambda e: e.reciprocal(out=self.rstdt[:, :], in_=tsq[:, :]),
                  after=[hsd])

        hxn = []
        for c in range(CH):
            b = c % 2
            h0 = p.d("sync", 1,
                     lambda e, c=c, b=b: [e.dma_start(out=xt[b][:, :],
                                                      in_=x_src[128 * c:128 * c + 128, :])],
                     after=[hmm[-1]] + ([hxn[c - 2]] if c >= 2 else []))

            def xn_c(e, c=c, b=b):
                e.tensor_tensor(out=tfA[:, :], in0=xt[b][:, :], in1=self.mt[:, :],
                                op=ALU.subtract)
                e.tensor_tensor(out=tfB[:, :], in0=tfA[:, :], in1=self.rstdt[:, :], op=ALU.mult)
                e.tensor_copy(out=xn[:, T * c:T * c + T], in_=tfB[:, :])
                return e.reduce_max(out=self.xmaxc[:, c:c + 1], in_=tfB[:, :], axis=AX.X,
                                    apply_absolute_value=True)
            hxn.append(p.c("vector", xn_c, after=[h0, hrs]))

        def cm(e):
            e.drain()
            return e.reduce_max(out=self.xmaxc[:, 8:9], in_=self.xmaxc[:, 0:8], axis=AX.X,
                                apply_absolute_value=True)
        hcm = p.c("vector", cm, after=hxn)
        hdm = p.d("scalar", 1, lambda e: [e.dma_start(out=self.cc_mx_in[mxslot],
                                                      in_=self.xmaxc[:, 8:9])], after=[hcm])
        hccm = p.cc(lambda e: e.collective_compute(
            "AllReduce", ALU.max, replica_groups=G8,
            ins=[self.cc_mx_in[mxslot].opt()], outs=[self.cc_mx_out[mxslot].opt()]),
            after=[hdm])
        def fin_sx():
            hrb = p.d("gpsimd", 1, lambda e: [e.dma_start(
                out=self.smrow[0:1, 64:192],
                in_=self.cc_mx_out[mxslot].rearrange("p one -> one p"))], after=[hccm])
            hg1 = p.c("vector", lambda e: e.reduce_max(out=self.smrow[0:1, 0:1],
                                                       in_=self.smrow[0:1, 64:192], axis=AX.X),
                      after=[hrb])
            hg2 = p.c("scalar", lambda e: e.activation(out=self.smrow[0:1, 4:5],
                                                       in_=self.clips[0:1, cx_col:cx_col + 1],
                                                       func=AF.Sigmoid,
                                                       bias=self.zeros1[0:1, 0:1]),
                      after=[self.h_cst, hg1])

            def sx_fin(e):
                e.drain()
                e.tensor_scalar(out=self.smrow[0:1, 5:6], in0=self.smrow[0:1, 4:5],
                                scalar1=self.smrow[0:1, 0:1], scalar2=1.0 / 127.0,
                                op0=ALU.mult, op1=ALU.mult)
                e.drain()
                return e.reciprocal(out=self.smrow[0:1, 6:7], in_=self.smrow[0:1, 5:6])
            hg3 = p.c("vector", sx_fin, after=[hg2])

            def sx_bc(e):
                e.matmul(self.psX[:, 0:1], self.onesrow[0:1, :], self.smrow[0:1, 5:6],
                         start=True, stop=True)
                return e.matmul(self.psX[:, 1:2], self.onesrow[0:1, :], self.smrow[0:1, 6:7],
                                start=True, stop=True)
            hg4 = p.c("tensor", sx_bc, after=[hg3])

            def sx_ev(e):
                e.tensor_copy(out=self.scal[:, 0:1], in_=self.psX[:, 0:1])
                e.tensor_copy(out=self.scal[:, 1:2], in_=self.psX[:, 1:2])
                return e.drain()
            return p.c("vector", sx_ev, after=[hg4])

        al.free("tbf0", "tbf1", "tsq")
        return {"xn": hxn[-1], "stats": hrs, "sx": None, "sx_fin": fin_sx,
                "mm_last": hmm[-1]}

    def emit_quant(self, x_src, deps):
        p = self.p
        xq, xt, tfA, tfB = self.xq, self.xt, self.tfA, self.tfB
        hs = []
        for c in range(CH):
            b = c % 2
            hld = p.d("sync", 1,
                      lambda e, c=c, b=b: [e.dma_start(out=xt[b][:, :],
                                                       in_=x_src[128 * c:128 * c + 128, :])],
                      after=list(deps) + ([hs[c - 2]] if c >= 2 else []))

            def q_c(e, c=c, b=b):
                e.tensor_tensor(out=tfA[:, :], in0=xt[b][:, :], in1=self.mt[:, :],
                                op=ALU.subtract)
                e.tensor_tensor(out=tfB[:, :], in0=tfA[:, :], in1=self.rstdt[:, :], op=ALU.mult)
                e.tensor_scalar(out=tfA[:, :], in0=tfB[:, :], scalar1=self.scal[:, 1:2],
                                scalar2=MAGIC, op0=ALU.mult, op1=ALU.add)
                e.tensor_scalar(out=tfB[:, :], in0=tfA[:, :], scalar1=MAGIC, scalar2=127.0,
                                op0=ALU.subtract, op1=ALU.min)
                return e.tensor_scalar(out=xq[:, T * c:T * c + T], in0=tfB[:, :],
                                       scalar1=-128.0, scalar2=None, op0=ALU.max)
            hs.append(p.c("vector", q_c, after=[hld] + list(deps)))
        return hs[-1]

    def emit_attn0(self, hln, post_energy_hook=None):
        p, al = self.p, self.al
        ps, psT = self.ps, self.psT
        xn = self.xn

        Wqb = al.alloc("Wqb", [128, 8 * 1024], BF16)
        Wkb = al.alloc("Wkb", [128, 8 * 1024], BF16)
        h_wq = p.d("gpsimd", 1, lambda e: [e.dma_start(
            out=Wqb[:, :].rearrange("p (c n) -> p c n", c=8),
            in_=self.Wq_d.rearrange("(c p) n -> p c n", p=128))],
            after=[hln["stats"], hln["mm_last"]])
        h_wk = p.d("gpsimd", 1, lambda e: [e.dma_start(
            out=Wkb[:, :].rearrange("p (c n) -> p c n", c=8),
            in_=self.Wk_d.rearrange("(c p) n -> p c n", p=128))],
            after=[hln["stats"], hln["mm_last"]])
        qT = al.alloc("qT", [128, 16 * 1024], BF16)
        kT = al.alloc("kT", [128, 16 * 1024], BF16)

        def proj_tokmajor(dst, Wb, h_w, hprev_ev=None):
            ev = []
            for tt in range(16):
                b0, b1 = (0, 1) if tt % 2 == 0 else (2, 3)

                def mm(e, tt=tt, b0=b0, b1=b1, Wb=Wb):
                    last = None
                    for k in range(CH):
                        e.matmul(ps[b0][:, :],
                                 xn[:, T * k + 128 * tt:T * k + 128 * tt + 128],
                                 Wb[:, 1024 * k:1024 * k + 512],
                                 start=(k == 0), stop=(k == CH - 1))
                        last = e.matmul(ps[b1][:, :],
                                        xn[:, T * k + 128 * tt:T * k + 128 * tt + 128],
                                        Wb[:, 1024 * k + 512:1024 * k + 1024],
                                        start=(k == 0), stop=(k == CH - 1))
                    return last
                bank_wait = []
                if tt >= 2:
                    bank_wait = [ev[tt - 2]]
                elif hprev_ev is not None:
                    bank_wait = [hprev_ev[tt - 2]]
                hmmx = p.c("tensor", mm, after=[h_w, hln["xn"]] + bank_wait)

                def evf(e, tt=tt, b0=b0, b1=b1, dst=dst):
                    e.tensor_copy(out=dst[:, 1024 * tt:1024 * tt + 512], in_=ps[b0][:, :])
                    return e.tensor_copy(out=dst[:, 1024 * tt + 512:1024 * tt + 1024],
                                         in_=ps[b1][:, :])
                ev.append(p.c("vector", evf, after=[hmmx]))
            return ev

        if getattr(self, "mtprobe", None):
            pass
        hq_ev = proj_tokmajor(qT, Wqb, h_wq)
        if getattr(self, "mtprobe", None):
            pass
        hk_ev = proj_tokmajor(kT, Wkb, h_wk, hprev_ev=hq_ev)
        if getattr(self, "mtprobe", None):
            pass

        hg_ev = []
        for qc in range(CH):
            b0, b1 = (0, 1) if qc % 2 == 0 else (2, 3)

            def mm_g(e, qc=qc, b0=b0, b1=b1):
                last = None
                for tt in range(16):
                    e.matmul(ps[b0][:, :],
                             qT[:, 1024 * tt + 128 * qc:1024 * tt + 128 * qc + 128],
                             kT[:, 1024 * tt:1024 * tt + 512],
                             start=(tt == 0), stop=(tt == 15))
                    last = e.matmul(ps[b1][:, :],
                                    qT[:, 1024 * tt + 128 * qc:1024 * tt + 128 * qc + 128],
                                    kT[:, 1024 * tt + 512:1024 * tt + 1024],
                                    start=(tt == 0), stop=(tt == 15))
                return last
            bank_wait = [hg_ev[qc - 2]] if qc >= 2 else [hk_ev[-2], hk_ev[-1]]
            hmg = p.c("tensor", mm_g, after=bank_wait)

            def ev_g(e, qc=qc, b0=b0, b1=b1):
                e.tensor_tensor(out=self.tfA[:, 0:512], in0=ps[b0][:, :],
                                in1=self.m64[:, 0:512], op=ALU.mult)
                e.tensor_tensor(out=self.tfA[:, 512:1024], in0=ps[b1][:, :],
                                in1=self.m64[:, 512:1024], op=ALU.mult)
                return e.reduce_sum(out=self.Ep[:, 16 * qc:16 * qc + 16],
                                    in_=self.tfA[:, 0:1024].rearrange("p (a b) -> p a b", b=64),
                                    axis=AX.X)
            hg_ev.append(p.c("vector", ev_g, after=[hmg, self.h_cst]))

        def pack_e(e):
            for qc in range(CH):
                for par in range(2):
                    a = 2 * qc + par
                    e.tensor_copy(out=self.Epk[:, 16 * a:16 * a + 16],
                                  in_=self.Ep[64 * par:64 * par + 64, 16 * qc:16 * qc + 16])
            e.tensor_scalar(out=self.Epk[:, 256:512], in0=self.Epk[:, 0:256],
                            scalar1=self.cp[0:64, 0:1], scalar2=None, op0=ALU.mult)
            return e.tensor_scalar(out=self.Epk[:, 512:768], in0=self.Epk[:, 0:256],
                                   scalar1=self.cp[0:64, 1:2], scalar2=None, op0=ALU.mult)
        hpk = p.c("vector", pack_e, after=hg_ev)
        hde = p.d("scalar", 1, lambda e: [e.dma_start(out=self.cc_e_in,
                                                      in_=self.Epk[0:64, 256:768])],
                  after=[hpk])
        hce = p.cc(lambda e: e.collective_compute(
            "AllReduce", ALU.add, replica_groups=G8,
            ins=[self.cc_e_in.opt()], outs=[self.cc_e_out.opt()]), after=[hde])
        hre = p.d("gpsimd", 1, lambda e: [e.dma_start(out=self.Epk[0:64, 256:768],
                                                      in_=self.cc_e_out)], after=[hce])
        hln["sx"] = hln["sx_fin"]()

        def en_fin(e):
            e.tensor_scalar(out=self.En[:, :], in0=self.Epk[0:64, 256:512],
                            scalar1=self.cp[0:64, 0:1], scalar2=None, op0=ALU.mult)
            e.drain()
            return e.scalar_tensor_tensor(out=self.En[:, :], in0=self.Epk[0:64, 512:768],
                                          scalar=self.cp[0:64, 1:2], in1=self.En[:, :],
                                          op0=ALU.mult, op1=ALU.add)
        hen = p.c("vector", en_fin, after=[hre])
        if getattr(self, "mtprobe", None):
            pass

        hrow = p.c("tensor", lambda e: e.matmul(self.psX[0:1, 0:256], self.ones64[0:64, 0:1],
                                                self.En[0:64, :], start=True, stop=True),
                   after=[hen, hg_ev[-1], hln["sx"]])
        def rowc(e):
            e.tensor_copy(out=self.erdbg[0:1, :], in_=self.psX[0:1, 0:256])
            return e.drain()
        hrowc = p.c("vector", rowc, after=[hrow])

        def sm_mx(e):
            for a in range(16):
                e.reduce_max(out=self.smrow[0:1, 96 + a:97 + a],
                             in_=self.erdbg[0:1, 16 * a:16 * a + 16], axis=AX.X)
            e.drain()
            last = None
            for a in range(16):
                last = e.tensor_scalar(out=self.attf[0:1, 16 * a:16 * a + 16],
                                       in0=self.erdbg[0:1, 16 * a:16 * a + 16],
                                       scalar1=self.smrow[0:1, 96 + a:97 + a],
                                       scalar2=None, op0=ALU.subtract)
            return last
        hmx = p.c("vector", sm_mx, after=[hrowc])
        hex_ = p.c("scalar", lambda e: e.activation(
            out=self.attf[0:1, :], in_=self.attf[0:1, :], func=AF.Exp, scale=0.125,
            bias=self.zeros1[0:1, 0:1]), after=[hmx])

        def sm_nrm(e):
            e.reduce_sum(out=self.smrow[0:1, 128:144],
                         in_=self.attf[0:1, :].rearrange("p (a b) -> p a b", b=16), axis=AX.X)
            e.drain()
            e.reciprocal(out=self.smrow[0:1, 144:160], in_=self.smrow[0:1, 128:144])
            e.drain()
            last = None
            for a in range(16):
                last = e.tensor_scalar(out=self.attf[0:1, 16 * a:16 * a + 16],
                                       in0=self.attf[0:1, 16 * a:16 * a + 16],
                                       scalar1=self.smrow[0:1, 144 + a:145 + a],
                                       scalar2=None, op0=ALU.mult)
            return last
        hsm = p.c("vector", sm_nrm, after=[hex_])
        if DEV_STOP == "att0":
            hfin = p.d("gpsimd", 9, lambda e: [
                e.dma_start(out=self.y_out[0:64, 0:768], in_=self.Epk[0:64, :]),
                e.dma_start(out=self.y_out[64:128, 0:256], in_=self.En[0:64, :]),
                e.dma_start(out=self.y_out[128:129, 0:256], in_=self.attf[0:1, :]),
                e.dma_start(out=self.y_out[129:130, 0:256], in_=self.erdbg[0:1, :]),
                e.dma_start(out=self.y_out[130:131, 0:192], in_=self.smrow[0:1, :]),
                e.dma_start(out=self.y_out[256:384, 0:1024], in_=qT[:, 0:1024]),
                e.dma_start(out=self.y_out[384:512, 0:1024], in_=kT[:, 0:1024]),
                e.dma_start(out=self.y_out[512:640, 0:1024], in_=Wqb[:, 0:1024]),
                e.dma_start(out=self.y_out[640:768, 0:128], in_=self.Ep[:, :]),
            ], after=[hsm, hen])
            p.c("sync", lambda e: e.nop(), after=[hfin])
            self.p.emit(self.nc)
            self.dev_stopped = True
            return {}

        hat0 = p.d("gpsimd", 1, lambda e: [e.dma_start(
            out=self.att_d.rearrange("a b -> (a b)").rearrange("(one f) -> one f", one=1),
            in_=self.attf[0:1, :])], after=[hsm])
        hat = p.d("gpsimd", 1, lambda e: [e.dma_start(
            out=self.attT[:, :], in_=self.att_d)], after=[hat0])

        al.free("qT", "kT", "Wqb", "Wkb")
        WvTb = al.alloc("WvTb", [128, 8 * 1024], BF16)
        atxp = al.alloc("atxp", [16, 1024], BF16)
        Aexp = al.alloc("Aexp", [128, 8 * 1024], BF16)
        h_wv = p.d("gpsimd", 1, lambda e: [e.dma_start(
            out=WvTb[:, :].rearrange("p (c n) -> p c n", c=8),
            in_=self.WvT_d.rearrange("(c p) n -> p c n", p=128))], after=[hg_ev[-1]])
        if post_energy_hook:
            post_energy_hook()

        def mm_ax(e):
            e.matmul(ps[4][0:16, :], self.attT[:, :], self.u16a[:, 0:512], start=True, stop=True)
            return e.matmul(ps[5][0:16, :], self.attT[:, :], self.u16a[:, 512:1024],
                            start=True, stop=True)
        hax = p.c("tensor", mm_ax, after=[hat, self.h_cst, self.h_bob])

        def ev_ax(e):
            e.tensor_copy(out=atxp[:, 0:512], in_=ps[4][0:16, :])
            return e.tensor_copy(out=atxp[:, 512:1024], in_=ps[5][0:16, :])
        hax2 = p.c("vector", ev_ax, after=[hax, hg_ev[-1]])

        hAe = []
        for vc in range(CH):
            b0, b1 = (0, 1) if vc % 2 == 0 else (2, 3)

            def mm_ae(e, vc=vc, b0=b0, b1=b1):
                e.matmul(ps[b0][:, :], self.u16b[:, 128 * vc:128 * vc + 128], atxp[:, 0:512],
                         start=True, stop=True)
                return e.matmul(ps[b1][:, :], self.u16b[:, 128 * vc:128 * vc + 128],
                                atxp[:, 512:1024], start=True, stop=True)
            hma = p.c("tensor", mm_ae, after=[hax2] + ([hAe[vc - 2]] if vc >= 2 else []))

            def ev_ae(e, vc=vc, b0=b0, b1=b1):
                e.tensor_tensor(out=Aexp[:, 1024 * vc:1024 * vc + 512], in0=ps[b0][:, :],
                                in1=self.m64[:, 0:512], op=ALU.mult)
                return e.tensor_tensor(out=Aexp[:, 1024 * vc + 512:1024 * vc + 1024],
                                       in0=ps[b1][:, :], in1=self.m64[:, 512:1024], op=ALU.mult)
            hAe.append(p.c("vector", ev_ae, after=[hma]))

        Wvmix = al.alloc("Wvmix", [128, 8 * 1024], BF16)
        hWm = []
        for xc in range(CH):
            b0, b1 = (0, 1) if xc % 2 == 0 else (2, 3)

            def mm_wm(e, xc=xc, b0=b0, b1=b1):
                last = None
                for k in range(CH):
                    e.matmul(ps[b0][:, :],
                             WvTb[:, 1024 * k + 128 * xc:1024 * k + 128 * xc + 128],
                             Aexp[:, 1024 * k:1024 * k + 512],
                             start=(k == 0), stop=(k == CH - 1))
                    last = e.matmul(ps[b1][:, :],
                                    WvTb[:, 1024 * k + 128 * xc:1024 * k + 128 * xc + 128],
                                    Aexp[:, 1024 * k + 512:1024 * k + 1024],
                                    start=(k == 0), stop=(k == CH - 1))
                return last
            hmw = p.c("tensor", mm_wm, after=[hAe[-1], h_wv] + ([hWm[xc - 2]] if xc >= 2 else []))

            def ev_wm(e, xc=xc, b0=b0, b1=b1):
                e.tensor_copy(out=Wvmix[:, 1024 * xc:1024 * xc + 512], in_=ps[b0][:, :])
                return e.tensor_copy(out=Wvmix[:, 1024 * xc + 512:1024 * xc + 1024],
                                     in_=ps[b1][:, :])
            hWm.append(p.c("vector", ev_wm, after=[hmw]))

        if getattr(self, "mtprobe", None):
            self.mtprobe(0, [hWm[-1]])
        al.free("Aexp", "WvTb", "atxp")
        vmixF = al.alloc("vmixF", [128, 8 * T], BF16)
        hvF = []
        for mc in range(CH):
            def mm_vf(e, mc=mc):
                last = None
                for k in range(CH):
                    for n in range(NT):
                        last = e.matmul(ps[n][:, :],
                                        Wvmix[:, 1024 * k + 128 * mc:1024 * k + 128 * mc + 128],
                                        xn[:, T * k + 512 * n:T * k + 512 * n + 512],
                                        start=(k == 0), stop=(k == CH - 1))
                return last
            hmv = p.c("tensor", mm_vf, after=[hWm[-1]] + ([hvF[mc - 1]] if mc >= 1 else []))

            def ev_vf(e, mc=mc):
                last = None
                for n in range(NT):
                    last = e.tensor_copy(out=vmixF[:, T * mc + 512 * n:T * mc + 512 * n + 512],
                                         in_=ps[n][:, :])
                return last
            hvF.append(p.c("vector", ev_vf, after=[hmv]))
        self.tap("vmixF", lambda c: vmixF[:, T * c:T * c + T], 8, T, BF16, [hvF[-1]])

        al.free("Wvmix")
        Wob = al.alloc("Wob", [128, 8 * 1024], BF16)
        h_wo = p.d("gpsimd", 1, lambda e: [e.dma_start(
            out=Wob[:, :].rearrange("p (c n) -> p c n", c=8),
            in_=self.Wo0_d.rearrange("(c p) n -> p c n", p=128))], after=[hvF[-1]])
        vmixr = [al.alloc(f"vmixr{i}", [128, 8 * 128], BF16) for i in range(2)]
        vstg = [al.alloc(f"vstg{i}", [128, 8 * 256], BF16) for i in range(2)]
        ubuf = [al.alloc(f"ubuf{i}", [128, 1024], F32) for i in range(2)]
        hvr = [None, None]
        hvp = [None, None]
        hvpe = [None, None]
        humm = [None, None]
        huev = [None, None]
        hu_dma = []
        for a in range(16):
            b = a % 2
            r0 = 64 * (a % 2)

            def cp_vr(e, a=a, b=b, r0=r0):
                last = None
                base = T * (a // 2)
                for j in range(16):
                    last = e.tensor_copy(
                        out=vstg[b][r0:r0 + 64, 256 * (j // 2) + 128 * (j % 2):
                                    256 * (j // 2) + 128 * (j % 2) + 128],
                        in_=vmixF[r0:r0 + 64, base + j:base + T:16])
                return last
            hvr[b] = p.c("vector", cp_vr, after=[hvF[a // 2]] + ([hvp[b]] if hvp[b] else []))

            def perm_vr(e, a=a, b=b, r0=r0):
                last = None
                b4 = 4 if b == 0 else 5
                for ci in range(CH):
                    for js in range(2):
                        tgt = 64 * js
                        sel = self.ident if tgt == r0 else self.swap64
                        last = e.matmul(
                            ps[b4][:, 0:128].bitcast(BF16)[:, 0:128] if False else ps[b4][:, 0:128],
                            sel[r0:r0 + 64, :],
                            vstg[b][r0:r0 + 64, 256 * ci + 128 * js:256 * ci + 128 * js + 128],
                            start=(js == 0), stop=(js == 1))
                return last

            def perm_and_evac(e, a=a, b=b, r0=r0):
                return None
            hplast = [hvr[b]]
            b4 = 4 if b == 0 else 5
            def perm_all(e, a=a, b=b, r0=r0, b4=b4):
                last = None
                for ci in range(CH):
                    for js in range(2):
                        tgt = 64 * js
                        sel = self.ident if tgt == r0 else self.swap64
                        last = e.matmul(
                            ps[b4][:, 128 * (ci % 4):128 * (ci % 4) + 128],
                            sel[r0:r0 + 64, :],
                            vstg[b][r0:r0 + 64, 256 * ci + 128 * js:256 * ci + 128 * js + 128],
                            start=(js == 0), stop=(js == 1))
                return last
            def perm_half(e, a=a, b=b, r0=r0, b4=b4, half=0):
                last = None
                for ci in range(4 * half, 4 * half + 4):
                    for js in range(2):
                        tgt = 64 * js
                        sel = self.ident if tgt == r0 else self.swap64
                        last = e.matmul(
                            ps[b4][:, 128 * (ci % 4):128 * (ci % 4) + 128],
                            sel[r0:r0 + 64, :],
                            vstg[b][r0:r0 + 64, 256 * ci + 128 * js:256 * ci + 128 * js + 128],
                            start=(js == 0), stop=(js == 1))
                return last
            def evac_half(e, b=b, b4=b4, half=0):
                last = None
                for ci in range(4 * half, 4 * half + 4):
                    last = e.tensor_copy(out=vmixr[b][:, 128 * ci:128 * ci + 128],
                                         in_=ps[b4][:, 128 * (ci % 4):128 * (ci % 4) + 128])
                return last
            import functools
            hp0 = p.c("tensor", functools.partial(perm_half, half=0),
                      after=[hvr[b]] + ([hvpe[b]] if hvpe[b] else []))
            he0 = p.c("vector", functools.partial(evac_half, half=0), after=[hp0])
            hp1 = p.c("tensor", functools.partial(perm_half, half=1), after=[he0])
            he1 = p.c("vector", functools.partial(evac_half, half=1), after=[hp1])
            hvp[b] = he1
            hvpe[b] = he1

            def mm_u(e, a=a, b=b):
                last = None
                b0, b1 = (0, 1) if b == 0 else (2, 3)
                for ci in range(CH):
                    e.matmul(ps[b0][:, :], vmixr[b][:, 128 * ci:128 * ci + 128],
                             Wob[:, 1024 * ci:1024 * ci + 512],
                             start=(ci == 0), stop=(ci == CH - 1))
                    last = e.matmul(ps[b1][:, :], vmixr[b][:, 128 * ci:128 * ci + 128],
                                    Wob[:, 1024 * ci + 512:1024 * ci + 1024],
                                    start=(ci == 0), stop=(ci == CH - 1))
                return last
            humm[b] = p.c("tensor", mm_u, after=[he1, h_wo] + ([huev[b]] if huev[b] else []))

            def ev_u(e, a=a, b=b):
                b0, b1 = (0, 1) if b == 0 else (2, 3)
                e.tensor_tensor(out=ubuf[b][:, 0:512], in0=ps[b0][:, :],
                                in1=self.bo_b[:, 0:512], op=ALU.add)
                return e.tensor_tensor(out=ubuf[b][:, 512:1024], in0=ps[b1][:, :],
                                       in1=self.bo_b[:, 512:1024], op=ALU.add)
            huev[b] = p.c("vector", ev_u, after=[humm[b], self.h_bob] +
                          ([hu_dma[-2]] if len(hu_dma) >= 2 else []))
            hu_dma.append(p.d("scalar", 1, lambda e, a=a, b=b: [
                e.dma_start(out=self.cc_u_in[128 * a:128 * a + 128, :], in_=ubuf[b][:, :])],
                after=[huev[b]]))

        if getattr(self, "mtprobe", None):
            self.mtprobe(1, [hvF[-1]])
            self.mtprobe(2, [huev[0], huev[1]])
        hag = p.cc(lambda e: e.collective_compute(
            "AllGather", ALU.bypass, replica_groups=G4,
            ins=[self.cc_u_in.opt()], outs=[self.cc_u_out.opt()]), after=hu_dma)

        return {"ag": hag, "vf": hvF[-1], "last_reads": [humm[0], humm[1], hu_dma[-1],
                                                         hu_dma[-2], hvr[0], hvr[1]]}

    def emit_consume_round(self, rr, hag, rounds, extra_after):
        p = self.p
        r, m = rr // 4, rr % 4
        b = rr % 2
        ust, ustb = self.ust, self.ustb
        prev2 = rounds[rr - 2] if rr >= 2 else None
        prev1 = rounds[rr - 1] if rr >= 1 else None
        hgat = p.d("gpsimd", 1, lambda e, r=r, m=m, b=b: [
            e.indirect_dma_start(
                out=ust[b][:, :], out_offset=None,
                in_=self.cc_u_out,
                in_offset=bass.IndirectOffsetOnAxis(
                    ap=self.uidx[:, 4 * r + m:4 * r + m + 1], axis=0))],
            after=[hag] + list(extra_after) + ([prev2["cst"]] if prev2 else []))
        hcst = p.c("scalar", lambda e, b=b: e.copy(out=ustb[b][:, :], in_=ust[b][:, :]),
                   after=[hgat] + ([prev2["tp"]] if prev2 else []))

        def tp(e, b=b):
            last = None
            for c in range(CH):
                last = e.transpose(self.psT[:, 128 * c:128 * c + 128],
                                   ustb[b][:, 128 * c:128 * c + 128], self.ident[:, :])
            return last
        htp = p.c("tensor", tp, after=[hcst] + ([prev1["ev"]] if prev1 else []) + extra_after)

        def ev(e, r=r, m=m):
            last = None
            for c in range(CH):
                last = e.copy(out=self.s_fm[:, T * c + 512 * m + 128 * r:
                                            T * c + 512 * m + 128 * r + 128],
                              in_=self.psT[:, 128 * c:128 * c + 128])
            return last
        hev = p.c("scalar", ev, after=[htp] + list(extra_after))
        return {"gat": hgat, "cst": hcst, "tp": htp, "ev": hev}

    def emit_ffn(self, layer, s_get, s_ready, x_src, x1_dst, deps, hook=None,
                 mid_free=None):
        p, al = self.p, self.al
        ps, psT = self.ps, self.psT
        xq = self.xq
        cd_col = 1 if layer == 0 else 3
        mxslot = 1 if layer == 0 else 3
        wg_d, wu_d, wd_d = self.wg_d[layer], self.wu_d[layer], self.wd_d[layer]

        wnat = [al.alloc(f"wnat{i}", [128, 1024], F32) for i in range(2)]
        unat = [al.alloc(f"unat{i}", [128, 1024], F32) for i in range(2)]
        wqn = [al.alloc(f"wqn{i}", [128, 1024], BF16) for i in range(2)]
        uqn = [al.alloc(f"uqn{i}", [128, 1024], BF16) for i in range(2)]
        wgT = [al.alloc(f"wgT{i}", [128, 1024], BF16) for i in range(2)]
        wuT = [al.alloc(f"wuT{i}", [128, 1024], BF16) for i in range(2)]
        qsc = self.qsc
        th1 = al.alloc("th1", [128, 512], F32, top=True)
        th2 = al.alloc("th2", [128, 512], F32, top=True)
        th3 = [al.alloc(f"th3{i}", [128, 512], F32, top=True) for i in range(2)]

        hz = p.c("vector", lambda e: e.memset(self.hmall[:, :], 0.0), after=list(deps))

        hqg = [None, None]
        hqu = [None, None]
        hwev = [None, None]
        hprod = {}
        hdh = {}
        hge_hist = {}
        last_hpd = None
        pending = None
        for i in range(IC):
            b = i % 2
            hlg = p.d("sync", 1, lambda e, i=i, b=b: [
                e.dma_start(out=wnat[b][:, :], in_=wg_d[128 * i:128 * i + 128, :])],
                after=list(deps) + ([hqg[b]] if hqg[b] else []))
            hlu = p.d("sync", 1, lambda e, i=i, b=b: [
                e.dma_start(out=unat[b][:, :], in_=wu_d[128 * i:128 * i + 128, :])],
                after=list(deps) + ([hqu[b]] if hqu[b] else []))

            def qg(e, i=i, b=b):
                e.reduce_max(out=qsc[:, 0:1], in_=wnat[b][:, :], axis=AX.X,
                             apply_absolute_value=True)
                e.drain()
                e.reciprocal(out=qsc[:, 1:2], in_=qsc[:, 0:1])
                e.drain()
                e.tensor_scalar(out=self.tfA[:, 0:1024], in0=wnat[b][:, :],
                                scalar1=qsc[:, 1:2], scalar2=127.0, op0=ALU.mult, op1=ALU.mult)
                e.tensor_scalar(out=wqn[b][:, :], in0=self.tfA[:, 0:1024], scalar1=MAGIC,
                                scalar2=MAGIC, op0=ALU.add, op1=ALU.subtract)
                return e.tensor_scalar(out=self.csg[:, i:i + 1], in0=qsc[:, 0:1],
                                       scalar1=self.scal[:, 0:1], scalar2=1.0 / 127.0,
                                       op0=ALU.mult, op1=ALU.mult)
            hq1 = p.c("vector", qg, after=[hlg, self.sx_h[layer]] +
                      ([hwev[b]] if hwev[b] else []) +
                      ([hge_hist[i - 2]] if i >= 2 else []))
            hqg[b] = hq1

            def qu(e, i=i, b=b):
                e.reduce_max(out=qsc[:, 0:1], in_=unat[b][:, :], axis=AX.X,
                             apply_absolute_value=True)
                e.drain()
                e.reciprocal(out=qsc[:, 1:2], in_=qsc[:, 0:1])
                e.drain()
                e.tensor_scalar(out=self.tfA[:, 0:1024], in0=unat[b][:, :],
                                scalar1=qsc[:, 1:2], scalar2=127.0, op0=ALU.mult, op1=ALU.mult)
                e.tensor_scalar(out=uqn[b][:, :], in0=self.tfA[:, 0:1024], scalar1=MAGIC,
                                scalar2=MAGIC, op0=ALU.add, op1=ALU.subtract)
                return e.tensor_scalar(out=self.csu[:, i:i + 1], in0=qsc[:, 0:1],
                                       scalar1=self.scal[:, 0:1], scalar2=1.0 / 127.0,
                                       op0=ALU.mult, op1=ALU.mult)
            hq2 = p.c("vector", qu, after=[hlu, hq1])
            hqu[b] = hq2

            def tw(e, b=b):
                last = None
                for k in range(CH):
                    last = e.transpose(psT[:, 128 * k:128 * k + 128],
                                       wqn[b][:, 128 * k:128 * k + 128], self.ident[:, :])
                return last
            htw = p.c("tensor", tw, after=[hq1] + ([hwev[b]] if hwev[b] else []) +
                      ([pending] if pending else []))
            pending = None
            hge = p.c("scalar", lambda e, b=b: e.copy(out=wgT[b][:, :], in_=psT[:, :]),
                      after=[htw])
            hge_hist[i] = hge

            def tu(e, b=b):
                last = None
                for k in range(CH):
                    last = e.transpose(psT[:, 128 * k:128 * k + 128],
                                       uqn[b][:, 128 * k:128 * k + 128], self.ident[:, :])
                return last
            htu = p.c("tensor", tu, after=[hq2, hge])
            hue = p.c("scalar", lambda e, b=b: e.copy(out=wuT[b][:, :], in_=psT[:, :]),
                      after=[htu])
            hwev[b] = hue

            for n in range(NT):
                bg, bu = (0, 1) if n % 2 == 0 else (2, 3)
                prev_pair = hprod.get((i, n - 2)) or (hprod.get((i - 1, n + 2)) if n < 2 else None)

                def mm_gu(e, i=i, b=b, n=n, bg=bg, bu=bu):
                    last = None
                    for k in range(CH):
                        e.matmul(ps[bg][:, :], wgT[b][:, 128 * k:128 * k + 128],
                                 xq[:, T * k + 512 * n:T * k + 512 * n + 512],
                                 start=(k == 0), stop=(k == CH - 1))
                    for k in range(CH):
                        last = e.matmul(ps[bu][:, :], wuT[b][:, 128 * k:128 * k + 128],
                                        xq[:, T * k + 512 * n:T * k + 512 * n + 512],
                                        start=(k == 0), stop=(k == CH - 1))
                    return last
                hmm = p.c("tensor", mm_gu, after=[hge, hue, self.xq_h[layer]] +
                          ([prev_pair] if prev_pair else []))
                hmm_last = hmm
                hsl = p.c("scalar", lambda e, i=i, bg=bg: e.activation(
                    out=th1[:, :], in_=ps[bg][:, :], func=AF.Silu,
                    scale=self.csg[:, i:i + 1], bias=self.zeros1[:, 0:1]),
                    after=[hmm, hz] + ([last_hpd] if last_hpd else []))

                def prod(e, i=i, n=n, bu=bu):
                    e.tensor_scalar(out=th2[:, :], in0=ps[bu][:, :],
                                    scalar1=self.csu[:, i:i + 1], scalar2=None, op0=ALU.mult)
                    e.tensor_tensor(out=th3[n % 2][:, :], in0=th1[:, :], in1=th2[:, :],
                                    op=ALU.mult)
                    return e.reduce_max(out=self.hmall[:, 4 * i + n:4 * i + n + 1],
                                        in_=th3[n % 2][:, :], axis=AX.X,
                                        apply_absolute_value=True)
                hpd = p.c("vector", prod, after=[hsl] +
                          ([hdh[(i, n - 2)]] if (i, n - 2) in hdh else
                           ([hdh[(i - 1, n + 2)]] if (i - 1, n + 2) in hdh else [])))
                last_hpd = hpd
                hprod[(i, n)] = hpd
                hdh[(i, n)] = p.d("scalar", 1, lambda e, i=i, n=n: [
                    e.dma_start(out=self.h_save[128 * i:128 * i + 128, 512 * n:512 * n + 512],
                                in_=th3[n % 2][:, :])], after=[hpd])
            if hook:
                pending = hook(i, hwev[b])

        al.free("wnat0", "wnat1", "unat0", "unat1", "wqn0", "wqn1", "uqn0", "uqn1",
                "wgT0", "wgT1", "wuT0", "wuT1")
        gateup_done = [hprod[(IC - 1, n)] for n in range(NT)]
        war = [hqg[0], hqg[1], hqu[0], hqu[1], hwev[0], hwev[1], hmm_last]
        if mid_free:
            war += list(mid_free() or [])

        wdnat = [al.alloc(f"wdnat{i}", [128, 2816], F32, top=True) for i in range(2)]
        wdq = al.alloc("wdq", [128, 2816], BF16, top=True)
        wdT = al.alloc("wdT", [128, 22 * 1024], BF16, top=True)
        hwd = []
        hdq = [None, None]
        first_td_extra = [pending] if pending else []
        for c in range(CH):
            b = c % 2
            hld = p.d("sync", 1, lambda e, c=c, b=b: [
                e.dma_start(out=wdnat[b][:, :], in_=wd_d[128 * c:128 * c + 128, :])],
                after=list(deps) + war + ([hdq[b]] if hdq[b] else []))

            def qd(e, c=c, b=b):
                e.reduce_max(out=self.rmd[:, c:c + 1], in_=wdnat[b][:, :], axis=AX.X,
                             apply_absolute_value=True)
                e.drain()
                e.reciprocal(out=qsc[:, 3:4], in_=self.rmd[:, c:c + 1])
                e.drain()
                for half in range(2):
                    sl = slice(1408 * half, 1408 * half + 1408)
                    e.tensor_scalar(out=self.tfA[:, 0:1408], in0=wdnat[b][:, sl],
                                    scalar1=qsc[:, 3:4], scalar2=127.0,
                                    op0=ALU.mult, op1=ALU.mult)
                    e.tensor_scalar(out=wdq[:, sl], in0=self.tfA[:, 0:1408], scalar1=MAGIC,
                                    scalar2=MAGIC, op0=ALU.add, op1=ALU.subtract)
                return e.engine_nop()
            hq = p.c("vector", qd, after=[hld] + ([hwd[-1]] if hwd else []))
            hdq[b] = hq
            hlast = hq
            for g in range(3):
                ks = list(range(8 * g, min(8 * g + 8, IC)))

                def tdg(e, ks=tuple(ks)):
                    last = None
                    for j, k in enumerate(ks):
                        last = e.transpose(psT[:, 128 * j:128 * j + 128],
                                           wdq[:, 128 * k:128 * k + 128], self.ident[:, :])
                    return last
                htd = p.c("tensor", tdg, after=[hlast] + (first_td_extra if (c == 0 and g == 0) else []))

                def evd(e, c=c, ks=tuple(ks)):
                    last = None
                    for j, k in enumerate(ks):
                        last = e.copy(out=wdT[:, 1024 * k + 128 * c:1024 * k + 128 * c + 128],
                                      in_=psT[:, 128 * j:128 * j + 128])
                    return last
                hlast = p.c("scalar", evd, after=[htd])
            hwd.append(hlast)

        hhm = p.c("vector", lambda e: e.reduce_max(out=self.xmaxc[:, 9:10],
                                                   in_=self.hmall[:, 0:88], axis=AX.X,
                                                   apply_absolute_value=True),
                  after=gateup_done)
        hdm = p.d("scalar", 1, lambda e: [e.dma_start(out=self.cc_mx_in[mxslot],
                                                      in_=self.xmaxc[:, 9:10])],
                  after=[hhm, hdh[(IC - 1, NT - 1)]])
        hcm2 = p.cc(lambda e: e.collective_compute(
            "AllReduce", ALU.max, replica_groups=G8,
            ins=[self.cc_mx_in[mxslot].opt()], outs=[self.cc_mx_out[mxslot].opt()]),
            after=[hdm])
        hrb = p.d("gpsimd", 1, lambda e: [e.dma_start(
            out=self.smrow[0:1, 64:192],
            in_=self.cc_mx_out[mxslot].rearrange("p one -> one p"))], after=[hcm2])
        hg1 = p.c("vector", lambda e: e.reduce_max(out=self.smrow[0:1, 8:9],
                                                   in_=self.smrow[0:1, 64:192], axis=AX.X),
                  after=[hrb])
        hg2 = p.c("scalar", lambda e: e.activation(out=self.smrow[0:1, 9:10],
                                                   in_=self.clips[0:1, cd_col:cd_col + 1],
                                                   func=AF.Sigmoid, bias=self.zeros1[0:1, 0:1]),
                  after=[hg1])

        def sh_fin(e):
            e.drain()
            e.tensor_scalar(out=self.smrow[0:1, 10:11], in0=self.smrow[0:1, 9:10],
                            scalar1=self.smrow[0:1, 8:9], scalar2=1.0 / 127.0,
                            op0=ALU.mult, op1=ALU.mult)
            e.drain()
            return e.reciprocal(out=self.smrow[0:1, 11:12], in_=self.smrow[0:1, 10:11])
        hg3 = p.c("vector", sh_fin, after=[hg2])

        def sh_bc(e):
            e.matmul(self.psX[:, 0:1], self.onesrow[0:1, :], self.smrow[0:1, 10:11],
                     start=True, stop=True)
            return e.matmul(self.psX[:, 1:2], self.onesrow[0:1, :], self.smrow[0:1, 11:12],
                            start=True, stop=True)
        hg4 = p.c("tensor", sh_bc, after=[hg3])

        def sh_ev(e):
            e.tensor_copy(out=self.scal[:, 2:3], in_=self.psX[:, 0:1])
            e.tensor_copy(out=self.scal[:, 3:4], in_=self.psX[:, 1:2])
            e.drain()
            e.tensor_scalar(out=self.csd[:, :], in0=self.rmd[:, :],
                            scalar1=self.scal[:, 2:3], scalar2=1.0 / 127.0,
                            op0=ALU.mult, op1=ALU.mult)
            return e.drain()
        hg5 = p.c("vector", sh_ev, after=[hg4, hwd[-1]])

        al.free("wdnat0", "wdnat1", "wdq")
        qhbuf = [[al.alloc(f"qh{i}_{g}", [128, (8 if g < 2 else 6) * 512], BF16)
                  for g in range(3)] for i in range(2)]
        xload = [al.alloc(f"xl{i}", [128, 512], F32) for i in range(2)]
        hstg = [al.alloc(f"hstg{i}", [128, 512], F32) for i in range(4)]

        hcomb = []
        hqh = {}
        y_dmas = []
        for n in range(NT):
            qbuf = qhbuf[n % 2]
            hq_handles = []
            for k in range(IC):
                bb = k % 4
                war_h = ([hq_handles[k - 4]] if k >= 4 else
                         ([hqh[(n - 1, IC - 4 + k)]] if n >= 1 else []))
                hl = p.d("sync", 1, lambda e, k=k, n=n, bb=bb: [
                    e.dma_start(out=hstg[bb][:, :],
                                in_=self.h_save[128 * k:128 * k + 128, 512 * n:512 * n + 512])],
                    after=[hdh[(k, n)], hwd[-1]] + war_h +
                          ([hqh[(n - 2, k)]] if (n - 2, k) in hqh else []))

                def qh_c(e, k=k, bb=bb, qbuf=qbuf):
                    gq, kq = k // 8, k % 8
                    e.tensor_scalar(out=th2[:, :], in0=hstg[bb][:, :],
                                    scalar1=self.scal[:, 3:4], scalar2=MAGIC,
                                    op0=ALU.mult, op1=ALU.add)
                    e.tensor_scalar(out=th2[:, :], in0=th2[:, :], scalar1=MAGIC, scalar2=127.0,
                                    op0=ALU.subtract, op1=ALU.min)
                    return e.tensor_scalar(out=qbuf[gq][:, 512 * kq:512 * kq + 512],
                                           in0=th2[:, :],
                                           scalar1=-128.0, scalar2=None, op0=ALU.max)
                hq = p.c("vector", qh_c, after=[hl, hg5])
                hq_handles.append(hq)
                hqh[(n, k)] = hq
            for c in range(CH):
                bk = c % 4
                bx = c % 2
                hlx = p.d("sync", 1, lambda e, c=c, n=n, bx=bx: [
                    e.dma_start(out=xload[bx][:, :],
                                in_=x_src[128 * c:128 * c + 128, 512 * n:512 * n + 512])],
                    after=[hg5] + ([hcomb[-2]] if len(hcomb) >= 2 else []))

                def mm_d(e, ks, c=c, bk=bk, qbuf=qbuf):
                    last = None
                    for k in ks:
                        gq, kq = k // 8, k % 8
                        last = e.matmul(ps[bk][:, :],
                                        wdT[:, 1024 * k + 128 * c:1024 * k + 128 * c + 128],
                                        qbuf[gq][:, 512 * kq:512 * kq + 512],
                                        start=(k == 0), stop=(k == IC - 1))
                    return last
                import functools as _ft
                war_bank = [hcomb[-4]] if len(hcomb) >= 4 else []
                hd1 = p.c("tensor", _ft.partial(mm_d, ks=range(0, 8)),
                          after=[hq_handles[7]] + war_bank)
                hd2 = p.c("tensor", _ft.partial(mm_d, ks=range(8, 16)),
                          after=[hq_handles[15], hd1])
                hmmd = p.c("tensor", _ft.partial(mm_d, ks=range(16, IC)),
                           after=[hq_handles[IC - 1], hd2])
                hsl2 = p.c("scalar", lambda e, c=c, bk=bk: e.activation(
                    out=th1[:, :], in_=ps[bk][:, :], func=AF.Silu,
                    scale=self.csd[:, c:c + 1], bias=self.zeros1[:, 0:1]),
                    after=[hmmd] + ([hcomb[-1]] if hcomb else []))

                def comb(e, c=c, n=n, bx=bx):
                    e.tensor_tensor(out=th2[:, :], in0=th1[:, :],
                                    in1=s_get(c)[:, 512 * n:512 * n + 512], op=ALU.mult)
                    return e.tensor_tensor(out=th3[c % 2][:, :], in0=th2[:, :],
                                           in1=xload[bx][:, :], op=ALU.add)
                hcb = p.c("vector", comb,
                          after=[hsl2, hlx] + list(s_ready) +
                          ([y_dmas[-2]] if len(y_dmas) >= 2 else []))
                hcomb.append(hcb)
                y_dmas.append(p.d("scalar", 1, lambda e, c=c, n=n: [
                    e.dma_start(out=x1_dst[128 * c:128 * c + 128, 512 * n:512 * n + 512],
                                in_=th3[c % 2][:, :])], after=[hcb]))

        al.free("qh0_0", "qh0_1", "qh0_2", "qh1_0", "qh1_1", "qh1_2",
                "xl0", "xl1", "hstg0", "hstg1", "hstg2", "hstg3",
                "wdT", "th1", "th2", "th30", "th31")
        return {"done": hcomb[-1], "y_dmas": y_dmas}

    def emit_attn1(self, hln, deps, wdeps, post_scan_hook=None):
        p, al = self.p, self.al
        ps = self.ps
        xn = self.xn
        tfB = self.tfB

        WzTb = al.alloc("WzTb", [128, 8 * 1024], BF16, top=True)
        WhTb = al.alloc("WhTb", [128, 8 * 1024], BF16, top=True)
        hwz = p.d("gpsimd", 1, lambda e: [e.dma_start(
            out=WzTb[:, :].rearrange("p (c n) -> p c n", c=8),
            in_=self.WzT_d.rearrange("(c p) n -> p c n", p=128))], after=list(wdeps))
        hwh = p.d("gpsimd", 1, lambda e: [e.dma_start(
            out=WhTb[:, :].rearrange("p (c n) -> p c n", c=8),
            in_=self.WhT_d.rearrange("(c p) n -> p c n", p=128))], after=list(wdeps))

        zt = al.alloc("zt", [128, T], F32)
        hloc = al.alloc("hloc", [128, 8 * T], BF16)
        Aloc = al.alloc("Aloc", [128, 8 * T], BF16)
        tfC = al.alloc("tfC", [128, T], F32)
        sA = [self.tfA, al.alloc("tfA2", [128, T], F32)]
        tfD = al.alloc("tfD", [128, T], F32)
        sD = [tfD, tfD]
        hz0 = p.c("gpsimd", lambda e: e.memset(zt[:, :], 0.0), after=list(deps))

        hsc = []
        het_prev = None
        for c in range(CH):
            A, D = sA[c % 2], sD[c % 2]

            def mm_z(e, c=c):
                last = None
                for k in range(CH):
                    for n in range(NT):
                        last = e.matmul(ps[n][:, :],
                                        WzTb[:, 1024 * k + 128 * c:1024 * k + 128 * c + 128],
                                        xn[:, T * k + 512 * n:T * k + 512 * n + 512],
                                        start=(k == 0), stop=(k == CH - 1))
                return last
            hmz = p.c("tensor", mm_z, after=[hwz, hln["xn"]] +
                      ([het_prev] if het_prev else []))

            def ev_c(e, c=c, A=A):
                last = None
                for n in range(NT):
                    last = e.activation(out=A[:, 512 * n:512 * n + 512], in_=ps[n][:, :],
                                        func=AF.Sigmoid, scale=-1.0,
                                        bias=self.negbz[:, c:c + 1])
                return last
            hec = p.c("scalar", ev_c, after=[hmz, self.h_bias] +
                      ([hsc[c - 2]] if c >= 2 else []))

            def mm_h(e, c=c):
                last = None
                for k in range(CH):
                    for n in range(NT):
                        last = e.matmul(ps[n][:, :],
                                        WhTb[:, 1024 * k + 128 * c:1024 * k + 128 * c + 128],
                                        xn[:, T * k + 512 * n:T * k + 512 * n + 512],
                                        start=(k == 0), stop=(k == CH - 1))
                return last
            hmh = p.c("tensor", mm_h, after=[hwh, hec])

            def ev_t(e, c=c, A=A, D=D):
                for n in range(NT):
                    sl = slice(512 * n, 512 * n + 512)
                    e.scalar_tensor_tensor(out=tfB[:, sl], in0=ps[n][:, :],
                                           scalar=self.bhm05[:, c:c + 1], in1=A[:, sl],
                                           op0=ALU.add, op1=ALU.add)
                e.tensor_scalar(out=tfB[:, :], in0=tfB[:, :], scalar1=0.0, scalar2=None,
                                op0=ALU.max)
                e.tensor_scalar(out=tfC[:, :], in0=A[:, :], scalar1=-1.0, scalar2=1.0,
                                op0=ALU.mult, op1=ALU.add)
                e.tensor_tensor(out=tfB[:, :], in0=tfB[:, :], in1=tfC[:, :], op=ALU.add)
                return e.tensor_tensor(out=D[:, :], in0=tfC[:, :], in1=tfB[:, :],
                                       op=ALU.mult)
            het = p.c("vector", ev_t, after=[hmh])
            het_prev = het

            def scans(e, c=c, A=A, D=D):
                e.tensor_tensor_scan(out=hloc[:, T * c:T * c + T], data0=A[:, :],
                                     data1=D[:, :], initial=0.0, op0=ALU.mult, op1=ALU.add)
                return e.tensor_tensor_scan(out=Aloc[:, T * c:T * c + T], data0=A[:, :],
                                            data1=zt[:, :], initial=1.0, op0=ALU.mult,
                                            op1=ALU.add)
            hsc.append(p.c("vector", scans, after=[het, hz0]))

            def ssum_c(e, c=c):
                e.tensor_copy(out=self.ssum[:, 2 * c:2 * c + 1],
                              in_=Aloc[:, T * c + T - 1:T * c + T])
                return e.tensor_copy(out=self.ssum[:, 2 * c + 1:2 * c + 2],
                                     in_=hloc[:, T * c + T - 1:T * c + T])
            hsum = p.c("gpsimd", ssum_c, after=[hsc[-1]])

        hln["sx"] = hln["sx_fin"]()
        if post_scan_hook:
            al.free("tfC", "tfD", "tfA2", "zt", "WzTb", "WhTb")
            post_scan_hook(hsc[-1])
        hds = p.d("scalar", 1, lambda e: [e.dma_start(out=self.cc_s_in, in_=self.ssum[:, :])],
                  after=[hsc[-1], hsum])
        hcs = p.cc(lambda e: e.collective_compute(
            "AllGather", ALU.bypass, replica_groups=G4,
            ins=[self.cc_s_in.opt()], outs=[self.cc_s_out.opt()]), after=[hds])
        hrs = p.d("gpsimd", 4, lambda e: [
            e.dma_start(out=self.sgth[j][:, :], in_=self.cc_s_out[128 * j:128 * j + 128, :])
            for j in range(4)], after=[hcs])

        hfx = []
        hpre_tap = None
        if "h1pre" in TAPS:
            hpo = self.nc.dram_tensor("tap_h1pre", [128 * 8, T], BF16,
                                      kind="ExternalOutput").ap()
            self.tap_outs["h1pre"] = hpo
            hpre_tap = p.d("scalar", 8, lambda e: [
                e.dma_start(out=hpo[128 * c:128 * c + 128, :], in_=hloc[:, T * c:T * c + T])
                for c in range(CH)], after=[hsc[-1]])
        if "A1" in TAPS:
            apo = self.nc.dram_tensor("tap_A1", [128 * 8, T], BF16,
                                      kind="ExternalOutput").ap()
            self.tap_outs["A1"] = apo
            hpre_a = p.d("scalar", 8, lambda e: [
                e.dma_start(out=apo[128 * c:128 * c + 128, :], in_=Aloc[:, T * c:T * c + T])
                for c in range(CH)], after=[hsc[-1]])
        if "cv7" in TAPS:
            cvo = self.nc.dram_tensor("tap_cv7", [256, T], F32, kind="ExternalOutput").ap()
            self.tap_outs["cv7"] = cvo
            hpre_cv = p.d("scalar", 2, lambda e: [
                e.dma_start(out=cvo[0:128, :], in_=sA[1][:, :]),
                e.dma_start(out=cvo[128:256, :], in_=sD[1][:, :])], after=[hsc[-1]])

        def pfx_all(e):
            sg = self.sgth
            e.tensor_tensor(out=self.pfx[:, 0:8], in0=sg[1][:, 0:16:2],
                            in1=sg[0][:, 1:16:2], op=ALU.mult)
            e.drain()
            e.tensor_tensor(out=self.pfx[:, 0:8], in0=self.pfx[:, 0:8],
                            in1=sg[1][:, 1:16:2], op=ALU.add)
            e.drain()
            e.tensor_tensor(out=self.pfx[:, 8:16], in0=sg[2][:, 0:16:2],
                            in1=self.pfx[:, 0:8], op=ALU.mult)
            e.drain()
            e.tensor_tensor(out=self.pfx[:, 8:16], in0=self.pfx[:, 8:16],
                            in1=sg[2][:, 1:16:2], op=ALU.add)
            e.drain()
            e.tensor_scalar(out=self.pfx[:, 16:24], in0=sg[0][:, 1:16:2],
                            scalar1=self.cp[:, 3:4], scalar2=None, op0=ALU.mult)
            e.drain()
            e.scalar_tensor_tensor(out=self.pfx[:, 16:24], in0=self.pfx[:, 0:8],
                                   scalar=self.cp[:, 4:5], in1=self.pfx[:, 16:24],
                                   op0=ALU.mult, op1=ALU.add)
            e.drain()
            e.scalar_tensor_tensor(out=self.pfx[:, 16:24], in0=self.pfx[:, 8:16],
                                   scalar=self.cp[:, 5:6], in1=self.pfx[:, 16:24],
                                   op0=ALU.mult, op1=ALU.add)
            return e.drain()
        hpfx = p.c("vector", pfx_all, after=[hrs, self.h_cst])
        if "sg1" in TAPS:
            sgo = self.nc.dram_tensor("tap_sg1", [128, 88], F32, kind="ExternalOutput").ap()
            self.tap_outs["sg1"] = sgo
            p.d("scalar", 5, lambda e: [
                e.dma_start(out=sgo[:, 0:16], in_=self.sgth[0][:, :]),
                e.dma_start(out=sgo[:, 16:32], in_=self.sgth[1][:, :]),
                e.dma_start(out=sgo[:, 32:48], in_=self.sgth[2][:, :]),
                e.dma_start(out=sgo[:, 48:64], in_=self.sgth[3][:, :]),
                e.dma_start(out=sgo[:, 64:88], in_=self.pfx[:, :]),
            ], after=[hpfx])
        for c in range(CH):
            def fix_c(e, c=c):
                return e.scalar_tensor_tensor(out=hloc[:, T * c:T * c + T],
                                              in0=Aloc[:, T * c:T * c + T],
                                              scalar=self.pfx[:, 16 + c:17 + c],
                                              in1=hloc[:, T * c:T * c + T],
                                              op0=ALU.mult, op1=ALU.add)
            hfx.append(p.c("vector", fix_c, after=[hpfx] +
                           ([hpre_tap] if hpre_tap else [])))
        self.tap("h1", lambda c: hloc[:, T * c:T * c + T], 8, T, BF16, hfx[-1:])

        Wo1Tb = self.Wo1Tb
        hwo = p.d("gpsimd", 1, lambda e: [e.dma_start(
            out=Wo1Tb[:, :].rearrange("p (c n) -> p c n", c=8),
            in_=self.Wo1T_d.rearrange("(c p) n -> p c n", p=128))], after=[hsc[-1]])
        al.free("Aloc")
        s1 = al.alloc("s1", [128, 8 * T], BF16)
        hs1 = []
        for dc in range(CH):
            def mm_o(e, dc=dc):
                last = None
                for k in range(CH):
                    for n in range(NT):
                        last = e.matmul(ps[n][:, :],
                                        Wo1Tb[:, 1024 * k + 128 * dc:1024 * k + 128 * dc + 128],
                                        hloc[:, T * k + 512 * n:T * k + 512 * n + 512],
                                        start=(k == 0), stop=(k == CH - 1))
                return last
            hmo = p.c("tensor", mm_o, after=[hwo, hfx[-1]] + ([hs1[dc - 1]] if dc >= 1 else []))

            def ev_o(e, dc=dc):
                last = None
                for n in range(NT):
                    last = e.tensor_scalar(out=s1[:, T * dc + 512 * n:T * dc + 512 * n + 512],
                                           in0=ps[n][:, :],
                                           scalar1=self.bo1_fm[:, dc:dc + 1],
                                           scalar2=None, op0=ALU.add)
                return last
            hs1.append(p.c("vector", ev_o, after=[hmo]))
        self.tap("s1", lambda c: s1[:, T * c:T * c + T], 8, T, BF16, hs1[-1:])

        al.free("Wo1Tb", "hloc")
        self.s1 = s1
        return {"s1": hs1[-1], "hloc_reads": [hs1[-1]]}

    def build(self):
        al = self.al
        self.decl()
        self.setup()
        al.free("bo_row")
        self.sx_h = {}
        self.xq_h = {}

        self.xn = al.alloc("xn", [128, 8 * T], BF16)
        self.xt = [al.alloc(f"xt{i}", [128, T], F32) for i in range(2)]
        hln0 = self.emit_ln(0, self.x_in, [self.h_cst, self.h_init, self.h_bob])
        self.tap("xn0", lambda c, t=self.xn: t[:, T * c:T * c + T], 8, T, BF16, [hln0["xn"]])

        self.qdbg = al.alloc("qdbg", [128, 512], F32) if os.environ.get("KB_DBGQ") else None

        def probe(slot, after):
            if self.qdbg is None:
                return
            self.p.c("vector", lambda e, slot=slot: e.tensor_copy(
                out=self.qdbg[:, slot * 128:(slot + 1) * 128], in_=self.mt[:, 0:128]),
                after=after)
        self.mtprobe = probe

        def post_energy():
            self.xq = al.alloc("xq", [128, 8 * T], BF16)
            self.xq_h[0] = self.emit_quant(self.x_in, [hln0["sx"], hln0["stats"]])
        a0 = self.emit_attn0(hln0, post_energy_hook=post_energy)
        self.sx_h[0] = hln0["sx"]
        if getattr(self, "dev_stopped", False):
            return
        if self.qdbg is not None:
            qdo = self.nc.dram_tensor("tap_qdbg", [128, 512], F32, kind="ExternalOutput").ap()
            self.tap_outs["qdbg"] = qdo
            self.p.d("scalar", 1, lambda e: [e.dma_start(out=qdo, in_=self.qdbg[:, :])],
                     after=[self.xq_h[0]])
            al.free("qdbg")
        self.tap("xq0", lambda c, t=self.xq: t[:, T * c:T * c + T], 8, T, BF16, [self.xq_h[0]])
        if DEV_STOP == "quant0":
            self.p.c("sync", lambda e: e.nop(), after=[self.xq_h[0], a0["ag"]])
            self.p.emit(self.nc)
            return

        al.free("vmixF", "Wob", "vmixr0", "vmixr1", "vstg0", "vstg1", "ubuf0", "ubuf1")
        al.free("xt0", "xt1")

        self.ust = [al.alloc(f"ust{i}", [128, 1024], F32, top=True) for i in range(2)]
        self.ustb = [al.alloc(f"ustb{i}", [128, 1024], BF16, top=True) for i in range(2)]
        self.s_fm = al.alloc("s_fm", [128, 8 * T], BF16, top=True)
        rounds = []
        s_evs = []

        def hook(i, hwev_b):
            if os.environ.get("KB_NO_ROUNDS"):
                return None
            if i < 5 or len(rounds) >= 16:
                return None
            r = self.emit_consume_round(len(rounds), a0["ag"],
                                        rounds, [hwev_b, self.xq_h[0]] + a0["last_reads"])
            rounds.append(r)
            s_evs.append(r["ev"])
            return r["ev"]

        def mid_free0():
            al.free("ust0", "ust1", "ustb0", "ustb1", "xn", "xq")
            return ([r["ev"] for r in rounds[-2:]] +
                    ([rounds[-1]["gat"]] if rounds else []))

        f0 = self.emit_ffn(0, lambda c: self.s_fm[:, T * c:T * c + T], s_evs,
                           self.x_in, self.x1_save,
                           [self.xq_h[0], a0["vf"]] + a0["last_reads"],
                           hook=hook, mid_free=mid_free0)
        assert os.environ.get("KB_NO_ROUNDS") or len(rounds) == 16, len(rounds)
        self.tap("s_fm", lambda c, t=self.s_fm: t[:, T * c:T * c + T], 8, T, BF16,
                 s_evs[-1:] + [f0["done"]])

        if DEV_STOP == "sfm0":
            hfin = self.p.d("gpsimd", 8, lambda e: [
                e.dma_start(out=self.y_out[128 * c:128 * c + 128, :].bitcast(BF16)[:, 0:T],
                            in_=self.s_fm[:, T * c:T * c + T]) for c in range(8)],
                after=s_evs[-1:] + [f0["done"]])
            self.p.c("sync", lambda e: e.nop(), after=[hfin])
            self.p.emit(self.nc)
            return

        if DEV_STOP == "hsave0":
            hfin = self.p.d("gpsimd", 1, lambda e: [
                e.dma_start(out=self.y_out, in_=self.h_save[0:1024, :])],
                after=[f0["done"], f0["y_dmas"][-1]])
            self.p.c("sync", lambda e: e.nop(), after=[hfin])
            self.p.emit(self.nc)
            return

        if DEV_STOP == "ffn0":
            hfin = self.p.d("gpsimd", 1, lambda e: [
                e.dma_start(out=self.y_out, in_=self.x1_save)],
                after=[f0["done"], f0["y_dmas"][-1]])
            self.p.c("sync", lambda e: e.nop(), after=[hfin])
            self.p.emit(self.nc)
            return

        al.free("m64", "bo_b", "u16a", "u16b")
        self.xn = al.alloc("xn", [128, 8 * T], BF16)
        self.xt = [al.alloc(f"xt{i}", [128, T], F32) for i in range(2)]
        hln1 = self.emit_ln(1, self.x1_save, [f0["done"], f0["y_dmas"][-1]])
        self.tap("xn1", lambda c, t=self.xn: t[:, T * c:T * c + T], 8, T, BF16, [hln1["xn"]])

        al.free("s_fm", "xt0", "xt1")
        def post_scan(hscan_done):
            self.xq = al.alloc("xq", [128, 8 * T], BF16, top=True)
            self.xt = [al.alloc(f"xtq{i}", [128, T], F32) for i in range(2)]
            self.Wo1Tb = al.alloc("Wo1Tb", [128, 8 * 1024], BF16)
            self.xq_h[1] = self.emit_quant(self.x1_save,
                                           [hln1["sx"], hln1["stats"], hscan_done])
        a1 = self.emit_attn1(hln1, [f0["done"], hln1["xn"]],
                             wdeps=[f0["done"], f0["y_dmas"][-1]],
                             post_scan_hook=post_scan)
        self.sx_h[1] = hln1["sx"]


        def mid_free1():
            al.free("xn", "xq", "xtq0", "xtq1")

        f1 = self.emit_ffn(1, lambda c: self.s1[:, T * c:T * c + T], [a1["s1"]],
                           self.x1_save, self.y_out, [self.xq_h[1], a1["s1"]],
                           hook=None, mid_free=mid_free1)

        self.p.c("scalar", lambda e: e.nop(), after=f1["y_dmas"][-1:])
        self.p.c("sync", lambda e: e.nop(), after=f1["y_dmas"][-1:])

        self.p.emit(self.nc)


def _host_inputs(inputs):
    x = np.ascontiguousarray(np.asarray(inputs["L0_attn_x"], dtype=np.float32))
    f32 = lambda a: np.ascontiguousarray(np.asarray(a, dtype=np.float32))
    clips = np.array([[np.asarray(inputs["L0_ffn_clip_x"]).reshape(-1)[0],
                       np.asarray(inputs["L0_ffn_clip_down"]).reshape(-1)[0],
                       np.asarray(inputs["L1_ffn_clip_x"]).reshape(-1)[0],
                       np.asarray(inputs["L1_ffn_clip_down"]).reshape(-1)[0]]],
                     dtype=np.float32)
    pp = np.arange(128)
    kk = np.arange(1024)
    m64 = ((pp[:, None] % 64) == (kk[None, :] % 64)).astype(np.float32)
    ident = np.eye(128, dtype=ml_dtypes.bfloat16)
    aa = np.arange(16)
    u16 = (aa[:, None] == (kk[None, :] // 64)).astype(ml_dtypes.bfloat16)
    common = {
        "Wq": f32(inputs["L0_attn_Wq"]), "Wk": f32(inputs["L0_attn_Wk"]),
        "WvT": np.ascontiguousarray(f32(inputs["L0_attn_Wv"]).T),
        "Wo0": f32(inputs["L0_attn_Wo"]),
        "bo0": f32(inputs["L0_attn_bo"]).reshape(1, 1024),
        "wg0": f32(inputs["L0_ffn_w_gate"]), "wu0": f32(inputs["L0_ffn_w_up"]),
        "wd0": f32(inputs["L0_ffn_w_down"]),
        "wg1": f32(inputs["L1_ffn_w_gate"]), "wu1": f32(inputs["L1_ffn_w_up"]),
        "wd1": f32(inputs["L1_ffn_w_down"]),
        "WzT": np.ascontiguousarray(f32(inputs["L1_attn_Wz"]).T),
        "WhT": np.ascontiguousarray(f32(inputs["L1_attn_Wh"]).T),
        "Wo1T": np.ascontiguousarray(f32(inputs["L1_attn_Wo"]).T),
        "bz_fm": np.ascontiguousarray(f32(inputs["L1_attn_bz"]).reshape(8, 128).T),
        "bh_fm": np.ascontiguousarray(f32(inputs["L1_attn_bh"]).reshape(8, 128).T),
        "bo1_fm": np.ascontiguousarray(f32(inputs["L1_attn_bo"]).reshape(8, 128).T),
        "clips": clips, "m64row": m64, "ident_bf": ident,
        "swap64_bf": np.block([[np.zeros((64, 64)), np.eye(64)],
                               [np.eye(64), np.zeros((64, 64))]]).astype(ml_dtypes.bfloat16),
        "u16a": u16, "u16b": u16.copy(),
    }
    in_maps = []
    for q in range(NCORE):
        n, qq = q // 4, q % 4
        xs = np.ascontiguousarray(x[n, 2048 * qq:2048 * (qq + 1), :].T)
        cp = np.zeros((128, 16), np.float32)
        cp[:, 0] = 1.0 if n == 0 else 0.0
        cp[:, 1] = 1.0 if n == 1 else 0.0
        cp[:, 2 + qq] = 1.0
        uidx = np.zeros((128, 16), np.int32)
        for r in range(4):
            for m in range(4):
                uidx[:, 4 * r + m] = 2048 * r + 128 * (4 * qq + m) + np.arange(128)
        in_maps.append(dict(common, x_fm=xs, coreparams=cp, uidx=uidx))
    return in_maps, x.shape


_CACHE = {}


def kernel(**inputs):
    in_maps, xshape = _host_inputs(inputs)
    if "kb" not in _CACHE:
        _CACHE["kb"] = KB()
    kb = _CACHE["kb"]
    res = run_bass_kernel_spmd(kb.nc, in_maps, core_ids=list(range(NCORE)))
    out = np.empty(xshape, dtype=np.float32)
    for q in range(NCORE):
        n, qq = q // 4, q % 4
        out[n, 2048 * qq:2048 * (qq + 1), :] = res.results[q]["y"].T
    kernel.last_results = res
    return out


# version 09
# speedup vs baseline: 1.0253x; 1.0011x over previous
import os
import numpy as np
import ml_dtypes
import concourse.bass as bass
import concourse.mybir as mybir
from concourse.bass_utils import run_bass_kernel_spmd

F32 = mybir.dt.float32
BF16 = mybir.dt.bfloat16
I32 = mybir.dt.int32
AF = mybir.ActivationFunctionType
ALU = mybir.AluOpType
AX = mybir.AxisListType

MAGIC = 12582912.0
NCORE = 8
T = 2048
CH = 8
IC = 22
NT = 4
SBUF_BASE = 16384
SBUF_TOP = 229312
G8 = [[0, 1, 2, 3, 4, 5, 6, 7]]
G4 = [[0, 1, 2, 3], [4, 5, 6, 7]]
ENGINES = ("sync", "tensor", "vector", "scalar", "gpsimd")

TAPS = ()
DEV_STOP = os.environ.get("KB_DEV_STOP", "")


class Prog:
    RDMA = 8

    def __init__(self):
        self.steps = {e: [] for e in ENGINES}
        self.pc = {e: 0 for e in ENGINES}
        self.dn = {e: 0 for e in ENGINES}
        self.dtot = {e: [0] * self.RDMA for e in ENGINES}
        self.ccc = 0

    def _waits(self, eng, after):
        waits = {}
        items = []
        for h in after:
            if h is None:
                continue
            if isinstance(h, list):
                items.extend(h)
            else:
                items.append(h)
        for key, val in items:
            if key == ("p", eng):
                continue
            waits[key] = max(waits.get(key, 0), val)
        return waits

    def c(self, eng, fn, after=()):
        self.steps[eng].append(("c", fn, self._waits(eng, after)))
        self.pc[eng] += 1
        return (("p", eng), self.pc[eng])

    def d(self, eng, ndma, fn, after=()):
        assert ndma <= self.RDMA, f"dma step with {ndma} > {self.RDMA} dmas"
        waits = self._waits(eng, after)
        plan = []
        for i in range(ndma):
            slot = self.dn[eng] % self.RDMA
            pre = self.dtot[eng][slot]
            self.dtot[eng][slot] = pre + 16
            self.dn[eng] += 1
            plan.append((slot, pre))
            if pre > 0:
                k = ("d", eng, slot)
                waits[k] = max(waits.get(k, 0), pre)
        self.steps[eng].append(("d", fn, waits, plan))
        handle = {}
        for slot, pre in plan:
            handle[("d", eng, slot)] = pre + 16
        return [(k, v) for k, v in handle.items()]

    def cc(self, fn, after=()):
        self.steps["gpsimd"].append(("cc", fn, self._waits("gpsimd", after)))
        self.ccc += 1
        return (("cc",), self.ccc)

    def emit(self, nc):
        sems = {}
        for e in ENGINES:
            sems[("p", e)] = nc.semaphore(f"p_{e}").__enter__()
            for j in range(self.RDMA):
                sems[("d", e, j)] = nc.semaphore(f"d_{e}{j}").__enter__()
        sems[("cc",)] = nc.semaphore("ccsem").__enter__()
        block = nc.Block().__enter__()

        def run(engobj, ename):
            for step in self.steps[ename]:
                kind, fn, waits = step[0], step[1], step[2]
                for key in sorted(waits, key=str):
                    engobj.wait_ge(sems[key], waits[key])
                if kind == "c":
                    last = fn(engobj)
                    assert last is not None, f"compute step on {ename} returned None"
                    last.then_inc(sems[("p", ename)], 1)
                elif kind == "d":
                    plan = step[3]
                    instrs = fn(engobj)
                    assert isinstance(instrs, (list, tuple)) and len(instrs) == len(plan)
                    for ins, (slot, pre) in zip(instrs, plan):
                        ins.then_inc(sems[("d", ename, slot)], 16)
                else:
                    fn(engobj).then_inc(sems[("cc",)], 1)

        block.sync(lambda e: run(e, "sync"))
        block.tensor(lambda e: run(e, "tensor"))
        block.vector(lambda e: run(e, "vector"))
        block.scalar(lambda e: run(e, "scalar"))
        block.gpsimd(lambda e: run(e, "gpsimd"))
        block.__exit__(None, None, None)


class Alloc:
    def __init__(self, nc):
        self.nc = nc
        self.free_list = [(SBUF_BASE, SBUF_TOP)]
        self.live = {}
        self.counter = 0

    def alloc(self, name, shape, dtype, top=False):
        esz = 2 if dtype == BF16 else 4
        nb = (int(np.prod(shape[1:])) * esz + 31) // 32 * 32
        order = reversed(list(enumerate(self.free_list))) if top else enumerate(self.free_list)
        for i, (lo, hi) in order:
            if hi - lo >= nb:
                self.counter += 1
                start = hi - nb if top else lo
                t = self.nc.alloc_sbuf_tensor_at(f"{name}_{self.counter}", list(shape),
                                                 dtype, offset=start)
                self.live[name] = (start, start + nb)
                if top:
                    if start == lo:
                        self.free_list.pop(i)
                    else:
                        self.free_list[i] = (lo, start)
                else:
                    if lo + nb == hi:
                        self.free_list.pop(i)
                    else:
                        self.free_list[i] = (lo + nb, hi)
                return t
        used = sum(v[1] - v[0] for v in self.live.values())
        raise MemoryError(f"SBUF alloc {name} {nb}B failed (used {used}B live="
                          f"{sorted(self.live.items(), key=lambda kv: kv[1][0])})")

    def free(self, *names):
        for name in names:
            self.free_list.append(self.live.pop(name))
        self.free_list.sort()
        merged = []
        for seg in self.free_list:
            if merged and merged[-1][1] >= seg[0]:
                merged[-1] = (merged[-1][0], max(merged[-1][1], seg[1]))
            else:
                merged.append(seg)
        self.free_list = merged


class KB:
    def __init__(self):
        self.nc = bass.Bass("TRN2", target_bir_lowering=False, detect_race_conditions=False)
        self.p = Prog()
        self.al = Alloc(self.nc)
        self.tap_outs = {}
        self.build()

    def decl(self):
        nc, al = self.nc, self.al
        D = lambda n, s, dt=F32: nc.dram_tensor(n, list(s), dt, kind="ExternalInput").ap()
        self.x_in = D("x_fm", [1024, T])
        self.Wq_d = D("Wq", [1024, 1024]); self.Wk_d = D("Wk", [1024, 1024])
        self.WvT_d = D("WvT", [1024, 1024]); self.Wo0_d = D("Wo0", [1024, 1024])
        self.bo0_d = D("bo0", [1, 1024])
        self.wg_d = [D("wg0", [2816, 1024]), D("wg1", [2816, 1024])]
        self.wu_d = [D("wu0", [2816, 1024]), D("wu1", [2816, 1024])]
        self.wd_d = [D("wd0", [1024, 2816]), D("wd1", [1024, 2816])]
        self.WzT_d = D("WzT", [1024, 1024]); self.WhT_d = D("WhT", [1024, 1024])
        self.Wo1T_d = D("Wo1T", [1024, 1024])
        self.bz_d = D("bz_fm", [128, 8]); self.bh_d = D("bh_fm", [128, 8])
        self.bo1_d = D("bo1_fm", [128, 8])
        self.clips_d = D("clips", [1, 4])
        self.m64_d = D("m64row", [128, 1024])
        self.ident_d = D("ident_bf", [128, 128], BF16)
        self.swap64_d = D("swap64_bf", [128, 128], BF16)
        self.u16a_d = D("u16a", [16, 1024], BF16)
        self.u16b_d = D("u16b", [16, 1024], BF16)
        self.cp_d = D("coreparams", [128, 16])
        self.uidx_d = D("uidx", [128, 16], I32)

        self.y_out = nc.dram_tensor("y", [1024, T], F32, kind="ExternalOutput").ap()

        self.x1_save = nc.dram_tensor("x1_save", [1024, T], F32).ap()
        self.h_save = nc.dram_tensor("h_save", [2816, T], F32).ap()
        self.cc_mx_in = [nc.dram_tensor(f"cc_mx_in{i}", [128, 1], F32).ap() for i in range(4)]
        self.cc_mx_out = [nc.dram_tensor(f"cc_mx_out{i}", [128, 1], F32).ap() for i in range(4)]
        self.cc_e_in = nc.dram_tensor("cc_e_in", [64, 512], F32).ap()
        self.cc_e_out = nc.dram_tensor("cc_e_out", [64, 512], F32).ap()
        self.cc_u_in = nc.dram_tensor("cc_u_in", [2048, 1024], F32).ap()
        self.cc_u_out = nc.dram_tensor("cc_u_out", [8192, 1024], F32).ap()
        self.cc_s_in = nc.dram_tensor("cc_s_in", [128, 16], F32).ap()
        self.att_d = nc.dram_tensor("att_d", [16, 16], F32).ap()
        self.cc_s_out = nc.dram_tensor("cc_s_out", [512, 16], F32).ap()

        self.ps = [nc.alloc_psum_tensor(f"ps{i}", [128, 512], F32) for i in range(8)]
        self.psT = self.ps[6][:, :].bitcast(BF16)
        self.psX = self.ps[7]

        A = al.alloc
        self.ident = A("ident", [128, 128], BF16)
        self.swap64 = A("swap64", [128, 128], BF16)
        self.m64 = A("m64", [128, 1024], F32)
        self.u16a = A("u16a", [16, 1024], BF16)
        self.u16b = A("u16b", [16, 1024], BF16)
        self.onesbf = A("onesbf", [128, 128], BF16)
        self.ones64 = A("ones64", [64, 1], F32)
        self.onesrow = A("onesrow", [1, 128], F32)
        self.zeros1 = A("zeros1", [128, 1], F32)
        self.eps_t = A("eps_t", [128, 1], F32)
        self.cp = A("cp", [128, 16], F32)
        self.uidx = A("uidx", [128, 16], I32)
        self.clips = A("clips", [1, 4], F32)
        self.bo_row = A("bo_row", [1, 1024], F32)
        self.bo_b = A("bo_b", [128, 1024], F32)
        self.bz_fm = A("bz_fm", [128, 8], F32)
        self.bh_fm = A("bh_fm", [128, 8], F32)
        self.bo1_fm = A("bo1_fm", [128, 8], F32)
        self.negbz = A("negbz", [128, 8], F32)
        self.bhm05 = A("bhm05", [128, 8], F32)
        self.scal = A("scal", [128, 8], F32)
        self.smrow = A("smrow", [1, 192], F32)
        self.xmaxc = A("xmaxc", [128, 12], F32)
        self.hmall = A("hmall", [128, 96], F32)
        self.csg = A("csg", [128, 22], F32)
        self.csu = A("csu", [128, 22], F32)
        self.rmd = A("rmd", [128, 8], F32)
        self.csd = A("csd", [128, 8], F32)
        self.ssum = A("ssum", [128, 16], F32)
        self.sgth = [A(f"sgth{j}", [128, 16], F32) for j in range(4)]
        self.pfx = A("pfx", [128, 24], F32)
        self.attf = A("attf", [1, 256], F32)
        self.attT = A("attT", [16, 16], BF16)
        self.Ep = A("Ep", [128, 128], F32)
        self.Epk = A("Epk", [64, 768], F32)
        self.En = A("En", [64, 256], F32)
        self.erdbg = A("erdbg", [1, 256], F32)
        self.qsc = A("qsc", [128, 4], F32)
        self.mt = A("mt", [128, T], F32)
        self.rstdt = A("rstdt", [128, T], F32)
        self.tfA = A("tfA", [128, T], F32)
        self.tfB = A("tfB", [128, T], F32)

    def tap(self, name, src_fn, nrow, ncol, dt, after):
        if name not in TAPS:
            return
        out = self.nc.dram_tensor("tap_" + name, [128 * nrow, ncol], dt,
                                  kind="ExternalOutput").ap()
        self.tap_outs[name] = out
        self.p.d("scalar", nrow,
                 lambda e: [e.dma_start(out=out[128 * c:128 * c + 128, :], in_=src_fn(c))
                            for c in range(nrow)], after=after)

    def setup(self):
        p = self.p
        h_cst1 = p.d("sync", 6, lambda e: [
            e.dma_start(out=self.m64[:, :], in_=self.m64_d),
            e.dma_start(out=self.ident[:, :], in_=self.ident_d),
            e.dma_start(out=self.swap64[:, :], in_=self.swap64_d),
            e.dma_start(out=self.u16a[:, :], in_=self.u16a_d),
            e.dma_start(out=self.u16b[:, :], in_=self.u16b_d),
            e.dma_start(out=self.cp[:, :], in_=self.cp_d),
        ])
        h_cst2 = p.d("sync", 6, lambda e: [
            e.dma_start(out=self.uidx[:, :], in_=self.uidx_d),
            e.dma_start(out=self.clips[:, :], in_=self.clips_d),
            e.dma_start(out=self.bo_row[:, :], in_=self.bo0_d),
            e.dma_start(out=self.bz_fm[:, :], in_=self.bz_d),
            e.dma_start(out=self.bh_fm[:, :], in_=self.bh_d),
            e.dma_start(out=self.bo1_fm[:, :], in_=self.bo1_d),
        ])
        self.h_cst = h_cst1 + h_cst2

        def init_c(e):
            e.memset(self.onesbf[:, :], 1.0)
            e.memset(self.ones64[:, :], 1.0)
            e.memset(self.onesrow[:, :], 1.0)
            e.memset(self.zeros1[:, :], 0.0)
            e.memset(self.eps_t[:, :], 1e-5)
            return e.memset(self.hmall[:, :], 0.0)
        self.h_init = p.c("vector", init_c)

        def bias_prep(e):
            e.tensor_scalar(out=self.negbz[:, :], in0=self.bz_fm[:, :], scalar1=-1.0,
                            scalar2=None, op0=ALU.mult)
            return e.tensor_scalar(out=self.bhm05[:, :], in0=self.bh_fm[:, :], scalar1=-0.5,
                                   scalar2=None, op0=ALU.add)
        self.h_bias = p.c("vector", bias_prep, after=[self.h_cst])

        def bob_mm(e):
            e.matmul(self.ps[4][:, :], self.onesrow[0:1, :], self.bo_row[0:1, 0:512],
                     start=True, stop=True)
            return e.matmul(self.ps[5][:, :], self.onesrow[0:1, :], self.bo_row[0:1, 512:1024],
                            start=True, stop=True)
        hm = p.c("tensor", bob_mm, after=[self.h_cst, self.h_init])

        def bob_ev(e):
            e.tensor_copy(out=self.bo_b[:, 0:512], in_=self.ps[4][:, :])
            return e.tensor_copy(out=self.bo_b[:, 512:1024], in_=self.ps[5][:, :])
        self.h_bob = p.c("vector", bob_ev, after=[hm])

    def emit_ln(self, layer, x_src, deps):
        p, al = self.p, self.al
        xn, tfA, tfB = self.xn, self.tfA, self.tfB
        xt = self.xt
        ps = self.ps
        mxslot = 0 if layer == 0 else 2
        cx_col = 0 if layer == 0 else 2
        tbf = [al.alloc(f"tbf{i}", [128, T], BF16) for i in range(2)]
        tsq = al.alloc("tsq", [128, T], F32)

        hld = [None, None]
        hmm = []
        hsplit = []
        for c in range(CH):
            b = c % 2
            hld[b] = p.d("sync", 1,
                         lambda e, c=c, b=b: [e.dma_start(out=xt[b][:, :],
                                                          in_=x_src[128 * c:128 * c + 128, :])],
                         after=list(deps) + ([hmm[c - 2]] if c >= 2 else []))
            h1 = p.c("vector", lambda e, b=b: e.tensor_copy(out=tbf[0][:, :], in_=xt[b][:, :]),
                     after=[hld[b], self.h_init] + ([hmm[c - 1]] if c >= 1 else []))
            h2 = p.c("vector", lambda e, b=b: e.tensor_tensor(out=tbf[1][:, :], in0=xt[b][:, :],
                                                              in1=tbf[0][:, :], op=ALU.subtract),
                     after=[h1])
            hsq = p.c("scalar", lambda e, b=b: e.activation(out=tsq[:, :], in_=xt[b][:, :],
                                                            func=AF.Square,
                                                            bias=self.zeros1[:, 0:1]),
                      after=[hld[b], self.h_init] + ([hsplit[c - 1]] if c >= 1 else []))

            def mm_x(e, c=c):
                last = None
                for part in range(2):
                    for n in range(NT):
                        last = e.matmul(ps[n][:, :], self.onesbf[:, :],
                                        tbf[part][:, 512 * n:512 * n + 512],
                                        start=(c == 0 and part == 0),
                                        stop=(c == CH - 1 and part == 1))
                return last
            hx = p.c("tensor", mm_x, after=[h2])
            h4 = p.c("vector", lambda e: e.tensor_copy(out=tbf[0][:, :], in_=tsq[:, :]),
                     after=[hsq, hx])
            h5 = p.c("vector", lambda e: e.tensor_tensor(out=tbf[1][:, :], in0=tsq[:, :],
                                                         in1=tbf[0][:, :], op=ALU.subtract),
                     after=[h4])
            hsplit.append(h5)

            def mm_sq(e, c=c):
                last = None
                for part in range(2):
                    for n in range(NT):
                        last = e.matmul(ps[4 + n][:, :], self.onesbf[:, :],
                                        tbf[part][:, 512 * n:512 * n + 512],
                                        start=(c == 0 and part == 0),
                                        stop=(c == CH - 1 and part == 1))
                return last
            hmm.append(p.c("tensor", mm_sq, after=[h5]))

        def fin_m(e):
            last = None
            for n in range(NT):
                last = e.tensor_scalar(out=self.mt[:, 512 * n:512 * n + 512], in0=ps[n][:, :],
                                       scalar1=1.0 / 1024.0, scalar2=None, op0=ALU.mult)
            return last
        hm = p.c("vector", fin_m, after=[hmm[-1]])

        def fin_v(e):
            last = None
            for n in range(NT):
                sl = slice(512 * n, 512 * n + 512)
                e.tensor_tensor(out=tfA[:, sl], in0=self.mt[:, sl], in1=self.mt[:, sl],
                                op=ALU.mult)
                last = e.scalar_tensor_tensor(out=tfB[:, sl], in0=ps[4 + n][:, :],
                                              scalar=1.0 / 1024.0, in1=tfA[:, sl],
                                              op0=ALU.mult, op1=ALU.subtract)
            return last
        hv = p.c("vector", fin_v, after=[hm])
        hsd = p.c("scalar", lambda e: e.activation(out=tsq[:, :], in_=tfB[:, :], func=AF.Sqrt,
                                                   bias=self.eps_t[:, 0:1]), after=[hv])
        hrs = p.c("vector", lambda e: e.reciprocal(out=self.rstdt[:, :], in_=tsq[:, :]),
                  after=[hsd])

        hxn = []
        for c in range(CH):
            b = c % 2
            h0 = p.d("sync", 1,
                     lambda e, c=c, b=b: [e.dma_start(out=xt[b][:, :],
                                                      in_=x_src[128 * c:128 * c + 128, :])],
                     after=[hmm[-1]] + ([hxn[c - 2]] if c >= 2 else []))

            def xn_c(e, c=c, b=b):
                e.tensor_tensor(out=tfA[:, :], in0=xt[b][:, :], in1=self.mt[:, :],
                                op=ALU.subtract)
                e.tensor_tensor(out=tfB[:, :], in0=tfA[:, :], in1=self.rstdt[:, :], op=ALU.mult)
                e.tensor_copy(out=xn[:, T * c:T * c + T], in_=tfB[:, :])
                return e.reduce_max(out=self.xmaxc[:, c:c + 1], in_=tfB[:, :], axis=AX.X,
                                    apply_absolute_value=True)
            hxn.append(p.c("vector", xn_c, after=[h0, hrs]))

        def cm(e):
            e.drain()
            return e.reduce_max(out=self.xmaxc[:, 8:9], in_=self.xmaxc[:, 0:8], axis=AX.X,
                                apply_absolute_value=True)
        hcm = p.c("vector", cm, after=hxn)
        hdm = p.d("scalar", 1, lambda e: [e.dma_start(out=self.cc_mx_in[mxslot],
                                                      in_=self.xmaxc[:, 8:9])], after=[hcm])
        hccm = p.cc(lambda e: e.collective_compute(
            "AllReduce", ALU.max, replica_groups=G8,
            ins=[self.cc_mx_in[mxslot].opt()], outs=[self.cc_mx_out[mxslot].opt()]),
            after=[hdm])
        def fin_sx():
            hrb = p.d("gpsimd", 1, lambda e: [e.dma_start(
                out=self.smrow[0:1, 64:192],
                in_=self.cc_mx_out[mxslot].rearrange("p one -> one p"))], after=[hccm])
            hg1 = p.c("vector", lambda e: e.reduce_max(out=self.smrow[0:1, 0:1],
                                                       in_=self.smrow[0:1, 64:192], axis=AX.X),
                      after=[hrb])
            hg2 = p.c("scalar", lambda e: e.activation(out=self.smrow[0:1, 4:5],
                                                       in_=self.clips[0:1, cx_col:cx_col + 1],
                                                       func=AF.Sigmoid,
                                                       bias=self.zeros1[0:1, 0:1]),
                      after=[self.h_cst, hg1])

            def sx_fin(e):
                e.drain()
                e.tensor_scalar(out=self.smrow[0:1, 5:6], in0=self.smrow[0:1, 4:5],
                                scalar1=self.smrow[0:1, 0:1], scalar2=1.0 / 127.0,
                                op0=ALU.mult, op1=ALU.mult)
                e.drain()
                return e.reciprocal(out=self.smrow[0:1, 6:7], in_=self.smrow[0:1, 5:6])
            hg3 = p.c("vector", sx_fin, after=[hg2])

            def sx_bc(e):
                e.matmul(self.psX[:, 0:1], self.onesrow[0:1, :], self.smrow[0:1, 5:6],
                         start=True, stop=True)
                return e.matmul(self.psX[:, 1:2], self.onesrow[0:1, :], self.smrow[0:1, 6:7],
                                start=True, stop=True)
            hg4 = p.c("tensor", sx_bc, after=[hg3])

            def sx_ev(e):
                e.tensor_copy(out=self.scal[:, 0:1], in_=self.psX[:, 0:1])
                e.tensor_copy(out=self.scal[:, 1:2], in_=self.psX[:, 1:2])
                return e.drain()
            return p.c("vector", sx_ev, after=[hg4])

        al.free("tbf0", "tbf1", "tsq")
        return {"xn": hxn[-1], "stats": hrs, "sx": None, "sx_fin": fin_sx,
                "mm_last": hmm[-1]}

    def emit_quant(self, x_src, deps):
        p = self.p
        xq, xt, tfA, tfB = self.xq, self.xt, self.tfA, self.tfB
        hs = []
        for c in range(CH):
            b = c % 2
            hld = p.d("sync", 1,
                      lambda e, c=c, b=b: [e.dma_start(out=xt[b][:, :],
                                                       in_=x_src[128 * c:128 * c + 128, :])],
                      after=list(deps) + ([hs[c - 2]] if c >= 2 else []))

            def q_c(e, c=c, b=b):
                e.tensor_tensor(out=tfA[:, :], in0=xt[b][:, :], in1=self.mt[:, :],
                                op=ALU.subtract)
                e.tensor_tensor(out=tfB[:, :], in0=tfA[:, :], in1=self.rstdt[:, :], op=ALU.mult)
                e.tensor_scalar(out=tfA[:, :], in0=tfB[:, :], scalar1=self.scal[:, 1:2],
                                scalar2=MAGIC, op0=ALU.mult, op1=ALU.add)
                e.tensor_scalar(out=tfB[:, :], in0=tfA[:, :], scalar1=MAGIC, scalar2=127.0,
                                op0=ALU.subtract, op1=ALU.min)
                return e.tensor_scalar(out=xq[:, T * c:T * c + T], in0=tfB[:, :],
                                       scalar1=-128.0, scalar2=None, op0=ALU.max)
            hs.append(p.c("vector", q_c, after=[hld] + list(deps)))
        return hs[-1]

    def emit_attn0(self, hln, post_energy_hook=None):
        p, al = self.p, self.al
        ps, psT = self.ps, self.psT
        xn = self.xn

        Wqb = al.alloc("Wqb", [128, 8 * 1024], BF16)
        Wkb = al.alloc("Wkb", [128, 8 * 1024], BF16)
        h_wq = p.d("gpsimd", 1, lambda e: [e.dma_start(
            out=Wqb[:, :].rearrange("p (c n) -> p c n", c=8),
            in_=self.Wq_d.rearrange("(c p) n -> p c n", p=128))],
            after=[hln["stats"], hln["mm_last"]])
        h_wk = p.d("gpsimd", 1, lambda e: [e.dma_start(
            out=Wkb[:, :].rearrange("p (c n) -> p c n", c=8),
            in_=self.Wk_d.rearrange("(c p) n -> p c n", p=128))],
            after=[hln["stats"], hln["mm_last"]])
        qT = al.alloc("qT", [128, 16 * 1024], BF16)
        kT = al.alloc("kT", [128, 16 * 1024], BF16)

        def proj_tokmajor(dst, Wb, h_w, hprev_ev=None):
            ev = []
            for tt in range(16):
                b0, b1 = (0, 1) if tt % 2 == 0 else (2, 3)

                def mm(e, tt=tt, b0=b0, b1=b1, Wb=Wb):
                    last = None
                    for k in range(CH):
                        e.matmul(ps[b0][:, :],
                                 xn[:, T * k + 128 * tt:T * k + 128 * tt + 128],
                                 Wb[:, 1024 * k:1024 * k + 512],
                                 start=(k == 0), stop=(k == CH - 1))
                        last = e.matmul(ps[b1][:, :],
                                        xn[:, T * k + 128 * tt:T * k + 128 * tt + 128],
                                        Wb[:, 1024 * k + 512:1024 * k + 1024],
                                        start=(k == 0), stop=(k == CH - 1))
                    return last
                bank_wait = []
                if tt >= 2:
                    bank_wait = [ev[tt - 2]]
                elif hprev_ev is not None:
                    bank_wait = [hprev_ev[tt - 2]]
                hmmx = p.c("tensor", mm, after=[h_w, hln["xn"]] + bank_wait)

                def evf(e, tt=tt, b0=b0, b1=b1, dst=dst):
                    e.tensor_copy(out=dst[:, 1024 * tt:1024 * tt + 512], in_=ps[b0][:, :])
                    return e.tensor_copy(out=dst[:, 1024 * tt + 512:1024 * tt + 1024],
                                         in_=ps[b1][:, :])
                ev.append(p.c("vector", evf, after=[hmmx]))
            return ev

        if getattr(self, "mtprobe", None):
            pass
        hq_ev = proj_tokmajor(qT, Wqb, h_wq)
        if getattr(self, "mtprobe", None):
            pass
        hk_ev = proj_tokmajor(kT, Wkb, h_wk, hprev_ev=hq_ev)
        if getattr(self, "mtprobe", None):
            pass

        hg_ev = []
        for qc in range(CH):
            b0, b1 = (0, 1) if qc % 2 == 0 else (2, 3)

            def mm_g(e, qc=qc, b0=b0, b1=b1):
                last = None
                for tt in range(16):
                    e.matmul(ps[b0][:, :],
                             qT[:, 1024 * tt + 128 * qc:1024 * tt + 128 * qc + 128],
                             kT[:, 1024 * tt:1024 * tt + 512],
                             start=(tt == 0), stop=(tt == 15))
                    last = e.matmul(ps[b1][:, :],
                                    qT[:, 1024 * tt + 128 * qc:1024 * tt + 128 * qc + 128],
                                    kT[:, 1024 * tt + 512:1024 * tt + 1024],
                                    start=(tt == 0), stop=(tt == 15))
                return last
            bank_wait = [hg_ev[qc - 2]] if qc >= 2 else [hk_ev[-2], hk_ev[-1]]
            hmg = p.c("tensor", mm_g, after=bank_wait)

            def ev_g(e, qc=qc, b0=b0, b1=b1):
                e.tensor_tensor(out=self.tfA[:, 0:512], in0=ps[b0][:, :],
                                in1=self.m64[:, 0:512], op=ALU.mult)
                e.tensor_tensor(out=self.tfA[:, 512:1024], in0=ps[b1][:, :],
                                in1=self.m64[:, 512:1024], op=ALU.mult)
                return e.reduce_sum(out=self.Ep[:, 16 * qc:16 * qc + 16],
                                    in_=self.tfA[:, 0:1024].rearrange("p (a b) -> p a b", b=64),
                                    axis=AX.X)
            hg_ev.append(p.c("vector", ev_g, after=[hmg, self.h_cst]))

        def pack_e(e):
            for qc in range(CH):
                for par in range(2):
                    a = 2 * qc + par
                    e.tensor_copy(out=self.Epk[:, 16 * a:16 * a + 16],
                                  in_=self.Ep[64 * par:64 * par + 64, 16 * qc:16 * qc + 16])
            e.tensor_scalar(out=self.Epk[:, 256:512], in0=self.Epk[:, 0:256],
                            scalar1=self.cp[0:64, 0:1], scalar2=None, op0=ALU.mult)
            return e.tensor_scalar(out=self.Epk[:, 512:768], in0=self.Epk[:, 0:256],
                                   scalar1=self.cp[0:64, 1:2], scalar2=None, op0=ALU.mult)
        hpk = p.c("vector", pack_e, after=hg_ev)
        hde = p.d("scalar", 1, lambda e: [e.dma_start(out=self.cc_e_in,
                                                      in_=self.Epk[0:64, 256:768])],
                  after=[hpk])
        hce = p.cc(lambda e: e.collective_compute(
            "AllReduce", ALU.add, replica_groups=G8,
            ins=[self.cc_e_in.opt()], outs=[self.cc_e_out.opt()]), after=[hde])
        hre = p.d("gpsimd", 1, lambda e: [e.dma_start(out=self.Epk[0:64, 256:768],
                                                      in_=self.cc_e_out)], after=[hce])
        hln["sx"] = hln["sx_fin"]()

        def en_fin(e):
            e.tensor_scalar(out=self.En[:, :], in0=self.Epk[0:64, 256:512],
                            scalar1=self.cp[0:64, 0:1], scalar2=None, op0=ALU.mult)
            e.drain()
            return e.scalar_tensor_tensor(out=self.En[:, :], in0=self.Epk[0:64, 512:768],
                                          scalar=self.cp[0:64, 1:2], in1=self.En[:, :],
                                          op0=ALU.mult, op1=ALU.add)
        hen = p.c("vector", en_fin, after=[hre])
        if getattr(self, "mtprobe", None):
            pass

        hrow = p.c("tensor", lambda e: e.matmul(self.psX[0:1, 0:256], self.ones64[0:64, 0:1],
                                                self.En[0:64, :], start=True, stop=True),
                   after=[hen, hg_ev[-1], hln["sx"]])
        def rowc(e):
            e.tensor_copy(out=self.erdbg[0:1, :], in_=self.psX[0:1, 0:256])
            return e.drain()
        hrowc = p.c("vector", rowc, after=[hrow])

        def sm_mx(e):
            for a in range(16):
                e.reduce_max(out=self.smrow[0:1, 96 + a:97 + a],
                             in_=self.erdbg[0:1, 16 * a:16 * a + 16], axis=AX.X)
            e.drain()
            last = None
            for a in range(16):
                last = e.tensor_scalar(out=self.attf[0:1, 16 * a:16 * a + 16],
                                       in0=self.erdbg[0:1, 16 * a:16 * a + 16],
                                       scalar1=self.smrow[0:1, 96 + a:97 + a],
                                       scalar2=None, op0=ALU.subtract)
            return last
        hmx = p.c("vector", sm_mx, after=[hrowc])
        hex_ = p.c("scalar", lambda e: e.activation(
            out=self.attf[0:1, :], in_=self.attf[0:1, :], func=AF.Exp, scale=0.125,
            bias=self.zeros1[0:1, 0:1]), after=[hmx])

        def sm_nrm(e):
            e.reduce_sum(out=self.smrow[0:1, 128:144],
                         in_=self.attf[0:1, :].rearrange("p (a b) -> p a b", b=16), axis=AX.X)
            e.drain()
            e.reciprocal(out=self.smrow[0:1, 144:160], in_=self.smrow[0:1, 128:144])
            e.drain()
            last = None
            for a in range(16):
                last = e.tensor_scalar(out=self.attf[0:1, 16 * a:16 * a + 16],
                                       in0=self.attf[0:1, 16 * a:16 * a + 16],
                                       scalar1=self.smrow[0:1, 144 + a:145 + a],
                                       scalar2=None, op0=ALU.mult)
            return last
        hsm = p.c("vector", sm_nrm, after=[hex_])
        if DEV_STOP == "att0":
            hfin = p.d("gpsimd", 9, lambda e: [
                e.dma_start(out=self.y_out[0:64, 0:768], in_=self.Epk[0:64, :]),
                e.dma_start(out=self.y_out[64:128, 0:256], in_=self.En[0:64, :]),
                e.dma_start(out=self.y_out[128:129, 0:256], in_=self.attf[0:1, :]),
                e.dma_start(out=self.y_out[129:130, 0:256], in_=self.erdbg[0:1, :]),
                e.dma_start(out=self.y_out[130:131, 0:192], in_=self.smrow[0:1, :]),
                e.dma_start(out=self.y_out[256:384, 0:1024], in_=qT[:, 0:1024]),
                e.dma_start(out=self.y_out[384:512, 0:1024], in_=kT[:, 0:1024]),
                e.dma_start(out=self.y_out[512:640, 0:1024], in_=Wqb[:, 0:1024]),
                e.dma_start(out=self.y_out[640:768, 0:128], in_=self.Ep[:, :]),
            ], after=[hsm, hen])
            p.c("sync", lambda e: e.nop(), after=[hfin])
            self.p.emit(self.nc)
            self.dev_stopped = True
            return {}

        hat0 = p.d("gpsimd", 1, lambda e: [e.dma_start(
            out=self.att_d.rearrange("a b -> (a b)").rearrange("(one f) -> one f", one=1),
            in_=self.attf[0:1, :])], after=[hsm])
        hat = p.d("gpsimd", 1, lambda e: [e.dma_start(
            out=self.attT[:, :], in_=self.att_d)], after=[hat0])

        al.free("qT", "kT", "Wqb", "Wkb")
        WvTb = al.alloc("WvTb", [128, 8 * 1024], BF16)
        atxp = al.alloc("atxp", [16, 1024], BF16)
        Aexp = al.alloc("Aexp", [128, 8 * 1024], BF16)
        h_wv = p.d("gpsimd", 1, lambda e: [e.dma_start(
            out=WvTb[:, :].rearrange("p (c n) -> p c n", c=8),
            in_=self.WvT_d.rearrange("(c p) n -> p c n", p=128))], after=[hg_ev[-1]])
        if post_energy_hook:
            post_energy_hook()

        def mm_ax(e):
            e.matmul(ps[4][0:16, :], self.attT[:, :], self.u16a[:, 0:512], start=True, stop=True)
            return e.matmul(ps[5][0:16, :], self.attT[:, :], self.u16a[:, 512:1024],
                            start=True, stop=True)
        hax = p.c("tensor", mm_ax, after=[hat, self.h_cst, self.h_bob])

        def ev_ax(e):
            e.tensor_copy(out=atxp[:, 0:512], in_=ps[4][0:16, :])
            return e.tensor_copy(out=atxp[:, 512:1024], in_=ps[5][0:16, :])
        hax2 = p.c("vector", ev_ax, after=[hax, hg_ev[-1]])

        hAe = []
        for vc in range(CH):
            b0, b1 = (0, 1) if vc % 2 == 0 else (2, 3)

            def mm_ae(e, vc=vc, b0=b0, b1=b1):
                e.matmul(ps[b0][:, :], self.u16b[:, 128 * vc:128 * vc + 128], atxp[:, 0:512],
                         start=True, stop=True)
                return e.matmul(ps[b1][:, :], self.u16b[:, 128 * vc:128 * vc + 128],
                                atxp[:, 512:1024], start=True, stop=True)
            hma = p.c("tensor", mm_ae, after=[hax2] + ([hAe[vc - 2]] if vc >= 2 else []))

            def ev_ae(e, vc=vc, b0=b0, b1=b1):
                e.tensor_tensor(out=Aexp[:, 1024 * vc:1024 * vc + 512], in0=ps[b0][:, :],
                                in1=self.m64[:, 0:512], op=ALU.mult)
                return e.tensor_tensor(out=Aexp[:, 1024 * vc + 512:1024 * vc + 1024],
                                       in0=ps[b1][:, :], in1=self.m64[:, 512:1024], op=ALU.mult)
            hAe.append(p.c("vector", ev_ae, after=[hma]))

        Wvmix = al.alloc("Wvmix", [128, 8 * 1024], BF16)
        hWm = []
        for xc in range(CH):
            b0, b1 = (0, 1) if xc % 2 == 0 else (2, 3)

            def mm_wm(e, xc=xc, b0=b0, b1=b1):
                last = None
                for k in range(CH):
                    e.matmul(ps[b0][:, :],
                             WvTb[:, 1024 * k + 128 * xc:1024 * k + 128 * xc + 128],
                             Aexp[:, 1024 * k:1024 * k + 512],
                             start=(k == 0), stop=(k == CH - 1))
                    last = e.matmul(ps[b1][:, :],
                                    WvTb[:, 1024 * k + 128 * xc:1024 * k + 128 * xc + 128],
                                    Aexp[:, 1024 * k + 512:1024 * k + 1024],
                                    start=(k == 0), stop=(k == CH - 1))
                return last
            hmw = p.c("tensor", mm_wm, after=[hAe[-1], h_wv] + ([hWm[xc - 2]] if xc >= 2 else []))

            def ev_wm(e, xc=xc, b0=b0, b1=b1):
                e.tensor_copy(out=Wvmix[:, 1024 * xc:1024 * xc + 512], in_=ps[b0][:, :])
                return e.tensor_copy(out=Wvmix[:, 1024 * xc + 512:1024 * xc + 1024],
                                     in_=ps[b1][:, :])
            hWm.append(p.c("vector", ev_wm, after=[hmw]))

        if getattr(self, "mtprobe", None):
            self.mtprobe(0, [hWm[-1]])
        al.free("Aexp", "WvTb", "atxp")
        vmixF = al.alloc("vmixF", [128, 8 * T], BF16)
        hvF = []
        for mc in range(CH):
            def mm_vf(e, mc=mc):
                last = None
                for k in range(CH):
                    for n in range(NT):
                        last = e.matmul(ps[n][:, :],
                                        Wvmix[:, 1024 * k + 128 * mc:1024 * k + 128 * mc + 128],
                                        xn[:, T * k + 512 * n:T * k + 512 * n + 512],
                                        start=(k == 0), stop=(k == CH - 1))
                return last
            hmv = p.c("tensor", mm_vf, after=[hWm[-1]] + ([hvF[mc - 1]] if mc >= 1 else []))

            def ev_vf(e, mc=mc):
                last = None
                for n in range(NT):
                    last = e.tensor_copy(out=vmixF[:, T * mc + 512 * n:T * mc + 512 * n + 512],
                                         in_=ps[n][:, :])
                return last
            hvF.append(p.c("vector", ev_vf, after=[hmv]))
        self.tap("vmixF", lambda c: vmixF[:, T * c:T * c + T], 8, T, BF16, [hvF[-1]])

        al.free("Wvmix")
        Wob = al.alloc("Wob", [128, 8 * 1024], BF16)
        h_wo = p.d("gpsimd", 1, lambda e: [e.dma_start(
            out=Wob[:, :].rearrange("p (c n) -> p c n", c=8),
            in_=self.Wo0_d.rearrange("(c p) n -> p c n", p=128))], after=[hvF[-1]])
        vmixr = [al.alloc(f"vmixr{i}", [128, 8 * 128], BF16) for i in range(2)]
        vstg = [al.alloc(f"vstg{i}", [128, 8 * 256], BF16) for i in range(2)]
        ubuf = [al.alloc(f"ubuf{i}", [128, 1024], F32) for i in range(2)]
        hvr = [None, None]
        hvp = [None, None]
        hvpe = [None, None]
        humm = [None, None]
        huev = [None, None]
        hu_dma = []
        for a in range(16):
            b = a % 2
            r0 = 64 * (a % 2)

            def cp_vr(e, a=a, b=b, r0=r0):
                last = None
                base = T * (a // 2)
                for j in range(16):
                    last = e.tensor_copy(
                        out=vstg[b][r0:r0 + 64, 256 * (j // 2) + 128 * (j % 2):
                                    256 * (j // 2) + 128 * (j % 2) + 128],
                        in_=vmixF[r0:r0 + 64, base + j:base + T:16])
                return last
            hvr[b] = p.c("vector", cp_vr, after=[hvF[a // 2]] + ([hvp[b]] if hvp[b] else []))

            def perm_vr(e, a=a, b=b, r0=r0):
                last = None
                b4 = 4 if b == 0 else 5
                for ci in range(CH):
                    for js in range(2):
                        tgt = 64 * js
                        sel = self.ident if tgt == r0 else self.swap64
                        last = e.matmul(
                            ps[b4][:, 0:128].bitcast(BF16)[:, 0:128] if False else ps[b4][:, 0:128],
                            sel[r0:r0 + 64, :],
                            vstg[b][r0:r0 + 64, 256 * ci + 128 * js:256 * ci + 128 * js + 128],
                            start=(js == 0), stop=(js == 1))
                return last

            def perm_and_evac(e, a=a, b=b, r0=r0):
                return None
            hplast = [hvr[b]]
            b4 = 4 if b == 0 else 5
            def perm_all(e, a=a, b=b, r0=r0, b4=b4):
                last = None
                for ci in range(CH):
                    for js in range(2):
                        tgt = 64 * js
                        sel = self.ident if tgt == r0 else self.swap64
                        last = e.matmul(
                            ps[b4][:, 128 * (ci % 4):128 * (ci % 4) + 128],
                            sel[r0:r0 + 64, :],
                            vstg[b][r0:r0 + 64, 256 * ci + 128 * js:256 * ci + 128 * js + 128],
                            start=(js == 0), stop=(js == 1))
                return last
            def perm_half(e, a=a, b=b, r0=r0, b4=b4, half=0):
                last = None
                for ci in range(4 * half, 4 * half + 4):
                    for js in range(2):
                        tgt = 64 * js
                        sel = self.ident if tgt == r0 else self.swap64
                        last = e.matmul(
                            ps[b4][:, 128 * (ci % 4):128 * (ci % 4) + 128],
                            sel[r0:r0 + 64, :],
                            vstg[b][r0:r0 + 64, 256 * ci + 128 * js:256 * ci + 128 * js + 128],
                            start=(js == 0), stop=(js == 1))
                return last
            def evac_half(e, b=b, b4=b4, half=0):
                last = None
                for ci in range(4 * half, 4 * half + 4):
                    last = e.tensor_copy(out=vmixr[b][:, 128 * ci:128 * ci + 128],
                                         in_=ps[b4][:, 128 * (ci % 4):128 * (ci % 4) + 128])
                return last
            import functools
            hp0 = p.c("tensor", functools.partial(perm_half, half=0),
                      after=[hvr[b]] + ([hvpe[b]] if hvpe[b] else []))
            he0 = p.c("vector", functools.partial(evac_half, half=0), after=[hp0])
            hp1 = p.c("tensor", functools.partial(perm_half, half=1), after=[he0])
            he1 = p.c("vector", functools.partial(evac_half, half=1), after=[hp1])
            hvp[b] = he1
            hvpe[b] = he1

            def mm_u(e, a=a, b=b):
                last = None
                b0, b1 = (0, 1) if b == 0 else (2, 3)
                for ci in range(CH):
                    e.matmul(ps[b0][:, :], vmixr[b][:, 128 * ci:128 * ci + 128],
                             Wob[:, 1024 * ci:1024 * ci + 512],
                             start=(ci == 0), stop=(ci == CH - 1))
                    last = e.matmul(ps[b1][:, :], vmixr[b][:, 128 * ci:128 * ci + 128],
                                    Wob[:, 1024 * ci + 512:1024 * ci + 1024],
                                    start=(ci == 0), stop=(ci == CH - 1))
                return last
            humm[b] = p.c("tensor", mm_u, after=[he1, h_wo] + ([huev[b]] if huev[b] else []))

            def ev_u(e, a=a, b=b):
                b0, b1 = (0, 1) if b == 0 else (2, 3)
                e.tensor_tensor(out=ubuf[b][:, 0:512], in0=ps[b0][:, :],
                                in1=self.bo_b[:, 0:512], op=ALU.add)
                return e.tensor_tensor(out=ubuf[b][:, 512:1024], in0=ps[b1][:, :],
                                       in1=self.bo_b[:, 512:1024], op=ALU.add)
            huev[b] = p.c("vector", ev_u, after=[humm[b], self.h_bob] +
                          ([hu_dma[-2]] if len(hu_dma) >= 2 else []))
            hu_dma.append(p.d("scalar", 1, lambda e, a=a, b=b: [
                e.dma_start(out=self.cc_u_in[128 * a:128 * a + 128, :], in_=ubuf[b][:, :])],
                after=[huev[b]]))

        if getattr(self, "mtprobe", None):
            self.mtprobe(1, [hvF[-1]])
            self.mtprobe(2, [huev[0], huev[1]])
        hag = p.cc(lambda e: e.collective_compute(
            "AllGather", ALU.bypass, replica_groups=G4,
            ins=[self.cc_u_in.opt()], outs=[self.cc_u_out.opt()]), after=hu_dma)

        return {"ag": hag, "vf": hvF[-1], "last_reads": [humm[0], humm[1], hu_dma[-1],
                                                         hu_dma[-2], hvr[0], hvr[1]]}

    def emit_consume_round(self, rr, hag, rounds, extra_after):
        p = self.p
        r, m = rr // 4, rr % 4
        b = rr % 2
        ust, ustb = self.ust, self.ustb
        prev2 = rounds[rr - 2] if rr >= 2 else None
        prev1 = rounds[rr - 1] if rr >= 1 else None
        hgat = p.d("gpsimd", 1, lambda e, r=r, m=m, b=b: [
            e.indirect_dma_start(
                out=ust[b][:, :], out_offset=None,
                in_=self.cc_u_out,
                in_offset=bass.IndirectOffsetOnAxis(
                    ap=self.uidx[:, 4 * r + m:4 * r + m + 1], axis=0))],
            after=[hag] + list(extra_after) + ([prev2["cst"]] if prev2 else []))
        hcst = p.c("scalar", lambda e, b=b: e.copy(out=ustb[b][:, :], in_=ust[b][:, :]),
                   after=[hgat] + ([prev2["tp"]] if prev2 else []))

        def tp(e, b=b):
            last = None
            for c in range(CH):
                last = e.transpose(self.psT[:, 128 * c:128 * c + 128],
                                   ustb[b][:, 128 * c:128 * c + 128], self.ident[:, :])
            return last
        htp = p.c("tensor", tp, after=[hcst] + ([prev1["ev"]] if prev1 else []) + extra_after)

        def ev(e, r=r, m=m):
            last = None
            for c in range(CH):
                last = e.copy(out=self.s_fm[:, T * c + 512 * m + 128 * r:
                                            T * c + 512 * m + 128 * r + 128],
                              in_=self.psT[:, 128 * c:128 * c + 128])
            return last
        hev = p.c("scalar", ev, after=[htp] + list(extra_after))
        return {"gat": hgat, "cst": hcst, "tp": htp, "ev": hev}

    def emit_ffn(self, layer, s_get, s_ready, x_src, x1_dst, deps, hook=None,
                 mid_free=None):
        p, al = self.p, self.al
        ps, psT = self.ps, self.psT
        xq = self.xq
        cd_col = 1 if layer == 0 else 3
        mxslot = 1 if layer == 0 else 3
        wg_d, wu_d, wd_d = self.wg_d[layer], self.wu_d[layer], self.wd_d[layer]

        wnat = [al.alloc(f"wnat{i}", [128, 1024], F32) for i in range(2)]
        unat = [al.alloc(f"unat{i}", [128, 1024], F32) for i in range(2)]
        wqn = [al.alloc(f"wqn{i}", [128, 1024], BF16) for i in range(2)]
        uqn = [al.alloc(f"uqn{i}", [128, 1024], BF16) for i in range(2)]
        wgT = [al.alloc(f"wgT{i}", [128, 1024], BF16) for i in range(2)]
        wuT = [al.alloc(f"wuT{i}", [128, 1024], BF16) for i in range(2)]
        qsc = self.qsc
        th1 = al.alloc("th1", [128, 512], F32, top=True)
        th2 = al.alloc("th2", [128, 512], F32, top=True)
        th3 = [al.alloc(f"th3{i}", [128, 512], F32, top=True) for i in range(2)]

        hz = p.c("vector", lambda e: e.memset(self.hmall[:, :], 0.0), after=list(deps))

        hqg = [None, None]
        hqu = [None, None]
        hwev = [None, None]
        hprod = {}
        hdh = {}
        hge_hist = {}
        last_hpd = None
        pending = None
        for i in range(IC):
            b = i % 2
            hlg = p.d("sync", 1, lambda e, i=i, b=b: [
                e.dma_start(out=wnat[b][:, :], in_=wg_d[128 * i:128 * i + 128, :])],
                after=list(deps) + ([hqg[b]] if hqg[b] else []))
            hlu = p.d("sync", 1, lambda e, i=i, b=b: [
                e.dma_start(out=unat[b][:, :], in_=wu_d[128 * i:128 * i + 128, :])],
                after=list(deps) + ([hqu[b]] if hqu[b] else []))

            def qg(e, i=i, b=b):
                e.reduce_max(out=qsc[:, 0:1], in_=wnat[b][:, :], axis=AX.X,
                             apply_absolute_value=True)
                e.drain()
                e.reciprocal(out=qsc[:, 1:2], in_=qsc[:, 0:1])
                e.drain()
                e.tensor_scalar(out=self.tfA[:, 0:1024], in0=wnat[b][:, :],
                                scalar1=qsc[:, 1:2], scalar2=127.0, op0=ALU.mult, op1=ALU.mult)
                e.tensor_scalar(out=wqn[b][:, :], in0=self.tfA[:, 0:1024], scalar1=MAGIC,
                                scalar2=MAGIC, op0=ALU.add, op1=ALU.subtract)
                return e.tensor_scalar(out=self.csg[:, i:i + 1], in0=qsc[:, 0:1],
                                       scalar1=self.scal[:, 0:1], scalar2=1.0 / 127.0,
                                       op0=ALU.mult, op1=ALU.mult)
            hq1 = p.c("vector", qg, after=[hlg, self.sx_h[layer]] +
                      ([hwev[b]] if hwev[b] else []) +
                      ([hge_hist[i - 2]] if i >= 2 else []))
            hqg[b] = hq1

            def qu(e, i=i, b=b):
                e.reduce_max(out=qsc[:, 0:1], in_=unat[b][:, :], axis=AX.X,
                             apply_absolute_value=True)
                e.drain()
                e.reciprocal(out=qsc[:, 1:2], in_=qsc[:, 0:1])
                e.drain()
                e.tensor_scalar(out=self.tfA[:, 0:1024], in0=unat[b][:, :],
                                scalar1=qsc[:, 1:2], scalar2=127.0, op0=ALU.mult, op1=ALU.mult)
                e.tensor_scalar(out=uqn[b][:, :], in0=self.tfA[:, 0:1024], scalar1=MAGIC,
                                scalar2=MAGIC, op0=ALU.add, op1=ALU.subtract)
                return e.tensor_scalar(out=self.csu[:, i:i + 1], in0=qsc[:, 0:1],
                                       scalar1=self.scal[:, 0:1], scalar2=1.0 / 127.0,
                                       op0=ALU.mult, op1=ALU.mult)
            hq2 = p.c("vector", qu, after=[hlu, hq1])
            hqu[b] = hq2

            def tw(e, b=b):
                last = None
                for k in range(CH):
                    last = e.transpose(psT[:, 128 * k:128 * k + 128],
                                       wqn[b][:, 128 * k:128 * k + 128], self.ident[:, :])
                return last
            htw = p.c("tensor", tw, after=[hq1] + ([hwev[b]] if hwev[b] else []) +
                      ([pending] if pending else []))
            pending = None
            hge = p.c("scalar", lambda e, b=b: e.copy(out=wgT[b][:, :], in_=psT[:, :]),
                      after=[htw])
            hge_hist[i] = hge

            def tu(e, b=b):
                last = None
                for k in range(CH):
                    last = e.transpose(psT[:, 128 * k:128 * k + 128],
                                       uqn[b][:, 128 * k:128 * k + 128], self.ident[:, :])
                return last
            htu = p.c("tensor", tu, after=[hq2, hge])
            hue = p.c("scalar", lambda e, b=b: e.copy(out=wuT[b][:, :], in_=psT[:, :]),
                      after=[htu])
            hwev[b] = hue

            for n in range(NT):
                bg, bu = (0, 1) if n % 2 == 0 else (2, 3)
                prev_pair = hprod.get((i, n - 2)) or (hprod.get((i - 1, n + 2)) if n < 2 else None)

                def mm_gu(e, i=i, b=b, n=n, bg=bg, bu=bu):
                    last = None
                    for k in range(CH):
                        e.matmul(ps[bg][:, :], wgT[b][:, 128 * k:128 * k + 128],
                                 xq[:, T * k + 512 * n:T * k + 512 * n + 512],
                                 start=(k == 0), stop=(k == CH - 1))
                    for k in range(CH):
                        last = e.matmul(ps[bu][:, :], wuT[b][:, 128 * k:128 * k + 128],
                                        xq[:, T * k + 512 * n:T * k + 512 * n + 512],
                                        start=(k == 0), stop=(k == CH - 1))
                    return last
                hmm = p.c("tensor", mm_gu, after=[hge, hue, self.xq_h[layer]] +
                          ([prev_pair] if prev_pair else []))
                hmm_last = hmm
                hsl = p.c("scalar", lambda e, i=i, bg=bg: e.activation(
                    out=th1[:, :], in_=ps[bg][:, :], func=AF.Silu,
                    scale=self.csg[:, i:i + 1], bias=self.zeros1[:, 0:1]),
                    after=[hmm, hz] + ([last_hpd] if last_hpd else []))

                def prod(e, i=i, n=n, bu=bu):
                    e.tensor_scalar(out=th2[:, :], in0=ps[bu][:, :],
                                    scalar1=self.csu[:, i:i + 1], scalar2=None, op0=ALU.mult)
                    e.tensor_tensor(out=th3[n % 2][:, :], in0=th1[:, :], in1=th2[:, :],
                                    op=ALU.mult)
                    return e.reduce_max(out=self.hmall[:, 4 * i + n:4 * i + n + 1],
                                        in_=th3[n % 2][:, :], axis=AX.X,
                                        apply_absolute_value=True)
                hpd = p.c("vector", prod, after=[hsl] +
                          ([hdh[(i, n - 2)]] if (i, n - 2) in hdh else
                           ([hdh[(i - 1, n + 2)]] if (i - 1, n + 2) in hdh else [])))
                last_hpd = hpd
                hprod[(i, n)] = hpd
                hdh[(i, n)] = p.d("scalar", 1, lambda e, i=i, n=n: [
                    e.dma_start(out=self.h_save[128 * i:128 * i + 128, 512 * n:512 * n + 512],
                                in_=th3[n % 2][:, :])], after=[hpd])
            if hook:
                pending = hook(i, hwev[b])

        al.free("wnat0", "wnat1", "unat0", "unat1", "wqn0", "wqn1", "uqn0", "uqn1",
                "wgT0", "wgT1", "wuT0", "wuT1")
        gateup_done = [hprod[(IC - 1, n)] for n in range(NT)]
        war = [hqg[0], hqg[1], hqu[0], hqu[1], hwev[0], hwev[1], hmm_last]
        if mid_free:
            war += list(mid_free() or [])

        wdnat = [al.alloc(f"wdnat{i}", [128, 2816], F32, top=True) for i in range(2)]
        wdq = al.alloc("wdq", [128, 2816], BF16, top=True)
        wdT = al.alloc("wdT", [128, 22 * 1024], BF16, top=True)
        hwd = []
        hdq = [None, None]
        first_td_extra = [pending] if pending else []
        for c in range(CH):
            b = c % 2
            hld = p.d("sync", 1, lambda e, c=c, b=b: [
                e.dma_start(out=wdnat[b][:, :], in_=wd_d[128 * c:128 * c + 128, :])],
                after=list(deps) + war + ([hdq[b]] if hdq[b] else []))

            def qd(e, c=c, b=b):
                e.reduce_max(out=self.rmd[:, c:c + 1], in_=wdnat[b][:, :], axis=AX.X,
                             apply_absolute_value=True)
                e.drain()
                e.reciprocal(out=qsc[:, 3:4], in_=self.rmd[:, c:c + 1])
                e.drain()
                for half in range(2):
                    sl = slice(1408 * half, 1408 * half + 1408)
                    e.tensor_scalar(out=self.tfA[:, 0:1408], in0=wdnat[b][:, sl],
                                    scalar1=qsc[:, 3:4], scalar2=127.0,
                                    op0=ALU.mult, op1=ALU.mult)
                    e.tensor_scalar(out=wdq[:, sl], in0=self.tfA[:, 0:1408], scalar1=MAGIC,
                                    scalar2=MAGIC, op0=ALU.add, op1=ALU.subtract)
                return e.engine_nop()
            hq = p.c("vector", qd, after=[hld] + ([hwd[-1]] if hwd else []))
            hdq[b] = hq
            hlast = hq
            for g in range(3):
                ks = list(range(8 * g, min(8 * g + 8, IC)))

                def tdg(e, ks=tuple(ks)):
                    last = None
                    for j, k in enumerate(ks):
                        last = e.transpose(psT[:, 128 * j:128 * j + 128],
                                           wdq[:, 128 * k:128 * k + 128], self.ident[:, :])
                    return last
                htd = p.c("tensor", tdg, after=[hlast] + (first_td_extra if (c == 0 and g == 0) else []))

                def evd(e, c=c, ks=tuple(ks)):
                    last = None
                    for j, k in enumerate(ks):
                        last = e.copy(out=wdT[:, 1024 * k + 128 * c:1024 * k + 128 * c + 128],
                                      in_=psT[:, 128 * j:128 * j + 128])
                    return last
                hlast = p.c("scalar", evd, after=[htd])
            hwd.append(hlast)

        hhm = p.c("vector", lambda e: e.reduce_max(out=self.xmaxc[:, 9:10],
                                                   in_=self.hmall[:, 0:88], axis=AX.X,
                                                   apply_absolute_value=True),
                  after=gateup_done)
        hdm = p.d("scalar", 1, lambda e: [e.dma_start(out=self.cc_mx_in[mxslot],
                                                      in_=self.xmaxc[:, 9:10])],
                  after=[hhm, hdh[(IC - 1, NT - 1)]])
        hcm2 = p.cc(lambda e: e.collective_compute(
            "AllReduce", ALU.max, replica_groups=G8,
            ins=[self.cc_mx_in[mxslot].opt()], outs=[self.cc_mx_out[mxslot].opt()]),
            after=[hdm])
        hrb = p.d("gpsimd", 1, lambda e: [e.dma_start(
            out=self.smrow[0:1, 64:192],
            in_=self.cc_mx_out[mxslot].rearrange("p one -> one p"))], after=[hcm2])
        hg1 = p.c("vector", lambda e: e.reduce_max(out=self.smrow[0:1, 8:9],
                                                   in_=self.smrow[0:1, 64:192], axis=AX.X),
                  after=[hrb])
        hg2 = p.c("scalar", lambda e: e.activation(out=self.smrow[0:1, 9:10],
                                                   in_=self.clips[0:1, cd_col:cd_col + 1],
                                                   func=AF.Sigmoid, bias=self.zeros1[0:1, 0:1]),
                  after=[hg1])

        def sh_fin(e):
            e.drain()
            e.tensor_scalar(out=self.smrow[0:1, 10:11], in0=self.smrow[0:1, 9:10],
                            scalar1=self.smrow[0:1, 8:9], scalar2=1.0 / 127.0,
                            op0=ALU.mult, op1=ALU.mult)
            e.drain()
            return e.reciprocal(out=self.smrow[0:1, 11:12], in_=self.smrow[0:1, 10:11])
        hg3 = p.c("vector", sh_fin, after=[hg2])

        def sh_bc(e):
            e.matmul(self.psX[:, 0:1], self.onesrow[0:1, :], self.smrow[0:1, 10:11],
                     start=True, stop=True)
            return e.matmul(self.psX[:, 1:2], self.onesrow[0:1, :], self.smrow[0:1, 11:12],
                            start=True, stop=True)
        hg4 = p.c("tensor", sh_bc, after=[hg3])

        def sh_ev(e):
            e.tensor_copy(out=self.scal[:, 2:3], in_=self.psX[:, 0:1])
            e.tensor_copy(out=self.scal[:, 3:4], in_=self.psX[:, 1:2])
            e.drain()
            e.tensor_scalar(out=self.csd[:, :], in0=self.rmd[:, :],
                            scalar1=self.scal[:, 2:3], scalar2=1.0 / 127.0,
                            op0=ALU.mult, op1=ALU.mult)
            return e.drain()
        hg5 = p.c("vector", sh_ev, after=[hg4, hwd[-1]])

        al.free("wdnat0", "wdnat1", "wdq")
        qhbuf = [[al.alloc(f"qh{i}_{g}", [128, (8 if g < 2 else 6) * 512], BF16)
                  for g in range(3)] for i in range(2)]
        xload = [al.alloc(f"xl{i}", [128, 512], F32) for i in range(2)]
        hstg = [al.alloc(f"hstg{i}", [128, 512], F32) for i in range(4)]

        hcomb = []
        hqh = {}
        y_dmas = []
        for n in range(NT):
            qbuf = qhbuf[n % 2]
            hq_handles = []
            for k in range(IC):
                bb = k % 4
                war_h = ([hq_handles[k - 4]] if k >= 4 else
                         ([hqh[(n - 1, IC - 4 + k)]] if n >= 1 else []))
                hl = p.d("sync", 1, lambda e, k=k, n=n, bb=bb: [
                    e.dma_start(out=hstg[bb][:, :],
                                in_=self.h_save[128 * k:128 * k + 128, 512 * n:512 * n + 512])],
                    after=[hdh[(k, n)], hwd[-1]] + war_h +
                          ([hqh[(n - 2, k)]] if (n - 2, k) in hqh else []))

                def qh_c(e, k=k, bb=bb, qbuf=qbuf):
                    gq, kq = k // 8, k % 8
                    e.tensor_scalar(out=th2[:, :], in0=hstg[bb][:, :],
                                    scalar1=self.scal[:, 3:4], scalar2=MAGIC,
                                    op0=ALU.mult, op1=ALU.add)
                    e.tensor_scalar(out=th2[:, :], in0=th2[:, :], scalar1=MAGIC, scalar2=127.0,
                                    op0=ALU.subtract, op1=ALU.min)
                    return e.tensor_scalar(out=qbuf[gq][:, 512 * kq:512 * kq + 512],
                                           in0=th2[:, :],
                                           scalar1=-128.0, scalar2=None, op0=ALU.max)
                hq = p.c("vector", qh_c, after=[hl, hg5])
                hq_handles.append(hq)
                hqh[(n, k)] = hq
            for c in range(CH):
                bk = c % 4
                bx = c % 2
                hlx = p.d("sync", 1, lambda e, c=c, n=n, bx=bx: [
                    e.dma_start(out=xload[bx][:, :],
                                in_=x_src[128 * c:128 * c + 128, 512 * n:512 * n + 512])],
                    after=[hg5] + ([hcomb[-2]] if len(hcomb) >= 2 else []))

                def mm_d(e, ks, c=c, bk=bk, qbuf=qbuf):
                    last = None
                    for k in ks:
                        gq, kq = k // 8, k % 8
                        last = e.matmul(ps[bk][:, :],
                                        wdT[:, 1024 * k + 128 * c:1024 * k + 128 * c + 128],
                                        qbuf[gq][:, 512 * kq:512 * kq + 512],
                                        start=(k == 0), stop=(k == IC - 1))
                    return last
                import functools as _ft
                war_bank = [hcomb[-4]] if len(hcomb) >= 4 else []
                hd1 = p.c("tensor", _ft.partial(mm_d, ks=range(0, 8)),
                          after=[hq_handles[7]] + war_bank)
                hd2 = p.c("tensor", _ft.partial(mm_d, ks=range(8, 16)),
                          after=[hq_handles[15], hd1])
                hmmd = p.c("tensor", _ft.partial(mm_d, ks=range(16, IC)),
                           after=[hq_handles[IC - 1], hd2])
                hsl2 = p.c("scalar", lambda e, c=c, bk=bk: e.activation(
                    out=th1[:, :], in_=ps[bk][:, :], func=AF.Silu,
                    scale=self.csd[:, c:c + 1], bias=self.zeros1[:, 0:1]),
                    after=[hmmd] + ([hcomb[-1]] if hcomb else []))

                def comb(e, c=c, n=n, bx=bx):
                    e.tensor_tensor(out=th2[:, :], in0=th1[:, :],
                                    in1=s_get(c)[:, 512 * n:512 * n + 512], op=ALU.mult)
                    return e.tensor_tensor(out=th3[c % 2][:, :], in0=th2[:, :],
                                           in1=xload[bx][:, :], op=ALU.add)
                hcb = p.c("vector", comb,
                          after=[hsl2, hlx] + list(s_ready) +
                          ([y_dmas[-2]] if len(y_dmas) >= 2 else []))
                hcomb.append(hcb)
                y_dmas.append(p.d("scalar", 1, lambda e, c=c, n=n: [
                    e.dma_start(out=x1_dst[128 * c:128 * c + 128, 512 * n:512 * n + 512],
                                in_=th3[c % 2][:, :])], after=[hcb]))

        al.free("qh0_0", "qh0_1", "qh0_2", "qh1_0", "qh1_1", "qh1_2",
                "xl0", "xl1", "hstg0", "hstg1", "hstg2", "hstg3",
                "wdT", "th1", "th2", "th30", "th31")
        return {"done": hcomb[-1], "y_dmas": y_dmas}

    def emit_attn1(self, hln, deps, wdeps, post_scan_hook=None):
        p, al = self.p, self.al
        ps = self.ps
        xn = self.xn
        tfB = self.tfB

        WzTb = al.alloc("WzTb", [128, 8 * 1024], BF16, top=True)
        WhTb = al.alloc("WhTb", [128, 8 * 1024], BF16, top=True)
        hwz = p.d("gpsimd", 1, lambda e: [e.dma_start(
            out=WzTb[:, :].rearrange("p (c n) -> p c n", c=8),
            in_=self.WzT_d.rearrange("(c p) n -> p c n", p=128))], after=list(wdeps))
        hwh = p.d("gpsimd", 1, lambda e: [e.dma_start(
            out=WhTb[:, :].rearrange("p (c n) -> p c n", c=8),
            in_=self.WhT_d.rearrange("(c p) n -> p c n", p=128))], after=list(wdeps))

        zt = al.alloc("zt", [128, T], F32)
        hloc = al.alloc("hloc", [128, 8 * T], BF16)
        Aloc = al.alloc("Aloc", [128, 8 * T], BF16)
        tfC = al.alloc("tfC", [128, T], F32)
        sA = [self.tfA, al.alloc("tfA2", [128, T], F32)]
        tfD = al.alloc("tfD", [128, T], F32)
        sD = [tfD, tfD]
        hz0 = p.c("gpsimd", lambda e: e.memset(zt[:, :], 0.0), after=list(deps))

        hsc = []
        het_prev = None
        for c in range(CH):
            A, D = sA[c % 2], sD[c % 2]

            def mm_z(e, c=c):
                last = None
                for k in range(CH):
                    for n in range(NT):
                        last = e.matmul(ps[n][:, :],
                                        WzTb[:, 1024 * k + 128 * c:1024 * k + 128 * c + 128],
                                        xn[:, T * k + 512 * n:T * k + 512 * n + 512],
                                        start=(k == 0), stop=(k == CH - 1))
                return last
            hmz = p.c("tensor", mm_z, after=[hwz, hln["xn"]] +
                      ([het_prev] if het_prev else []))

            def ev_c(e, c=c, A=A):
                last = None
                for n in range(NT):
                    last = e.activation(out=A[:, 512 * n:512 * n + 512], in_=ps[n][:, :],
                                        func=AF.Sigmoid, scale=-1.0,
                                        bias=self.negbz[:, c:c + 1])
                return last
            hec = p.c("scalar", ev_c, after=[hmz, self.h_bias] +
                      ([hsc[c - 2]] if c >= 2 else []))

            def mm_h(e, c=c):
                last = None
                for k in range(CH):
                    for n in range(NT):
                        last = e.matmul(ps[n][:, :],
                                        WhTb[:, 1024 * k + 128 * c:1024 * k + 128 * c + 128],
                                        xn[:, T * k + 512 * n:T * k + 512 * n + 512],
                                        start=(k == 0), stop=(k == CH - 1))
                return last
            hmh = p.c("tensor", mm_h, after=[hwh, hec])

            def ev_ta(e, c=c, A=A):
                last = None
                for n in range(NT):
                    sl = slice(512 * n, 512 * n + 512)
                    last = e.scalar_tensor_tensor(out=tfB[:, sl], in0=ps[n][:, :],
                                                  scalar=self.bhm05[:, c:c + 1],
                                                  in1=A[:, sl],
                                                  op0=ALU.add, op1=ALU.add)
                return last
            het_a = p.c("vector", ev_ta, after=[hmh])

            def ev_tb(e, c=c, A=A, D=D):
                e.tensor_scalar(out=tfB[:, :], in0=tfB[:, :], scalar1=0.0, scalar2=None,
                                op0=ALU.max)
                e.tensor_scalar(out=tfC[:, :], in0=A[:, :], scalar1=-1.0, scalar2=1.0,
                                op0=ALU.mult, op1=ALU.add)
                e.tensor_tensor(out=tfB[:, :], in0=tfB[:, :], in1=tfC[:, :], op=ALU.add)
                return e.tensor_tensor(out=D[:, :], in0=tfC[:, :], in1=tfB[:, :],
                                       op=ALU.mult)
            het = p.c("vector", ev_tb, after=[het_a])
            het_prev = het_a

            def scans(e, c=c, A=A, D=D):
                e.tensor_tensor_scan(out=hloc[:, T * c:T * c + T], data0=A[:, :],
                                     data1=D[:, :], initial=0.0, op0=ALU.mult, op1=ALU.add)
                return e.tensor_tensor_scan(out=Aloc[:, T * c:T * c + T], data0=A[:, :],
                                            data1=zt[:, :], initial=1.0, op0=ALU.mult,
                                            op1=ALU.add)
            hsc.append(p.c("vector", scans, after=[het, hz0]))

            def ssum_c(e, c=c):
                e.tensor_copy(out=self.ssum[:, 2 * c:2 * c + 1],
                              in_=Aloc[:, T * c + T - 1:T * c + T])
                return e.tensor_copy(out=self.ssum[:, 2 * c + 1:2 * c + 2],
                                     in_=hloc[:, T * c + T - 1:T * c + T])
            hsum = p.c("gpsimd", ssum_c, after=[hsc[-1]])

        hln["sx"] = hln["sx_fin"]()
        if post_scan_hook:
            al.free("tfC", "tfD", "tfA2", "zt", "WzTb", "WhTb")
            post_scan_hook(hsc[-1])
        hds = p.d("scalar", 1, lambda e: [e.dma_start(out=self.cc_s_in, in_=self.ssum[:, :])],
                  after=[hsc[-1], hsum])
        hcs = p.cc(lambda e: e.collective_compute(
            "AllGather", ALU.bypass, replica_groups=G4,
            ins=[self.cc_s_in.opt()], outs=[self.cc_s_out.opt()]), after=[hds])
        hrs = p.d("gpsimd", 4, lambda e: [
            e.dma_start(out=self.sgth[j][:, :], in_=self.cc_s_out[128 * j:128 * j + 128, :])
            for j in range(4)], after=[hcs])

        hfx = []
        hpre_tap = None
        if "h1pre" in TAPS:
            hpo = self.nc.dram_tensor("tap_h1pre", [128 * 8, T], BF16,
                                      kind="ExternalOutput").ap()
            self.tap_outs["h1pre"] = hpo
            hpre_tap = p.d("scalar", 8, lambda e: [
                e.dma_start(out=hpo[128 * c:128 * c + 128, :], in_=hloc[:, T * c:T * c + T])
                for c in range(CH)], after=[hsc[-1]])
        if "A1" in TAPS:
            apo = self.nc.dram_tensor("tap_A1", [128 * 8, T], BF16,
                                      kind="ExternalOutput").ap()
            self.tap_outs["A1"] = apo
            hpre_a = p.d("scalar", 8, lambda e: [
                e.dma_start(out=apo[128 * c:128 * c + 128, :], in_=Aloc[:, T * c:T * c + T])
                for c in range(CH)], after=[hsc[-1]])
        if "cv7" in TAPS:
            cvo = self.nc.dram_tensor("tap_cv7", [256, T], F32, kind="ExternalOutput").ap()
            self.tap_outs["cv7"] = cvo
            hpre_cv = p.d("scalar", 2, lambda e: [
                e.dma_start(out=cvo[0:128, :], in_=sA[1][:, :]),
                e.dma_start(out=cvo[128:256, :], in_=sD[1][:, :])], after=[hsc[-1]])

        def pfx_all(e):
            sg = self.sgth
            e.tensor_tensor(out=self.pfx[:, 0:8], in0=sg[1][:, 0:16:2],
                            in1=sg[0][:, 1:16:2], op=ALU.mult)
            e.drain()
            e.tensor_tensor(out=self.pfx[:, 0:8], in0=self.pfx[:, 0:8],
                            in1=sg[1][:, 1:16:2], op=ALU.add)
            e.drain()
            e.tensor_tensor(out=self.pfx[:, 8:16], in0=sg[2][:, 0:16:2],
                            in1=self.pfx[:, 0:8], op=ALU.mult)
            e.drain()
            e.tensor_tensor(out=self.pfx[:, 8:16], in0=self.pfx[:, 8:16],
                            in1=sg[2][:, 1:16:2], op=ALU.add)
            e.drain()
            e.tensor_scalar(out=self.pfx[:, 16:24], in0=sg[0][:, 1:16:2],
                            scalar1=self.cp[:, 3:4], scalar2=None, op0=ALU.mult)
            e.drain()
            e.scalar_tensor_tensor(out=self.pfx[:, 16:24], in0=self.pfx[:, 0:8],
                                   scalar=self.cp[:, 4:5], in1=self.pfx[:, 16:24],
                                   op0=ALU.mult, op1=ALU.add)
            e.drain()
            e.scalar_tensor_tensor(out=self.pfx[:, 16:24], in0=self.pfx[:, 8:16],
                                   scalar=self.cp[:, 5:6], in1=self.pfx[:, 16:24],
                                   op0=ALU.mult, op1=ALU.add)
            return e.drain()
        hpfx = p.c("vector", pfx_all, after=[hrs, self.h_cst])
        if "sg1" in TAPS:
            sgo = self.nc.dram_tensor("tap_sg1", [128, 88], F32, kind="ExternalOutput").ap()
            self.tap_outs["sg1"] = sgo
            p.d("scalar", 5, lambda e: [
                e.dma_start(out=sgo[:, 0:16], in_=self.sgth[0][:, :]),
                e.dma_start(out=sgo[:, 16:32], in_=self.sgth[1][:, :]),
                e.dma_start(out=sgo[:, 32:48], in_=self.sgth[2][:, :]),
                e.dma_start(out=sgo[:, 48:64], in_=self.sgth[3][:, :]),
                e.dma_start(out=sgo[:, 64:88], in_=self.pfx[:, :]),
            ], after=[hpfx])
        for c in range(CH):
            def fix_c(e, c=c):
                return e.scalar_tensor_tensor(out=hloc[:, T * c:T * c + T],
                                              in0=Aloc[:, T * c:T * c + T],
                                              scalar=self.pfx[:, 16 + c:17 + c],
                                              in1=hloc[:, T * c:T * c + T],
                                              op0=ALU.mult, op1=ALU.add)
            hfx.append(p.c("vector", fix_c, after=[hpfx] +
                           ([hpre_tap] if hpre_tap else [])))
        self.tap("h1", lambda c: hloc[:, T * c:T * c + T], 8, T, BF16, hfx[-1:])

        Wo1Tb = self.Wo1Tb
        hwo = p.d("gpsimd", 1, lambda e: [e.dma_start(
            out=Wo1Tb[:, :].rearrange("p (c n) -> p c n", c=8),
            in_=self.Wo1T_d.rearrange("(c p) n -> p c n", p=128))], after=[hsc[-1]])
        al.free("Aloc")
        s1 = al.alloc("s1", [128, 8 * T], BF16)
        hs1 = []
        for dc in range(CH):
            def mm_o(e, dc=dc):
                last = None
                for k in range(CH):
                    for n in range(NT):
                        last = e.matmul(ps[n][:, :],
                                        Wo1Tb[:, 1024 * k + 128 * dc:1024 * k + 128 * dc + 128],
                                        hloc[:, T * k + 512 * n:T * k + 512 * n + 512],
                                        start=(k == 0), stop=(k == CH - 1))
                return last
            hmo = p.c("tensor", mm_o, after=[hwo, hfx[-1]] + ([hs1[dc - 1]] if dc >= 1 else []))

            def ev_o(e, dc=dc):
                last = None
                for n in range(NT):
                    last = e.tensor_scalar(out=s1[:, T * dc + 512 * n:T * dc + 512 * n + 512],
                                           in0=ps[n][:, :],
                                           scalar1=self.bo1_fm[:, dc:dc + 1],
                                           scalar2=None, op0=ALU.add)
                return last
            hs1.append(p.c("vector", ev_o, after=[hmo]))
        self.tap("s1", lambda c: s1[:, T * c:T * c + T], 8, T, BF16, hs1[-1:])

        al.free("Wo1Tb", "hloc")
        self.s1 = s1
        return {"s1": hs1[-1], "hloc_reads": [hs1[-1]]}

    def build(self):
        al = self.al
        self.decl()
        self.setup()
        al.free("bo_row")
        self.sx_h = {}
        self.xq_h = {}

        self.xn = al.alloc("xn", [128, 8 * T], BF16)
        self.xt = [al.alloc(f"xt{i}", [128, T], F32) for i in range(2)]
        hln0 = self.emit_ln(0, self.x_in, [self.h_cst, self.h_init, self.h_bob])
        self.tap("xn0", lambda c, t=self.xn: t[:, T * c:T * c + T], 8, T, BF16, [hln0["xn"]])

        self.qdbg = al.alloc("qdbg", [128, 512], F32) if os.environ.get("KB_DBGQ") else None

        def probe(slot, after):
            if self.qdbg is None:
                return
            self.p.c("vector", lambda e, slot=slot: e.tensor_copy(
                out=self.qdbg[:, slot * 128:(slot + 1) * 128], in_=self.mt[:, 0:128]),
                after=after)
        self.mtprobe = probe

        def post_energy():
            self.xq = al.alloc("xq", [128, 8 * T], BF16)
            self.xq_h[0] = self.emit_quant(self.x_in, [hln0["sx"], hln0["stats"]])
        a0 = self.emit_attn0(hln0, post_energy_hook=post_energy)
        self.sx_h[0] = hln0["sx"]
        if getattr(self, "dev_stopped", False):
            return
        if self.qdbg is not None:
            qdo = self.nc.dram_tensor("tap_qdbg", [128, 512], F32, kind="ExternalOutput").ap()
            self.tap_outs["qdbg"] = qdo
            self.p.d("scalar", 1, lambda e: [e.dma_start(out=qdo, in_=self.qdbg[:, :])],
                     after=[self.xq_h[0]])
            al.free("qdbg")
        self.tap("xq0", lambda c, t=self.xq: t[:, T * c:T * c + T], 8, T, BF16, [self.xq_h[0]])
        if DEV_STOP == "quant0":
            self.p.c("sync", lambda e: e.nop(), after=[self.xq_h[0], a0["ag"]])
            self.p.emit(self.nc)
            return

        al.free("vmixF", "Wob", "vmixr0", "vmixr1", "vstg0", "vstg1", "ubuf0", "ubuf1")
        al.free("xt0", "xt1")

        self.ust = [al.alloc(f"ust{i}", [128, 1024], F32, top=True) for i in range(2)]
        self.ustb = [al.alloc(f"ustb{i}", [128, 1024], BF16, top=True) for i in range(2)]
        self.s_fm = al.alloc("s_fm", [128, 8 * T], BF16, top=True)
        rounds = []
        s_evs = []

        def hook(i, hwev_b):
            if os.environ.get("KB_NO_ROUNDS"):
                return None
            if i < 5 or len(rounds) >= 16:
                return None
            r = self.emit_consume_round(len(rounds), a0["ag"],
                                        rounds, [hwev_b, self.xq_h[0]] + a0["last_reads"])
            rounds.append(r)
            s_evs.append(r["ev"])
            return r["ev"]

        def mid_free0():
            al.free("ust0", "ust1", "ustb0", "ustb1", "xn", "xq")
            return ([r["ev"] for r in rounds[-2:]] +
                    ([rounds[-1]["gat"]] if rounds else []))

        f0 = self.emit_ffn(0, lambda c: self.s_fm[:, T * c:T * c + T], s_evs,
                           self.x_in, self.x1_save,
                           [self.xq_h[0], a0["vf"]] + a0["last_reads"],
                           hook=hook, mid_free=mid_free0)
        assert os.environ.get("KB_NO_ROUNDS") or len(rounds) == 16, len(rounds)
        self.tap("s_fm", lambda c, t=self.s_fm: t[:, T * c:T * c + T], 8, T, BF16,
                 s_evs[-1:] + [f0["done"]])

        if DEV_STOP == "sfm0":
            hfin = self.p.d("gpsimd", 8, lambda e: [
                e.dma_start(out=self.y_out[128 * c:128 * c + 128, :].bitcast(BF16)[:, 0:T],
                            in_=self.s_fm[:, T * c:T * c + T]) for c in range(8)],
                after=s_evs[-1:] + [f0["done"]])
            self.p.c("sync", lambda e: e.nop(), after=[hfin])
            self.p.emit(self.nc)
            return

        if DEV_STOP == "hsave0":
            hfin = self.p.d("gpsimd", 1, lambda e: [
                e.dma_start(out=self.y_out, in_=self.h_save[0:1024, :])],
                after=[f0["done"], f0["y_dmas"][-1]])
            self.p.c("sync", lambda e: e.nop(), after=[hfin])
            self.p.emit(self.nc)
            return

        if DEV_STOP == "ffn0":
            hfin = self.p.d("gpsimd", 1, lambda e: [
                e.dma_start(out=self.y_out, in_=self.x1_save)],
                after=[f0["done"], f0["y_dmas"][-1]])
            self.p.c("sync", lambda e: e.nop(), after=[hfin])
            self.p.emit(self.nc)
            return

        al.free("m64", "bo_b", "u16a", "u16b")
        self.xn = al.alloc("xn", [128, 8 * T], BF16)
        self.xt = [al.alloc(f"xt{i}", [128, T], F32) for i in range(2)]
        hln1 = self.emit_ln(1, self.x1_save, [f0["done"], f0["y_dmas"][-1]])
        self.tap("xn1", lambda c, t=self.xn: t[:, T * c:T * c + T], 8, T, BF16, [hln1["xn"]])

        al.free("s_fm", "xt0", "xt1")
        def post_scan(hscan_done):
            self.xq = al.alloc("xq", [128, 8 * T], BF16, top=True)
            self.xt = [al.alloc(f"xtq{i}", [128, T], F32) for i in range(2)]
            self.Wo1Tb = al.alloc("Wo1Tb", [128, 8 * 1024], BF16)
            self.xq_h[1] = self.emit_quant(self.x1_save,
                                           [hln1["sx"], hln1["stats"], hscan_done])
        a1 = self.emit_attn1(hln1, [f0["done"], hln1["xn"]],
                             wdeps=[f0["done"], f0["y_dmas"][-1]],
                             post_scan_hook=post_scan)
        self.sx_h[1] = hln1["sx"]


        def mid_free1():
            al.free("xn", "xq", "xtq0", "xtq1")

        f1 = self.emit_ffn(1, lambda c: self.s1[:, T * c:T * c + T], [a1["s1"]],
                           self.x1_save, self.y_out, [self.xq_h[1], a1["s1"]],
                           hook=None, mid_free=mid_free1)

        self.p.c("scalar", lambda e: e.nop(), after=f1["y_dmas"][-1:])
        self.p.c("sync", lambda e: e.nop(), after=f1["y_dmas"][-1:])

        self.p.emit(self.nc)


def _host_inputs(inputs):
    x = np.ascontiguousarray(np.asarray(inputs["L0_attn_x"], dtype=np.float32))
    f32 = lambda a: np.ascontiguousarray(np.asarray(a, dtype=np.float32))
    clips = np.array([[np.asarray(inputs["L0_ffn_clip_x"]).reshape(-1)[0],
                       np.asarray(inputs["L0_ffn_clip_down"]).reshape(-1)[0],
                       np.asarray(inputs["L1_ffn_clip_x"]).reshape(-1)[0],
                       np.asarray(inputs["L1_ffn_clip_down"]).reshape(-1)[0]]],
                     dtype=np.float32)
    pp = np.arange(128)
    kk = np.arange(1024)
    m64 = ((pp[:, None] % 64) == (kk[None, :] % 64)).astype(np.float32)
    ident = np.eye(128, dtype=ml_dtypes.bfloat16)
    aa = np.arange(16)
    u16 = (aa[:, None] == (kk[None, :] // 64)).astype(ml_dtypes.bfloat16)
    common = {
        "Wq": f32(inputs["L0_attn_Wq"]), "Wk": f32(inputs["L0_attn_Wk"]),
        "WvT": np.ascontiguousarray(f32(inputs["L0_attn_Wv"]).T),
        "Wo0": f32(inputs["L0_attn_Wo"]),
        "bo0": f32(inputs["L0_attn_bo"]).reshape(1, 1024),
        "wg0": f32(inputs["L0_ffn_w_gate"]), "wu0": f32(inputs["L0_ffn_w_up"]),
        "wd0": f32(inputs["L0_ffn_w_down"]),
        "wg1": f32(inputs["L1_ffn_w_gate"]), "wu1": f32(inputs["L1_ffn_w_up"]),
        "wd1": f32(inputs["L1_ffn_w_down"]),
        "WzT": np.ascontiguousarray(f32(inputs["L1_attn_Wz"]).T),
        "WhT": np.ascontiguousarray(f32(inputs["L1_attn_Wh"]).T),
        "Wo1T": np.ascontiguousarray(f32(inputs["L1_attn_Wo"]).T),
        "bz_fm": np.ascontiguousarray(f32(inputs["L1_attn_bz"]).reshape(8, 128).T),
        "bh_fm": np.ascontiguousarray(f32(inputs["L1_attn_bh"]).reshape(8, 128).T),
        "bo1_fm": np.ascontiguousarray(f32(inputs["L1_attn_bo"]).reshape(8, 128).T),
        "clips": clips, "m64row": m64, "ident_bf": ident,
        "swap64_bf": np.block([[np.zeros((64, 64)), np.eye(64)],
                               [np.eye(64), np.zeros((64, 64))]]).astype(ml_dtypes.bfloat16),
        "u16a": u16, "u16b": u16.copy(),
    }
    in_maps = []
    for q in range(NCORE):
        n, qq = q // 4, q % 4
        xs = np.ascontiguousarray(x[n, 2048 * qq:2048 * (qq + 1), :].T)
        cp = np.zeros((128, 16), np.float32)
        cp[:, 0] = 1.0 if n == 0 else 0.0
        cp[:, 1] = 1.0 if n == 1 else 0.0
        cp[:, 2 + qq] = 1.0
        uidx = np.zeros((128, 16), np.int32)
        for r in range(4):
            for m in range(4):
                uidx[:, 4 * r + m] = 2048 * r + 128 * (4 * qq + m) + np.arange(128)
        in_maps.append(dict(common, x_fm=xs, coreparams=cp, uidx=uidx))
    return in_maps, x.shape


_CACHE = {}


def kernel(**inputs):
    in_maps, xshape = _host_inputs(inputs)
    if "kb" not in _CACHE:
        _CACHE["kb"] = KB()
    kb = _CACHE["kb"]
    res = run_bass_kernel_spmd(kb.nc, in_maps, core_ids=list(range(NCORE)))
    out = np.empty(xshape, dtype=np.float32)
    for q in range(NCORE):
        n, qq = q // 4, q % 4
        out[n, 2048 * qq:2048 * (qq + 1), :] = res.results[q]["y"].T
    kernel.last_results = res
    return out
